# version 07
# baseline (speedup 1.0000x reference)
import numpy as np
import ml_dtypes
import concourse.bass as bass
import concourse.mybir as mybir
from concourse.bass_utils import run_bass_kernel_spmd

F32 = mybir.dt.float32
BF16 = mybir.dt.bfloat16
AF = mybir.ActivationFunctionType
ALU = mybir.AluOpType
AX = mybir.AxisListType.X

NCORES = 8
D = 1024
NH = 16
HD = 64
F = 4096
NS = 16
EPS = 1e-5
SCALE = 0.125

B0SET = [0, 1, 2, 3, 8, 9, 10, 11]
B64SET = [4, 5, 6, 7, 12, 13, 14, 15]
PAIRS = list(zip(B0SET, B64SET))


def qcol(h):
    if h in B0SET:
        return B0SET.index(h) * 128
    return B64SET.index(h) * 128 + 64

KB = 1024


def _ceil(a, b):
    return (a + b - 1) // b


class Walk:

    lenient = False

    def __init__(self, kind, eng, sems):
        self.kind = kind
        self.eng = eng
        self.sems = sems
        self.ctr = {k: 0 for k in sems}
        self.waited = {k: 0 for k in sems}
        self.ev = {}

    def _wait(self, sem, val):
        if val > self.waited[sem]:
            self.eng.wait_ge(self.sems[sem], val)
            self.waited[sem] = val

    def _dep(self, dep):
        if isinstance(dep, tuple) and len(dep) == 2 and isinstance(dep[0], str) \
                and dep[0] in self.sems and isinstance(dep[1], int):
            return dep
        return self.ev[dep]

    def emit(self, kind, f=None, sem=None, deps=(), key=None, inc=None):
        if inc is None:
            inc = 16 if (sem is not None and sem.startswith("q")) else 1
        if self.kind == kind:
            for dep in deps:
                if dep is None:
                    continue
                try:
                    s, v = self._dep(dep)
                except KeyError:
                    if self.lenient:
                        continue
                    raise
                self._wait(s, v)
            inst = f(self.eng)
            if sem is not None:
                inst.then_inc(self.sems[sem], inc)
        if sem is not None:
            self.ctr[sem] += inc
            if key is not None:
                self.ev[key] = (sem, self.ctr[sem])
        elif key is not None:
            raise ValueError("key requires sem")

    def mark(self, key, sem):
        self.ev[key] = (sem, self.ctr[sem])

    def barrier_wait(self, kind, keys):
        if self.kind == kind:
            for dep in keys:
                if dep is None:
                    continue
                s, v = self._dep(dep)
                self._wait(s, v)


def build_program(S, taps=(), upto="I"):
    Walk.lenient = (upto != "I")
    NT = S // 128
    NJ = S // 512
    nc = bass.Bass("TRN2", target_bir_lowering=False, debug=False)

    dx = nc.dram_tensor("x", [S, D], F32, kind="ExternalInput").ap()
    dwq = nc.dram_tensor("wq", [D, D], F32, kind="ExternalInput").ap()
    dwk = nc.dram_tensor("wk", [256, D], F32, kind="ExternalInput").ap()
    dwv = nc.dram_tensor("wv", [256, D], F32, kind="ExternalInput").ap()
    dwo = nc.dram_tensor("wo", [D, D], F32, kind="ExternalInput").ap()
    dsq = nc.dram_tensor("slotq", [NS, D], F32, kind="ExternalInput").ap()
    dW = {}
    for nm in ("dWq", "dWk", "dWv", "dWo", "cWq", "cWk", "cWv", "cWo"):
        dW[nm] = nc.dram_tensor(nm, [D, D], F32, kind="ExternalInput").ap()
    dB = {}
    for nm in ("dbq", "dbk", "dbv", "dbo", "cbq", "cbk", "cbv", "cbo"):
        dB[nm] = nc.dram_tensor(nm, [1, D], F32, kind="ExternalInput").ap()
    dlng = nc.dram_tensor("lng", [NS, D], F32, kind="ExternalInput").ap()
    dlnb = nc.dram_tensor("lnb", [NS, D], F32, kind="ExternalInput").ap()
    deW1 = nc.dram_tensor("eW1", [2, D, F], F32, kind="ExternalInput").ap()
    deWg = nc.dram_tensor("eWg", [2, D, F], F32, kind="ExternalInput").ap()
    deW2 = nc.dram_tensor("eW2", [2, F, D], F32, kind="ExternalInput").ap()
    deb1 = nc.dram_tensor("eb1", [2, F], F32, kind="ExternalInput").ap()
    debg = nc.dram_tensor("ebg", [2, F], F32, kind="ExternalInput").ap()
    deb2 = nc.dram_tensor("eb2", [2, D], F32, kind="ExternalInput").ap()
    dbsel = nc.dram_tensor("bsel", [128, 16], F32, kind="ExternalInput").ap()
    dind2 = nc.dram_tensor("ind2", [2, 128], F32, kind="ExternalInput").ap()
    desel = nc.dram_tensor("esel", [128, 16], F32, kind="ExternalInput").ap()
    dout = nc.dram_tensor("out", [S, D], F32, kind="ExternalOutput").ap()

    dx1 = nc.dram_tensor("x1d", [S, D], F32).ap()
    dqt = nc.dram_tensor("qtd", [8, 128, S], BF16).ap()
    dewb1 = nc.dram_tensor("ewb1", [16, 128, 8, KB], BF16).ap()
    dewb2 = nc.dram_tensor("ewb2", [8, 128, 8, KB], BF16).ap()
    dewbF = nc.dram_tensor("ewbF", [8, 128, 8, KB], BF16).ap()
    dlrow = nc.dram_tensor("lrowd", [8, KB], F32).ap()
    drl = nc.dram_tensor("rld", [16, 512], F32).ap()
    dcc1i = nc.dram_tensor("cc1i", [NS, D], F32)
    dcc1o = nc.dram_tensor("cc1o", [128, D], F32)
    dcc2i = nc.dram_tensor("cc2i", [NS, D], F32)
    dcc2o = nc.dram_tensor("cc2o", [128, D], F32)

    dtaps = {}
    for name, shape, dt_ in taps:
        dtaps[name] = nc.dram_tensor("tap_" + name, list(shape), dt_, kind="ExternalOutput").ap()

    SBASE = (nc.sbuf_base + 31) // 32 * 32
    cur = [0]

    def alloc_at(name, shape, dtype, offset):
        return nc.alloc_sbuf_tensor_at(name, list(shape), dtype, offset=SBASE + offset)

    def sb(name, shape, dtype, at=None):
        per = int(np.prod(shape[1:])) * mybir.dt.size(dtype)
        if at is None:
            at = cur[0]
            cur[0] = (at + per + 31) // 32 * 32
        return alloc_at(name, shape, dtype, at)

    ident = sb("ident", [128, 128], BF16)
    identf = sb("identf", [128, 128], F32)
    trimask = sb("trimask", [128, 128], BF16)
    st_s = sb("st_s", [128, NT], F32)
    st_b = sb("st_b", [128, NT], F32)
    lnst = sb("lnst", [128, 12], F32)
    bias16 = sb("bias16", [128, 16], F32)
    bias16p = sb("bias16p", [128, 8], F32)
    bias16c = sb("bias16c", [128, 16], F32)
    bias16d_t = sb("bias16d_t", [128, 8], F32)
    bcols = sb("bcols", [128, 48], F32)
    nrm = sb("nrm", [128, 64], F32)
    qnacc = sb("qnacc", [128, 128], F32)
    lstage_off = cur[0]
    lstage = sb("lstage", [128, KB], F32)
    xnsc = sb("xnsc", [128, KB], BF16)
    sqsc = sb("sqsc", [128, 512], BF16)
    sqT = sb("sqT", [128, 128], BF16)
    sqTb = sb("sqTb", [128, 128], F32)
    odTb = sb("odTb", [128, 128], BF16)
    oT = sb("oT", [128, 128], F32)
    o2T = sb("o2T", [128, 128], F32)
    o2Tb = sb("o2Tb", [128, 128], BF16)
    QdT = sb("QdT", [128, 128], BF16)
    QdBD = sb("QdBD", [128, 8 * 32], BF16)

    KcT = sb("KcT", [128, 128], BF16)
    VcT = sb("VcT", [128, 128], F32)
    VcOnes = sb("VcOnes", [16, 16 * 65], BF16)
    onesc = sb("onesc", [1, 128], F32)
    ug = sb("ug", [128, 256], BF16)
    u_sb = sb("u_sb", [128, 256], BF16)
    g_sb = sb("g_sb", [128, 256], BF16)
    y_acc = sb("y_acc", [128, 64], F32)
    bsel_sb = sb("bsel_sb", [128, 16], F32)
    l_run = sb("l_run", [32, 8], F32)
    bonesb = sb("bonesb", [128, 2], BF16)
    rl32 = sb("rl32", [32, 8], F32)
    rlB = sb("rlB", [16, 8], F32)
    lpart = sb("lpart", [32, 2], F32)
    ind2 = sb("ind2", [2, 128], F32)
    mrow2 = sb("mrow2", [2, 8], F32)
    bstg = sb("bstg", [8, 128], F32)

    A1 = cur[0]; cur[0] += 16 * KB + 512
    A3 = cur[0]; cur[0] += 8 * KB
    A4 = cur[0]; cur[0] += 16 * KB
    A5 = cur[0]; cur[0] += 16 * KB
    A6 = cur[0]; cur[0] += 8 * KB
    A7 = cur[0]; cur[0] += 8 * KB
    A8 = cur[0]; cur[0] += 16 * KB
    A9 = cur[0]; cur[0] += 32 * KB
    A10 = cur[0]; cur[0] += 8 * KB
    A11 = cur[0]; cur[0] += 40 * KB
    AKT = cur[0]; cur[0] += max(4 * S, 16 * KB)
    ASQ = cur[0]; cur[0] += 4 * KB

    assert cur[0] <= 212832, f"SBUF over budget: {cur[0]} bytes/partition"

    V_sb = alloc_at("V_sb", [128, NT * 260], BF16, A1)
    W1bfh = alloc_at("W1bfh", [128, 8192], BF16, A1)
    xnTg = alloc_at("xnTg", [128, 8 * 512], BF16, A3)
    QTJ = [alloc_at(f"QTJ{i}", [128, 8 * 512], BF16, A4 + i * 8 * KB) for i in range(2)]
    stage1 = alloc_at("stage1", [128, 4096], F32, A4)
    QcT = alloc_at("QcT", [128, 8 * KB], BF16, A4)
    ctxT = [alloc_at(f"ctxT{i}", [128, 8 * 512], BF16, A5 + i * 8 * KB) for i in range(2)]
    stage2 = alloc_at("stage2", [128, 4096], F32, A5)
    ctxcT = [alloc_at(f"ctxcT{i}", [128, 8 * 512], BF16, A5 + i * 8 * KB) for i in range(2)]
    PT = [alloc_at(f"PT{i}", [128, KB], BF16, A6 + i * 2 * KB) for i in range(4)]
    PTpair = [alloc_at(f"PTp{i}", [128, 2 * KB], BF16, A6 + i * 4 * KB) for i in range(2)]
    KdTg = alloc_at("KdTg", [128, 8 * 512], BF16, A6)
    PcB = [alloc_at(f"PcB{i}", [16, 512], BF16, A6 + i * KB) for i in range(4)]
    KcBD = alloc_at("KcBD", [128, 8 * 80], BF16, A6 + 4 * KB)
    rlb = alloc_at("rlb", [64, 2048], F32, A7)
    ones64 = alloc_at("ones64", [128, 64], F32, A7)
    dbvb = alloc_at("dbvb", [128, KB], F32, A5)
    cbob = alloc_at("cbob", [128, KB], F32, A11)
    xt = [alloc_at(f"xt{i}", [128, KB], F32, A8 + i * 4 * KB) for i in range(4)]
    Wgbfh = alloc_at("Wgbfh", [128, 8192], BF16, A8)
    ffnW = [alloc_at(f"ffnW{i}", [128, 8 * KB], BF16, A9 + i * 16 * KB) for i in range(2)]
    Oall = alloc_at("Oall", [128, KB], F32, A9)
    Yall = alloc_at("Yall", [128, KB], F32, A9 + 4 * KB)
    Gt = alloc_at("Gt", [128, KB], F32, A9 + 8 * KB)
    Bt = alloc_at("Bt", [128, KB], F32, A9 + 12 * KB)
    W2bfw = alloc_at("W2bfw", [128, 8 * KB], BF16, A9 + 16 * KB)
    sqf32 = alloc_at("sqf32", [128, KB], F32, ASQ)
    o_tok = alloc_at("o_tok", [16, KB], F32, lstage_off)
    wstg = [alloc_at(f"wstg{i}", [128, KB], F32, A10 + i * 4 * KB) for i in range(2)]
    pstg = [alloc_at(f"pstg{i}", [128, 4096], F32, A9 + i * 16 * KB) for i in range(2)]
    pwb = [alloc_at(f"pwb{i}", [128, 2048], BF16, A10 + i * 4 * KB) for i in range(2)]
    wqT = alloc_at("wqT", [128, 8 * KB], BF16, A11)
    wkT = alloc_at("wkT", [128, 2 * KB], BF16, A11 + 16 * KB)
    wvT = alloc_at("wvT", [128, 2 * KB], BF16, A11 + 20 * KB)
    woT = alloc_at("woT", [128, 8 * KB], BF16, A11 + 24 * KB)
    VdG = [alloc_at(f"VdG{i}", [128, 4 * 1040], BF16, A11 + i * 9 * KB) for i in range(2)]
    PdB = [alloc_at(f"PdB{i}", [32, 512], BF16, A11 + 18 * KB + i * KB) for i in range(4)]
    PdTG = alloc_at("PdTG", [128, 4 * 256], BF16, A11 + 22 * KB)
    od_acc = alloc_at("od_acc", [16, KB], F32, A11 + 24 * KB)
    b1T = alloc_at("b1T", [128, 64], F32, A11 + 28 * KB)
    bgT = alloc_at("bgT", [128, 64], F32, A11 + 28 * KB + 256)
    b2T = alloc_at("b2T", [128, 16], F32, A11 + 28 * KB + 512)
    eb1c2 = alloc_at("eb1c2", [32, 256], F32, A11 + 29 * KB)
    PcT = [alloc_at(f"PcT{i}", [16, 2048], BF16, A11 + 30 * KB + i * 4 * KB) for i in range(2)]
    Yle = [alloc_at(f"Yle{i}", [8, KB], F32, A6 + i * 4 * KB) for i in range(2)]
    hsel = alloc_at("hsel", [16, KB], BF16, AKT + 11 * KB)
    hTe = alloc_at("hTe", [128, 128], BF16, AKT + 15 * KB)
    esel_sb = alloc_at("esel_sb", [128, 16], F32, AKT + 15 * KB + 256)
    esel_bf = alloc_at("esel_bf", [128, 16], BF16, AKT + 15 * KB + 320)
    Pcn = alloc_at("Pcn", [128, 256], BF16, A11 + 38 * KB)
    Scm = alloc_at("Scm", [128, 16 * 3], F32, A11 + 38 * KB + 512)
    Sct = alloc_at("Sct", [128, 256], F32, A11 + 39 * KB)
    KT = alloc_at("KT", [128, 2 * S], BF16, AKT)
    o2_tok = alloc_at("o2_tok", [16, KB], F32, AKT)
    Vctok = alloc_at("Vctok", [16, KB], F32, AKT + 4 * KB)
    eb1c = alloc_at("eb1c", [32, 256], F32, AKT + 8 * KB)
    ebgc = alloc_at("ebgc", [32, 256], F32, AKT + 9 * KB)
    eb2c = alloc_at("eb2c", [8, 256], F32, AKT + 10 * KB)
    hT = alloc_at("hT", [128, KB], BF16, AKT + 11 * KB)
    h_sb = alloc_at("h_sb", [128, KB], BF16, AKT + 13 * KB)
    qstg = [alloc_at(f"qstg{i}", [128, 512], BF16, A5 + i * KB) for i in range(2)]

    PS_guard = nc.psum_tensor("PS", [128, 4096], F32)
    PS = PS_guard.__enter__()
    PSb = PS.ap().bitcast(BF16)

    def psf(bank, cols=512, rows=128, row0=0):
        return PS[row0:row0 + rows, bank * 512:bank * 512 + cols]

    def psfc(bank, c0, cols, rows=128, row0=0):
        return PS[row0:row0 + rows, bank * 512 + c0:bank * 512 + c0 + cols]

    def psb(bank, c0, cols, rows=128, row0=0):
        return PSb[row0:row0 + rows, bank * 1024 + c0:bank * 1024 + c0 + cols]

    sem_names = ["tM", "vV", "sS", "gP", "cc", "qO", "qM",
                 "qW0", "qW1", "qX0", "qX1", "qQ0", "qQ1", "qT0", "qT1",
                 "qR0", "qR1", "qS0", "qS1", "qLG", "qRB", "qLD0", "qLD1", "qRL",
                 "qF0", "qF1", "qB", "qC", "qE0", "qE1", "qO2",
                 "qPW0", "qPW1", "qPB0", "qPB1"]
    sems = {}
    sem_guards = []
    for nm in sem_names:
        g = nc.semaphore(nm)
        sems[nm] = g.__enter__()
        sem_guards.append(g)

    def ch(t, n):
        return t[:].rearrange("p (c n) -> p c n", n=n)

    def walk(w: Walk):
        E = w.emit

        def tap(name, ap_f, deps):
            if name in dtaps:
                E("gpsimd", lambda e: e.dma_start(out=dtaps[name], in_=ap_f()), sem="qM",
                  deps=deps, key=("tap", name))

        E("gpsimd", lambda e: e.memset(ident[:], 0.0), sem="gP", key="identz")
        E("gpsimd", lambda e: e.affine_select(out=ident[:], in_=ident[:], compare_op=ALU.not_equal,
                                              fill=1.0, base=0, pattern=[[-1, 128]], channel_multiplier=1),
          sem="gP", deps=["identz"], key="ident")
        E("gpsimd", lambda e: e.memset(identf[:], 0.0), sem="gP", deps=["ident"], key="identfz")
        E("gpsimd", lambda e: e.affine_select(out=identf[:], in_=identf[:], compare_op=ALU.not_equal,
                                              fill=1.0, base=0, pattern=[[-1, 128]], channel_multiplier=1),
          sem="gP", deps=["identfz"], key="identf")
        E("gpsimd", lambda e: e.memset(trimask[:], 1.0), sem="gP", deps=["identf"], key="trimaskz")
        E("gpsimd", lambda e: e.affine_select(out=trimask[:], in_=trimask[:], compare_op=ALU.is_ge,
                                              fill=0.0, base=0, pattern=[[1, 128]], channel_multiplier=-1),
          sem="gP", deps=["trimaskz"], key="trimask")
        E("gpsimd", lambda e: e.memset(onesc[:], 1.0), sem="gP", key="onesc")
        E("gpsimd", lambda e: e.memset(V_sb[:], 1.0), sem="gP", key="vones")
        E("gpsimd", lambda e: e.memset(nrm[:, 0:2], 0.0), sem="gP", key="bz0")
        E("gpsimd", lambda e: e.memset(nrm[0:64, 0:1], 1.0), sem="gP", deps=["bz0"])
        E("gpsimd", lambda e: e.memset(nrm[64:128, 1:2], 1.0), sem="gP", deps=["bz0"])
        E("gpsimd", lambda e: e.memset(bonesb[:], 0.0), sem="gP", key="bz1")
        E("gpsimd", lambda e: e.memset(bonesb[0:64, 0:1], 1.0), sem="gP", deps=["bz1"])
        E("gpsimd", lambda e: e.memset(bonesb[64:128, 1:2], 1.0), sem="gP", deps=["bz1"], key="bones")
        E("gpsimd", lambda e: e.memset(qnacc[:, 0:16], 0.0), sem="gP", key="qnz")
        E("gpsimd", lambda e: e.memset(nrm[0:2, 40:48], 0.0), sem="gP", key="qnz2")
        E("gpsimd", lambda e: e.memset(ones64[64:65, 0:64], 1.0), sem="gP", key="ones64")

        prev_tag = [None]

        def prep_attn_w(tag, dsrc, nrows, dstT, cols_per_chunk, permute_q=False):
            nrc = nrows // 128
            pt = prev_tag[0]
            for rc in range(nrc):
                stg = wstg[rc % 2]
                E("sync", lambda e, rc=rc, stg=stg: e.dma_start(out=stg[:], in_=dsrc[rc * 128:(rc + 1) * 128, :]),
                  sem="qW%d" % (rc % 2),
                  deps=[("wprep_cast", tag, rc - 2) if rc >= 2 else (("wprep_done", pt) if pt else None)],
                  key=("wprep_ld", tag, rc))
                E("vector", lambda e, stg=stg: e.tensor_copy(out=xnsc[:], in_=stg[:]),
                  sem="vV",
                  deps=[("wprep_ld", tag, rc),
                        ("wprep_tr", tag, rc - 1) if rc >= 1 else (("wprep_done", pt) if pt else None)],
                  key=("wprep_cast", tag, rc))
                for dcn in range(8):
                    E("tensor", lambda e, dcn=dcn: e.transpose(out=psb(7, dcn * 128, 128),
                                                               in_=xnsc[:, dcn * 128:(dcn + 1) * 128],
                                                               identity=ident[:]),
                      sem="tM" if dcn == 7 else None,
                      deps=[("wprep_cast", tag, rc), "ident",
                            ("wprep_ev", tag, rc - 1) if rc >= 1 else (("wprep_done", pt) if pt else None)],
                      key=("wprep_tr", tag, rc) if dcn == 7 else None)
                if permute_q:
                    for s2 in range(2):
                        h = 2 * rc + s2
                        dc_ = qcol(h)
                        E("vector", lambda e, s2=s2, dc_=dc_: e.tensor_copy(
                            out=ch(dstT, cols_per_chunk)[:, :, dc_:dc_ + 64],
                            in_=psb(7, 0, 1024).rearrange("p (c n) -> p c n", n=128)[:, :, s2 * 64:(s2 + 1) * 64]),
                          sem="vV" if s2 == 1 else None, deps=[("wprep_tr", tag, rc)],
                          key=("wprep_ev", tag, rc) if s2 == 1 else None)
                else:
                    E("vector", lambda e, rc=rc: e.tensor_copy(
                        out=ch(dstT, cols_per_chunk)[:, :, rc * 128:(rc + 1) * 128],
                        in_=psb(7, 0, 1024).rearrange("p (c n) -> p c n", n=128)),
                      sem="vV", deps=[("wprep_tr", tag, rc)], key=("wprep_ev", tag, rc))
            w.mark(("wprep_done", tag), "vV")
            prev_tag[0] = tag

        prep_attn_w("wq", dwq, D, wqT, KB, permute_q=True)
        prep_attn_w("wk", dwk, 256, wkT, 256)
        prep_attn_w("wv", dwv, 256, wvT, 256)
        prep_attn_w("wo", dwo, D, woT, KB)

        def ln_stats(ti, xbuf, kpref, sdst, bdst, extra0=()):
            K = lambda nm: (kpref, nm, ti)
            E("vector", lambda e: e.reduce_sum(out=lnst[:, 0:1], in_=xbuf[:], axis=AX),
              sem="vV", deps=[*extra0], key=K("sum"))
            E("vector", lambda e: e.tensor_mul(out=sqf32[:], in0=xbuf[:], in1=xbuf[:]), sem="vV", key=K("sq"))
            E("vector", lambda e: e.reduce_sum(out=lnst[:, 1:2], in_=sqf32[:], axis=AX),
              sem="vV", deps=[K("sq")], key=K("sumsq"))
            E("vector", lambda e: e.tensor_scalar_mul(out=lnst[:, 2:3], in0=lnst[:, 0:1], scalar1=1.0 / D),
              sem="vV", deps=[K("sum")], key=K("mean"))
            E("vector", lambda e: e.tensor_scalar_mul(out=lnst[:, 3:4], in0=lnst[:, 1:2], scalar1=1.0 / D),
              sem="vV", deps=[K("sumsq")], key=K("ms"))
            E("vector", lambda e: e.tensor_mul(out=lnst[:, 4:5], in0=lnst[:, 2:3], in1=lnst[:, 2:3]),
              sem="vV", deps=[K("mean")], key=K("m2"))
            E("vector", lambda e: e.tensor_sub(out=lnst[:, 3:4], in0=lnst[:, 3:4], in1=lnst[:, 4:5]),
              sem="vV", deps=[K("ms"), K("m2")], key=K("v0"))
            E("vector", lambda e: e.tensor_scalar_add(out=lnst[:, 3:4], in0=lnst[:, 3:4], scalar1=EPS),
              sem="vV", deps=[K("v0")], key=K("var"))
            E("scalar", lambda e: e.activation(out=lnst[:, 5:6], in_=lnst[:, 3:4], func=AF.Ln, bias=0.0, scale=1.0),
              sem="sS", deps=[K("var")], key=K("ln"))
            E("scalar", lambda e: e.activation(out=sdst, in_=lnst[:, 5:6], func=AF.Exp, bias=0.0, scale=-0.5),
              sem="sS", deps=[K("ln")], key=K("rstd"))
            E("vector", lambda e: e.tensor_mul(out=lnst[:, 6:7], in0=lnst[:, 2:3], in1=sdst),
              sem="vV", deps=[K("rstd")], key=K("mr"))
            E("vector", lambda e: e.tensor_scalar_mul(out=bdst, in0=lnst[:, 6:7], scalar1=-1.0),
              sem="vV", deps=[K("mr"), ("xnapply", ti - 2) if (kpref == "l1" and ti >= 2) else None],
              key=(kpref + "bias", ti))

        def ln_tile(ti, xbuf):
            p = ti % 2
            sdst = lnst[:, 8 + 2 * p:9 + 2 * p]
            bdst = lnst[:, 9 + 2 * p:10 + 2 * p]
            ln_stats(ti, xbuf, "l1", sdst, bdst,
                     extra0=[("xld", ti), ("l1", "rstd", ti - 1) if ti >= 1 else None,
                             ("tap", "lnst0") if ("lnst0" in dtaps and ti == 2) else None])
            E("scalar", lambda e: e.activation(out=xnsc[:], in_=xbuf[:], func=AF.Identity, bias=bdst, scale=sdst),
              sem="sS",
              deps=[("l1bias", ti), ("xntr", ti - 1) if ti >= 1 else ("wprep_done", "wo"),
                    ("tap", "xnsc0") if ("xnsc0" in dtaps and ti == 1) else None],
              key=("xnapply", ti))

        def xn_transpose(ti, dstcol):
            bank = 6 if ti % 2 == 0 else 7
            for dcn in range(8):
                E("tensor", lambda e, dcn=dcn, bank=bank: e.transpose(
                    out=psb(bank, dcn * 128, 128), in_=xnsc[:, dcn * 128:(dcn + 1) * 128], identity=ident[:]),
                  sem="tM" if dcn == 7 else None,
                  deps=[("xnapply", ti), "ident",
                        ("xntrev", ti - 2) if ti >= 2 else ("wprep_done", "wo"),
                        ("tap", "xng0") if ("xng0" in dtaps and ti == 4) else None],
                  key=("xntr", ti) if dcn == 7 else None)
            E("scalar", lambda e, bank=bank, dstcol=dstcol: e.copy(
                out=ch(xnTg, 512)[:, :, dstcol:dstcol + 128],
                in_=psb(bank, 0, 1024).rearrange("p (c n) -> p c n", n=128)),
              sem="sS", deps=[("xntr", ti)], key=("xntrev", ti))

        for Jg in range(NJ):
            for tt in range(4):
                ti = Jg * 4 + tt
                xbuf = xt[ti % 2]
                E("sync", lambda e, ti=ti, xbuf=xbuf: e.dma_start(out=xbuf[:], in_=dx[ti * 128:(ti + 1) * 128, :]),
                  sem="qX%d" % (ti % 2), deps=[("xnapply", ti - 2)] if ti >= 2 else [], key=("xld", ti))
                ln_tile(ti, xbuf)
                if ti == 0:
                    tap("xnsc0", lambda: xnsc[:], [("xnapply", 0)])
                    tap("lnst0", lambda: lnst[:], [("xnapply", 0)])
                xn_transpose(ti, tt * 128)
                if ti == 3:
                    tap("xng0", lambda: xnTg[:], [("xntrev", 3)])
                for dcn in range(8):
                    E("tensor", lambda e, dcn=dcn, tt=tt: e.matmul(
                        out=psf(4, 256),
                        lhsT=ch(xnTg, 512)[:, dcn, tt * 128:(tt + 1) * 128],
                        rhs=ch(wvT, 256)[:, dcn, :],
                        start=(dcn == 0), stop=(dcn == 7)),
                      sem="tM" if dcn == 7 else None,
                      deps=[("xntrev", ti), ("wprep_done", "wv"),
                            ("vev", ti - 1) if ti >= 1 else "vones"],
                      key=("vmm", ti) if dcn == 7 else None)
                E("scalar", lambda e, ti=ti: e.copy(
                    out=V_sb[:].rearrange("p (t g n) -> p t g n", g=4, n=65)[:, ti, :, 0:64],
                    in_=psf(4, 256).rearrange("p (g n) -> p g n", n=64)),
                  sem="sS", deps=[("vmm", ti), "vones"], key=("vev", ti))
            for qc in range(8):
                bank = 2 + qc % 2
                for dcn in range(8):
                    E("tensor", lambda e, dcn=dcn, qc=qc, bank=bank: e.matmul(
                        out=psf(bank, 512),
                        lhsT=ch(wqT, KB)[:, dcn, qc * 128:(qc + 1) * 128],
                        rhs=ch(xnTg, 512)[:, dcn, :],
                        start=(dcn == 0), stop=(dcn == 7)),
                      sem="tM" if dcn == 7 else None,
                      deps=[("xntrev", Jg * 4 + 3), ("wprep_done", "wq"),
                            ("qev", Jg, qc - 2) if qc >= 2 else (("qev", Jg - 1, 6 + qc) if Jg >= 1 else None)],
                      key=("qmm", Jg, qc) if dcn == 7 else None)
                stq = qstg[qc % 2]
                E("scalar", lambda e, bank=bank, stq=stq: e.copy(out=stq[:], in_=psf(bank, 512)),
                  sem="sS", deps=[("qmm", Jg, qc), ("qdma", Jg, qc - 2) if qc >= 2 else
                                  (("qdma", Jg - 1, 6 + qc) if Jg >= 1 else None)],
                  key=("qev", Jg, qc))
                E("gpsimd", lambda e, qc=qc, Jg=Jg, stq=stq: e.dma_start(
                    out=dqt[qc, :, Jg * 512:(Jg + 1) * 512], in_=stq[:]),
                  sem="qQ%d" % (qc % 2), deps=[("qev", Jg, qc)], key=("qdma", Jg, qc))
                E("scalar", lambda e, stq=stq: e.activation(out=sqsc[:], in_=stq[:], func=AF.Square, bias=0.0, scale=1.0),
                  sem="sS",
                  deps=[("qev", Jg, qc),
                        ("qnmm", Jg, qc - 1) if qc >= 1 else (("knmm", Jg - 1, 1) if Jg >= 1 else None)],
                  key=("qsq", Jg, qc))
                E("tensor", lambda e: e.matmul(out=psf(5, 512, rows=2), lhsT=bonesb[:], rhs=sqsc[:],
                                               start=True, stop=True),
                  sem="tM", deps=[("qsq", Jg, qc), "bones",
                                  ("qnred0", Jg, qc - 1) if qc >= 1 else (("knred0", Jg - 1, 1) if Jg >= 1 else None)],
                  key=("qnmm", Jg, qc))
                E("vector", lambda e, qc=qc: e.reduce_max(out=qnacc[0:2, 16 + qc:17 + qc],
                                                          in_=psf(5, 512, rows=2), axis=AX),
                  sem="vV", deps=[("qnmm", Jg, qc), ("qnupd", Jg - 1, qc) if Jg >= 1 else "qnz"],
                  key=("qnred0", Jg, qc))
                E("vector", lambda e, qc=qc: e.tensor_max(out=qnacc[0:2, qc:qc + 1],
                                                          in0=qnacc[0:2, qc:qc + 1],
                                                          in1=qnacc[0:2, 16 + qc:17 + qc]),
                  sem="vV", deps=[("qnred0", Jg, qc), "qnz"], key=("qnupd", Jg, qc))
            for kc in range(2):
                bank = 2 + kc
                for dcn in range(8):
                    E("tensor", lambda e, dcn=dcn, kc=kc, bank=bank: e.matmul(
                        out=psf(bank, 512),
                        lhsT=ch(wkT, 256)[:, dcn, kc * 128:(kc + 1) * 128],
                        rhs=ch(xnTg, 512)[:, dcn, :],
                        start=(dcn == 0), stop=(dcn == 7)),
                      sem="tM" if dcn == 7 else None,
                      deps=[("xntrev", Jg * 4 + 3), ("wprep_done", "wk"), ("qev", Jg, 6 + kc)],
                      key=("kmm", Jg, kc) if dcn == 7 else None)
                E("scalar", lambda e, kc=kc, Jg=Jg, bank=bank: e.copy(
                    out=KT[:, kc * S + Jg * 512:kc * S + (Jg + 1) * 512], in_=psf(bank, 512)),
                  sem="sS", deps=[("kmm", Jg, kc)], key=("kev", Jg, kc))
                E("scalar", lambda e, kc=kc, Jg=Jg: e.activation(
                    out=sqsc[:], in_=KT[:, kc * S + Jg * 512:kc * S + (Jg + 1) * 512],
                    func=AF.Square, bias=0.0, scale=1.0),
                  sem="sS", deps=[("kev", Jg, kc),
                                  ("knmm", Jg, kc - 1) if kc >= 1 else ("qnmm", Jg, 7)],
                  key=("ksq", Jg, kc))
                E("tensor", lambda e: e.matmul(out=psf(5, 512, rows=2), lhsT=bonesb[:], rhs=sqsc[:],
                                               start=True, stop=True),
                  sem="tM", deps=[("ksq", Jg, kc), "bones",
                                  ("knred0", Jg, kc - 1) if kc >= 1 else ("qnred0", Jg, 7)],
                  key=("knmm", Jg, kc))
                E("vector", lambda e, kc=kc: e.reduce_max(out=qnacc[0:2, 24 + kc:25 + kc],
                                                          in_=psf(5, 512, rows=2), axis=AX),
                  sem="vV", deps=[("knmm", Jg, kc), ("knupd", Jg - 1, kc) if Jg >= 1 else "qnz"],
                  key=("knred0", Jg, kc))
                E("vector", lambda e, kc=kc: e.tensor_max(out=qnacc[0:2, 8 + kc:9 + kc],
                                                          in0=qnacc[0:2, 8 + kc:9 + kc],
                                                          in1=qnacc[0:2, 24 + kc:25 + kc]),
                  sem="vV", deps=[("knred0", Jg, kc), "qnz"], key=("knupd", Jg, kc))

        E("scalar", lambda e: e.activation(out=qnacc[0:2, 32:40], in_=qnacc[0:2, 0:8], func=AF.Sqrt,
                                           bias=0.0, scale=1.0),
          sem="sS", deps=[("qnupd", NJ - 1, 7)], key="qsqrt")
        E("scalar", lambda e: e.activation(out=qnacc[0:2, 40:42], in_=qnacc[0:2, 8:10], func=AF.Sqrt,
                                           bias=0.0, scale=1.0),
          sem="sS", deps=[("knupd", NJ - 1, 1)], key="ksqrt")
        E("gpsimd", lambda e: e.dma_start(out=qnacc[0:1, 48:58], in_=qnacc[0:1, 32:42]),
          sem="qM", deps=["qsqrt", "ksqrt"])
        E("gpsimd", lambda e: e.dma_start(out=qnacc[0:1, 58:68], in_=qnacc[1:2, 32:42]),
          sem="qM", deps=["qsqrt", "ksqrt"], key="nflat")
        for slot in range(8):
            for s2 in range(2):
                h = PAIRS[slot][s2]
                qcol_ = 48 + (h % 2) * 10 + h // 2
                kcol_ = 48 + ((h >> 2) % 2) * 10 + 8 + (h >> 2) // 2
                col = slot * 2 + s2
                E("vector", lambda e, qcol_=qcol_, kcol_=kcol_, col=col: e.tensor_tensor(
                    out=nrm[0:1, 32 + col:33 + col], in0=qnacc[0:1, qcol_:qcol_ + 1],
                    in1=qnacc[0:1, kcol_:kcol_ + 1], op=ALU.mult),
                  sem="vV", deps=["nflat"], key=("mh0", col))
                E("vector", lambda e, col=col: e.tensor_scalar_mul(
                    out=nrm[0:1, 32 + col:33 + col], in0=nrm[0:1, 32 + col:33 + col], scalar1=-SCALE * 1.02),
                  sem="vV", deps=[("mh0", col)], key=("mh", col))
        E("tensor", lambda e: e.matmul(out=psf(5, 16), lhsT=onesc[0:1, :], rhs=nrm[0:1, 32:48],
                                       start=True, stop=True),
          sem="tM", deps=[("mh", 15), "onesc", ("knred0", NJ - 1, 1)], key="mhbc")
        E("vector", lambda e: e.tensor_copy(out=bias16[:], in_=psf(5, 16)), sem="vV",
          deps=["mhbc"], key="bias16")
        for pi_ in range(8):
            E("vector", lambda e, pi_=pi_: e.tensor_tensor(
                out=bias16p[:, pi_:pi_ + 1], in0=bias16[:, 2 * pi_:2 * pi_ + 1],
                in1=bias16[:, 2 * pi_ + 1:2 * pi_ + 2], op=ALU.min),
              sem="vV", deps=["bias16"], key="bias16pz" if pi_ == 7 else None)
        tap("qt", lambda: dqt, [("qdma", NJ - 1, 7)])
        tap("kt", lambda: KT[:], [("kev", NJ - 1, 1)])
        tap("vsb", lambda: V_sb[:], [("vev", NT - 1)])
        tap("bias16", lambda: bias16[:], ["bias16"])
        tap("wqT", lambda: wqT[:], [("wprep_done", "wq")])

        def ln2_tile(ti, xbuf):
            ln_stats(ti, xbuf, "l2", st_s[:, ti:ti + 1], st_b[:, ti:ti + 1],
                     extra0=[("x1ev", ti), ("l2", "rstd", ti - 1) if ti >= 1 else None])

        FNAMES = ("dWq", "dWk", "dWv", "dWo", "cWq", "cWk", "cWv", "cWo")
        pp_pieces = []
        _sc = [0, 0]

        def _add_small(src, dst, skey):
            vs = _sc[0] % 8
            pp_pieces.append(dict(big=False, sl=vs % 2, vq=vs // 2, src=src,
                                  dsts=[(dst, 0)], skey=skey))
            _sc[0] += 1

        for m, nm in enumerate(FNAMES):
            for rc in range(8):
                _add_small(dW[nm][rc * 128:(rc + 1) * 128, :], dewbF[m][:, rc:rc + 1, :], ("pwsF", nm, rc))
        for mat, dsrc in ((0, deW1), (1, deWg)):
            for le in range(2):
                for rcb in range(8):
                    dsts = [(dewb1[(mat * 2 + le) * 4 + q4][:, rcb:rcb + 1, :], q4) for q4 in range(4)]
                    pp_pieces.append(dict(big=True, sl=_sc[1] % 2, vq=None,
                                          src=dsrc[le, rcb * 128:(rcb + 1) * 128, :],
                                          dsts=dsts, skey=("pws1", mat, le, rcb)))
                    _sc[1] += 1
        for le in range(2):
            for wv2 in range(4):
                for rc in range(8):
                    _add_small(deW2[le, wv2 * KB + rc * 128:wv2 * KB + (rc + 1) * 128, :],
                               dewb2[le * 4 + wv2][:, rc:rc + 1, :], ("pws2", le, wv2, rc))

        pp_state = dict(loads=0, casts=0,
                        lastload={0: None, 1: None},
                        lastcast={},
                        laststore={0: ("wprep_cast", "wo", 6), 1: ("wprep_cast", "wo", 7)})

        def pp_quota(t):
            q = min(max(t - 8, 0) * 6, 64)
            q += min(max(t - 19, 0) * 2, 32)
            q += min(max(t - 35, 0) * 4, 64)
            return q

        def pp_emit_n(nl, ncst):
            st = pp_state
            while st["casts"] < min(ncst, len(pp_pieces)):
                k = st["casts"]
                pc = pp_pieces[k]
                sl = pc["sl"]
                ckeys = []
                if pc["big"]:
                    for h in range(2):
                        E("gpsimd", lambda e, h=h, sl=sl: e.tensor_copy(
                            out=pwb[h][:], in_=pstg[sl][:, h * 2048:(h + 1) * 2048]),
                          sem="gP", deps=[st["lastload"][sl], st["laststore"][h]],
                          key=("pwc", k, h))
                        ckeys.append(("pwc", k, h))
                    st["lastcast"][("p", sl)] = ckeys
                    for dst, q4 in pc["dsts"]:
                        h = q4 // 2
                        col = (q4 % 2) * KB
                        E("gpsimd", lambda e, dst=dst, h=h, col=col: e.dma_start(
                            out=dst, in_=pwb[h][:, col:col + KB].rearrange("p (o n) -> p o n", o=1)),
                          sem="qPB%d" % h, deps=[("pwc", k, h)], key=(*pc["skey"], q4))
                        st["laststore"][h] = (*pc["skey"], q4)
                else:
                    h = sl
                    vq = pc["vq"]
                    E("gpsimd", lambda e, h=h, sl=sl, vq=vq: e.tensor_copy(
                        out=pwb[h][:, 0:KB], in_=pstg[sl][:, vq * KB:(vq + 1) * KB]),
                      sem="gP", deps=[st["lastload"][sl], st["laststore"][h]],
                      key=("pwc", k, 0))
                    st["lastcast"][("p", sl, vq)] = [("pwc", k, 0)]
                    dst, _ = pc["dsts"][0]
                    E("gpsimd", lambda e, dst=dst, h=h: e.dma_start(
                        out=dst, in_=pwb[h][:, 0:KB].rearrange("p (o n) -> p o n", o=1)),
                      sem="qPB%d" % h, deps=[("pwc", k, 0)], key=(*pc["skey"], 0))
                    st["laststore"][h] = (*pc["skey"], 0)
                st["casts"] = k + 1
            while st["loads"] < min(nl, len(pp_pieces)):
                k = st["loads"]
                pc = pp_pieces[k]
                sl = pc["sl"]
                if pc["big"]:
                    rs = [("p", sl)] + [("p", sl, q) for q in range(4)]
                    tgt = pstg[sl][:]
                else:
                    rs = [("p", sl), ("p", sl, pc["vq"])]
                    tgt = pstg[sl][:, pc["vq"] * KB:(pc["vq"] + 1) * KB]
                deps = []
                for r in rs:
                    deps += st["lastcast"].get(r, [])
                E("gpsimd", lambda e, tgt=tgt, src=pc["src"]: e.dma_start(out=tgt, in_=src),
                  sem="qPW%d" % sl, deps=deps, key=("pwl", k))
                st["lastload"][sl] = ("pwl", k)
                st["loads"] = k + 1

        def pp_emit(t):
            pp_emit_n(pp_quota(t), pp_quota(t - 1))

        def kinfo(h):
            g = h >> 2
            return g // 2, 64 * (g % 2)

        def qinfo(h):
            if h in B0SET:
                return B0SET.index(h), 0
            return B64SET.index(h), 64

        for J in range(NJ):
            qb = QTJ[J % 2]
            for qcc in range(8):
                E("sync", lambda e, J=J, qb=qb, qcc=qcc: e.dma_start(
                    out=qb[:, qcc * 512:(qcc + 1) * 512], in_=dqt[qcc, :, J * 512:(J + 1) * 512]),
                  sem="qT%d" % (J % 2),
                  deps=[("qdma", J, 6), ("qdma", J, 7), ("scJlast", J - 2) if J >= 2 else None],
                  key=("qtj", J) if qcc == 7 else None)
            ntt = 4 * J + 4
            nbt = _ceil(ntt, 2)
            for pi in range(8):
                ha, hb = PAIRS[pi]
                ob = 4 if pi % 2 == 0 else 6
                for ts in range(ntt):
                    it = ts
                    c0 = max(0, (it - 4 * J) * 128)
                    first_dep = None
                    extra_dep = None
                    if ts == 0:
                        if pi >= 2:
                            first_dep = ("ctxevp", J, pi - 2)
                        elif J >= 1:
                            first_dep = ("Jwo", J - 1)
                            extra_dep = ("x1ev", 4 * J - 1)
                        else:
                            first_dep = "bias16"
                    for s2 in range(2):
                        h = (ha, hb)[s2]
                        kc2, kr = kinfo(h)
                        qc2, qr = qinfo(h)
                        E("tensor", lambda e, kc2=kc2, kr=kr, qc2=qc2, qr=qr, it=it, s2=s2, c0=c0, ts=ts, qb=qb: e.matmul(
                            out=psfc((ts % 2) * 2 + s2, c0, 512 - c0),
                            lhsT=KT[kr:kr + 64, kc2 * S + it * 128:kc2 * S + (it + 1) * 128],
                            rhs=ch(qb, 512)[qr:qr + 64, qc2, c0:512],
                            start=True, stop=True),
                          sem="tM" if s2 == 1 else None,
                          deps=[("qtj", J), ("kev", NJ - 1, 1), "bias16",
                                ("vev", NT - 1) if (J == 0 and pi == 0 and ts == 0) else None,
                                ("exp", J, pi, ts - 2) if ts >= 2 else None,
                                first_dep, extra_dep],
                          key=("sc", J, pi, ts) if s2 == 1 else None)
                    E("scalar", lambda e, ptp=PTpair[ts % 2], pi=pi, ts=ts: e.activation(
                        out=ptp[:, 0:1024], in_=PS[:, (ts % 2) * 1024:(ts % 2) * 1024 + 1024],
                        func=AF.Exp, bias=bias16p[:, pi:pi + 1], scale=SCALE),
                      sem="sS",
                      deps=[("sc", J, pi, ts), "bias16pz",
                            ("pvdone", J, pi, ts - 2) if ts >= 2 else first_dep,
                            ("mskd", J, pi, ts - 2) if (ts >= 2 and ts - 2 >= 4 * J) else None],
                      key=("exph", J, pi, ts, 1))
                    w.mark(("exph", J, pi, ts, 0), "sS")
                    w.mark(("exp", J, pi, ts), "sS")
                    anymask = it >= 4 * J
                    if anymask:
                        cb = (it - 4 * J) * 128
                        for s2 in range(2):
                            ptb = PTpair[ts % 2]
                            E("gpsimd", lambda e, cb=cb, s2=s2, ptb=ptb: e.tensor_mul(
                                out=ptb[:, s2 * 512 + cb:s2 * 512 + cb + 128],
                                in0=ptb[:, s2 * 512 + cb:s2 * 512 + cb + 128], in1=trimask[:]),
                              sem="gP", deps=[("exph", J, pi, ts, s2), "trimask"],
                              key=("msk", J, pi, ts))
                        w.mark(("mskd", J, pi, ts), "gP")
                    for s2 in range(2):
                        h = (ha, hb)[s2]
                        g = h >> 2
                        ptb = PTpair[ts % 2]
                        E("tensor", lambda e, s2=s2, it=it, c0=c0, g=g, ptb=ptb, ob=ob, ntt=ntt: e.matmul(
                            out=psfc(ob + s2, c0, 512 - c0, rows=65),
                            lhsT=V_sb[:].rearrange("p (t gg n) -> p t gg n", gg=4, n=65)[:, it, g, :],
                            rhs=ptb[:, s2 * 512 + c0:(s2 + 1) * 512],
                            start=(it == 0), stop=(it == ntt - 1)),
                          sem="tM" if s2 == 1 else None,
                          deps=[("exp", J, pi, ts),
                                ("mskd", J, pi, ts) if anymask else None,
                                ("ctxevp", J, pi - 2) if (ts == 0 and pi >= 2) else None],
                          key=("pv", J, pi, ts) if s2 == 1 else None)
                    w.mark(("pvdone", J, pi, ts), "tM")
                if pi % 2 == 1:
                    cT = ctxT[J % 2]
                    for pj in (pi - 1, pi):
                        obj = 4 if pj % 2 == 0 else 6
                        if pj == pi:
                            lwar = ("drl", J, pi - 1, 1)
                        elif pi >= 3:
                            lwar = ("drl", J, pi - 2, 1)
                        else:
                            lwar = ("drl", J - 1, 7, 1) if J >= 1 else None
                        E("vector", lambda e, obj=obj: e.reciprocal(
                            out=lstage[64:65, 0:1024], in_=PS[64:65, obj * 512:(obj + 2) * 512]),
                          sem="vV", deps=[("pvdone", J, pj, ntt - 1), lwar],
                          key=("dlrec", J, pj))
                        for s2 in range(2):
                            E("tensor", lambda e, s2=s2, obj=obj: e.matmul(
                                out=PS[64:128, (obj + s2) * 512:(obj + s2 + 1) * 512],
                                lhsT=ones64[64:65, 0:64],
                                rhs=lstage[64:65, s2 * 512:(s2 + 1) * 512],
                                start=True, stop=True, tile_position=(64, 64)),
                              sem="tM",
                              deps=[("dlrec", J, pj), "ones64"],
                              key=("drl", J, pj, s2))
                        for s2 in range(2):
                            hc = (pj * 2 + s2) % 4
                            E("vector", lambda e, obj=obj, s2=s2, hc=hc: e.tensor_copy(
                                out=rlb[0:64, hc * 512:(hc + 1) * 512],
                                in_=PS[64:128, (obj + s2) * 512:(obj + s2 + 1) * 512]),
                              sem="vV", deps=[("drl", J, pj, s2)], key=("drle", J, pj, s2))
                        for s2 in range(2):
                            h = PAIRS[pj][s2]
                            hc = (pj * 2 + s2) % 4
                            E("vector", lambda e, h=h, obj=obj, s2=s2, hc=hc, cT=cT: e.tensor_tensor(
                                out=ch(cT, 512)[64 * (h % 2):64 * (h % 2) + 64, h // 2, :],
                                in0=PS[0:64, (obj + s2) * 512:(obj + s2 + 1) * 512],
                                in1=rlb[:, hc * 512:(hc + 1) * 512],
                                op=ALU.mult),
                              sem="vV",
                              deps=[("drle", J, pj, s2),
                                    ("Jwo", J - 2) if (J >= 2 and pj == pi - 1 and s2 == 0) else None],
                              key=("ctxevp", J, pj) if s2 == 1 else None)
                pp_emit(J * 8 + pi + 1)
            w.mark(("ctxevd", J), "vV")
            w.mark(("scJlast", J), "tM")
            cT = ctxT[J % 2]
            for tt in range(4):
                ti = J * 4 + tt
                wb = 0 if tt % 2 == 0 else 2
                for dcn in range(8):
                    for hf in range(2):
                        E("tensor", lambda e, dcn=dcn, hf=hf, tt=tt, wb=wb, cT=cT: e.matmul(
                            out=psf(wb + hf, 512),
                            lhsT=ch(cT, 512)[:, dcn, tt * 128:(tt + 1) * 128],
                            rhs=ch(woT, KB)[:, dcn, hf * 512:(hf + 1) * 512],
                            start=(dcn == 0), stop=(dcn == 7)),
                          sem="tM" if (dcn == 7 and hf == 1) else None,
                          deps=[("ctxevd", J), ("wprep_done", "wo"),
                                ("x1ev", ti - 2) if tt >= 2 else (("x1ev", J * 4 - 1) if (tt == 0 and J >= 1) else None)],
                          key=("womm", ti) if (dcn == 7 and hf == 1) else None)
                xre = xt[2 + ti % 2]
                E("sync", lambda e, ti=ti, xre=xre: e.dma_start(out=xre[:], in_=dx[ti * 128:(ti + 1) * 128, :]),
                  sem="qR%d" % (ti % 2),
                  deps=[("x1st", ti - 2) if ti >= 2 else None,
                        ("l2", "sq", ti - 2) if ti >= 2 else None], key=("xreld", ti))
                E("vector", lambda e, wb=wb, xre=xre: e.tensor_add(out=xre[:], in0=xre[:],
                                                                   in1=PS[:, wb * 512:(wb + 2) * 512]),
                  sem="vV", deps=[("womm", ti), ("xreld", ti)], key=("x1ev", ti))
                E("gpsimd", lambda e, ti=ti, xre=xre: e.dma_start(out=dx1[ti * 128:(ti + 1) * 128, :], in_=xre[:]),
                  sem="qS%d" % (ti % 2), deps=[("x1ev", ti)], key=("x1st", ti))
                ln2_tile(ti, xre)
            w.mark(("Jwo", J), "tM")

        tap("ctx0", lambda: ctxT[0][:], [("ctxevd", 0)])
        tap("x1", lambda: dx1, [("x1st", NT - 1)])

        FD = ("Jwo", NJ - 1)
        X1ALL = [("x1st", NT - 2), ("x1st", NT - 1)]

        for _k in range(len(pp_pieces)):
            pp_emit_n(_k + 1, _k)
        pp_emit_n(len(pp_pieces), len(pp_pieces))
        PP_DONE = [pp_state["laststore"][0], pp_state["laststore"][1]]
        FIDX = {nm: i for i, nm in enumerate(FNAMES)}
        _slot_first = [True, True]

        def load_ffnW(tag, dsrc, slot, free_ev=None):
            deps = [free_ev]
            if _slot_first[slot]:
                deps += PP_DONE
                _slot_first[slot] = False
            E("sync", lambda e, m=FIDX[tag], slot=slot: e.dma_start(
                out=ffnW[slot][:].rearrange("p (a n) -> p a n", n=KB), in_=dewbF[m]),
              sem="qW%d" % slot, deps=deps, key=(tag, "cast", 7))
            w.mark((tag, "cast", 6), "qW%d" % slot)
            w.mark((tag, "done"), "qW%d" % slot)

        qBprev = [None]

        def qb_dma(kind_eng, f, key, extra=()):
            E(kind_eng, f, sem="qB" if kind_eng == "sync" else "qC", deps=[qBprev[0], *extra], key=key)
            qBprev[0] = key

        BIAS_ORDER = ("dbk", "dbq", "dbo", "cbq", "cbk", "cbv")
        for i, nm in enumerate(BIAS_ORDER):
            qb_dma("sync", lambda e, nm=nm: e.dma_start(
                out=bstg[:], in_=dB[nm].rearrange("a (b c) -> (a b) c", c=128)),
                ("fbld", i), extra=[("fbev", i - 1) if i >= 1 else None])
            E("tensor", lambda e: e.transpose(out=PS[0:128, 2 * 512:2 * 512 + 8], in_=bstg[:], identity=identf[0:8, 0:8]),
              sem="tM", deps=[("fbld", i), "identf", FD, ("x1ev", NT - 1), ("x1ev", NT - 2),
                              ("fbev", i - 1) if i >= 1 else None],
              key=("fbtr", i))
            E("vector", lambda e, i=i: e.tensor_copy(out=bcols[:, i * 8:(i + 1) * 8], in_=PS[0:128, 2 * 512:2 * 512 + 8]),
              sem="vV", deps=[("fbtr", i)], key=("fbev", i))
        qb_dma("sync", lambda e: e.dma_start(out=Vctok[:], in_=dsq), ("fsqld",))
        qb_dma("sync", lambda e: e.dma_start(out=ind2[:], in_=dind2), ("find2",))
        qb_dma("sync", lambda e: e.dma_start(out=dbvb[:], in_=dB["dbv"][0:1, :].broadcast_to([128, KB])), ("fdbvb",))
        E("vector", lambda e: e.tensor_copy(out=h_sb[0:16, :], in_=Vctok[:]),
          sem="vV", deps=[("fsqld",)], key="fsqcast")
        for oc in range(8):
            E("tensor", lambda e, oc=oc: e.transpose(out=psb(3, oc * 16, 16), in_=h_sb[0:16, oc * 128:(oc + 1) * 128],
                                                     identity=ident[0:16, 0:16]),
              sem="tM" if oc == 7 else None, deps=["fsqcast", "ident", FD, ("fbtr", 5),
                                                    ("x1ev", NT - 1), ("x1ev", NT - 2)],
              key="fsqtrb" if oc == 7 else None)
        E("vector", lambda e: e.tensor_copy(out=sqT[:], in_=psb(3, 0, 128)), sem="vV",
          deps=["fsqtrb"], key="fsqTev")
        for oc in range(8):
            E("tensor", lambda e, oc=oc: e.transpose(out=PS[0:128, 4 * 512 + oc * 16:4 * 512 + (oc + 1) * 16],
                                                     in_=Vctok[0:16, oc * 128:(oc + 1) * 128], identity=identf[0:16, 0:16]),
              sem="tM" if oc == 7 else None, deps=[("fsqld",), "identf", FD, "fsqtrb"],
              key="fsqtrf" if oc == 7 else None)
        for oc in range(8):
            E("vector", lambda e, oc=oc: e.tensor_scalar_add(
                out=sqTb[:, oc * 16:(oc + 1) * 16], in0=PS[0:128, 4 * 512 + oc * 16:4 * 512 + (oc + 1) * 16],
                scalar1=bcols[:, 16 + oc:17 + oc]),
              sem="vV", deps=["fsqtrf", ("fbev", 2)], key=("fsqb", oc))

        load_ffnW("dWq", dW["dWq"], 0)
        for oc in range(8):
            for dcn in range(8):
                E("tensor", lambda e, oc=oc, dcn=dcn: e.matmul(
                    out=psf(0, 16), lhsT=ffnW[0][:, dcn * KB + oc * 128:dcn * KB + (oc + 1) * 128],
                    rhs=sqT[:, dcn * 16:(dcn + 1) * 16], start=(dcn == 0), stop=(dcn == 7)),
                  sem="tM" if dcn == 7 else None,
                  deps=[("dWq", "done"), "fsqTev", FD, ("qdtev", oc - 1) if oc >= 1 else "fsqtrf"],
                  key=("qdtmm", oc) if dcn == 7 else None)
            E("scalar", lambda e, oc=oc: e.activation(out=QdT[:, oc * 16:(oc + 1) * 16], in_=psf(0, 16),
                                                      func=AF.Identity, bias=bcols[:, 8 + oc:9 + oc], scale=1.0),
              sem="sS", deps=[("qdtmm", oc), ("fbev", 1)], key=("qdtev", oc))
        E("gpsimd", lambda e: e.memset(QdBD[:], 0.0), sem="gP", deps=[("qdtev", 7)], key="qdbdz")
        for pp in range(8):
            E("vector", lambda e, pp=pp: e.tensor_copy(out=QdBD[0:64, pp * 32:pp * 32 + 16],
                                                       in_=QdT[0:64, pp * 16:(pp + 1) * 16]),
              sem="vV", deps=[("qdtev", pp), "qdbdz"])
            E("vector", lambda e, pp=pp: e.tensor_copy(out=QdBD[64:128, pp * 32 + 16:pp * 32 + 32],
                                                       in_=QdT[64:128, pp * 16:(pp + 1) * 16]),
              sem="vV", deps=[("qdtev", pp), "qdbdz"], key=("qdbd", pp))
        for oc in range(8):
            E("scalar", lambda e, oc=oc: e.activation(out=sqsc[:, 0:16], in_=QdT[:, oc * 16:(oc + 1) * 16],
                                                      func=AF.Square, bias=0.0, scale=1.0),
              sem="sS", deps=[("qdtev", oc), ("qdnm", oc - 1) if oc >= 1 else None], key=("qdsq", oc))
            E("tensor", lambda e: e.matmul(out=psf(2, 16, rows=2), lhsT=bonesb[:], rhs=sqsc[:, 0:16],
                                           start=True, stop=True),
              sem="tM", deps=[("qdsq", oc), "bones", ("qdnr", oc - 1) if oc >= 1 else ("fbtr", 5)],
              key=("qdnm", oc))
            E("vector", lambda e, oc=oc: e.reduce_max(out=nrm[0:2, 48 + oc:49 + oc], in_=psf(2, 16, rows=2), axis=AX),
              sem="vV", deps=[("qdnm", oc)], key=("qdnr", oc))

        load_ffnW("dWk", dW["dWk"], 0, free_ev=("qdtmm", 7))

        def xn1_group(Jg, pref, bankpair, start_dep=None, war=None):
            for tt in range(4):
                ti = Jg * 4 + tt
                xbuf = xt[ti % 2]
                E("sync", lambda e, ti=ti, xbuf=xbuf: e.dma_start(out=xbuf[:], in_=dx1[ti * 128:(ti + 1) * 128, :]),
                  sem="qR%d" % (ti % 2),
                  deps=[*(X1ALL if ti < 2 else []), (pref, "ap", ti - 2) if ti >= 2 else start_dep],
                  key=(pref, "ld", ti))
                E("scalar", lambda e, ti=ti, xbuf=xbuf: e.activation(
                    out=xnsc[:], in_=xbuf[:], func=AF.Identity,
                    bias=st_b[:, ti:ti + 1], scale=st_s[:, ti:ti + 1]),
                  sem="sS", deps=[(pref, "ld", ti), ("l2", "rstd", NT - 1), ("l2bias", NT - 1),
                                  (pref, "tr", ti - 1) if ti >= 1 else start_dep],
                  key=(pref, "ap", ti))
                bank = bankpair[tt % 2]
                for dcn in range(8):
                    E("tensor", lambda e, dcn=dcn, bank=bank: e.transpose(
                        out=psb(bank, dcn * 128, 128), in_=xnsc[:, dcn * 128:(dcn + 1) * 128], identity=ident[:]),
                      sem="tM" if dcn == 7 else None,
                      deps=[(pref, "ap", ti), "ident", (pref, "ev", ti - 2) if ti >= 2 else FD,
                            war(Jg, tt) if (war is not None and tt < 2) else None],
                      key=(pref, "tr", ti) if dcn == 7 else None)
                if pref == "i1":
                    E("scalar", lambda e, bank=bank, tt=tt: e.copy(
                        out=ch(xnTg, 512)[:, :, tt * 128:(tt + 1) * 128],
                        in_=psb(bank, 0, 1024).rearrange("p (c n) -> p c n", n=128)),
                      sem="sS", deps=[(pref, "tr", ti), (pref, "con", Jg - 1) if (Jg >= 1 and tt == 0) else
                                      (start_dep if (Jg == 0 and tt == 0) else None)],
                      key=(pref, "ev", ti))
                else:
                    E("vector", lambda e, bank=bank, tt=tt: e.tensor_copy(
                        out=ch(xnTg, 512)[:, :, tt * 128:(tt + 1) * 128],
                        in_=psb(bank, 0, 1024).rearrange("p (c n) -> p c n", n=128)),
                      sem="vV", deps=[(pref, "tr", ti), (pref, "con", Jg - 1) if (Jg >= 1 and tt == 0) else
                                      (start_dep if (Jg == 0 and tt == 0) else None)],
                      key=(pref, "ev", ti))
            w.mark((pref, "grp", Jg), "sS" if pref == "i1" else "vV")

        for Jg in range(NJ):
            xn1_group(Jg, "f0", (6, 7), start_dep=FD)
            for kc in range(8):
                bank = kc % 2
                for dcn in range(8):
                    E("tensor", lambda e, kc=kc, dcn=dcn, bank=bank: e.matmul(
                        out=psf(bank, 512), lhsT=ffnW[0][:, dcn * KB + kc * 128:dcn * KB + (kc + 1) * 128],
                        rhs=ch(xnTg, 512)[:, dcn, :], start=(dcn == 0), stop=(dcn == 7)),
                      sem="tM" if dcn == 7 else None,
                      deps=[("f0", "grp", Jg), ("dWk", "done"),
                            ("f0ks", Jg, kc - 2) if kc >= 2 else (("f0ks", Jg - 1, 6 + kc) if Jg >= 1 else None)],
                      key=("f0km", Jg, kc) if dcn == 7 else None)
                E("scalar", lambda e, bank=bank: e.activation(out=sqsc[:], in_=psf(bank, 512),
                                                              func=AF.Square, bias=0.0, scale=1.0),
                  sem="sS", deps=[("f0km", Jg, kc), ("f0knm", Jg, kc - 1) if kc >= 1 else
                                  (("f0knm", Jg - 1, 7) if Jg >= 1 else ("qdnm", 7))],
                  key=("f0ks", Jg, kc))
                E("tensor", lambda e: e.matmul(out=psf(2, 512, rows=2), lhsT=bonesb[:], rhs=sqsc[:],
                                               start=True, stop=True),
                  sem="tM", deps=[("f0ks", Jg, kc), "bones",
                                  ("f0knr", Jg, kc - 1) if kc >= 1 else (("f0knr", Jg - 1, 7) if Jg >= 1 else ("qdnr", 7))],
                  key=("f0knm", Jg, kc))
                E("vector", lambda e, kc=kc: e.reduce_max(out=nrm[0:2, 56 + kc:57 + kc], in_=psf(2, 512, rows=2), axis=AX),
                  sem="vV", deps=[("f0knm", Jg, kc), ("f0knu", Jg - 1, kc) if Jg >= 1 else "qnz"],
                  key=("f0knr", Jg, kc))
                E("vector", lambda e, kc=kc: e.tensor_max(out=nrm[0:2, 40 + kc:41 + kc],
                                                          in0=nrm[0:2, 40 + kc:41 + kc], in1=nrm[0:2, 56 + kc:57 + kc]),
                  sem="vV", deps=[("f0knr", Jg, kc), "qnz2"], key=("f0knu", Jg, kc))
            w.mark(("f0", "con", Jg), "tM")

        E("scalar", lambda e: e.activation(out=nrm[0:2, 48:56], in_=nrm[0:2, 48:56], func=AF.Sqrt, bias=0.0, scale=1.0),
          sem="sS", deps=[("qdnr", 7), ("f0ks", NJ - 1, 7)], key="dqsqrt")
        E("scalar", lambda e: e.activation(out=nrm[0:2, 40:48], in_=nrm[0:2, 40:48], func=AF.Sqrt, bias=0.0, scale=1.0),
          sem="sS", deps=[("f0knu", NJ - 1, 7), "dqsqrt"], key="dksqrt")
        qb_dma("gpsimd", lambda e: e.dma_start(out=qnacc[0:1, 96:104], in_=nrm[0:1, 48:56]), ("dnf", 0), extra=["dqsqrt", "dksqrt"])
        qb_dma("gpsimd", lambda e: e.dma_start(out=qnacc[0:1, 104:112], in_=nrm[1:2, 48:56]), ("dnf", 1))
        qb_dma("gpsimd", lambda e: e.dma_start(out=qnacc[0:1, 112:120], in_=nrm[0:1, 40:48]), ("dnf", 2))
        qb_dma("gpsimd", lambda e: e.dma_start(out=qnacc[0:1, 120:128], in_=nrm[1:2, 40:48]), ("dnf", 3))
        for h in range(16):
            qc_ = 96 + (h % 2) * 8 + h // 2
            kc_ = 112 + (h % 2) * 8 + h // 2
            E("vector", lambda e, qc_=qc_, kc_=kc_, h=h: e.tensor_tensor(
                out=qnacc[0:1, 68 + h:69 + h], in0=qnacc[0:1, qc_:qc_ + 1], in1=qnacc[0:1, kc_:kc_ + 1], op=ALU.mult),
              sem="vV", deps=[("dnf", 3)], key=("dmh0", h))
            E("vector", lambda e, h=h: e.tensor_scalar_mul(out=qnacc[0:1, 68 + h:69 + h],
                                                           in0=qnacc[0:1, 68 + h:69 + h], scalar1=-SCALE * 1.02),
              sem="vV", deps=[("dmh0", h)], key=("dmh", h))
        qb_dma("gpsimd", lambda e: e.dma_start(out=mrow2[0:1, :], in_=qnacc[0:1, 68:84].rearrange("a (b c) -> a b c", c=2)[:, :, 0],
                                               allow_slow_non_contiguous=True),
               ("dmr", 0), extra=[("dmh", 15)])
        qb_dma("gpsimd", lambda e: e.dma_start(out=mrow2[1:2, :], in_=qnacc[0:1, 68:84].rearrange("a (b c) -> a b c", c=2)[:, :, 1],
                                               allow_slow_non_contiguous=True),
               ("dmr", 1))
        E("tensor", lambda e: e.matmul(out=psf(4, 8), lhsT=ind2[:], rhs=mrow2[:], start=True, stop=True),
          sem="tM", deps=[("dmr", 1), ("find2",), ("f0", "con", NJ - 1)], key="dbias_mm")
        E("vector", lambda e: e.tensor_copy(out=bias16d_t[:], in_=psf(4, 8)), sem="vV",
          deps=["dbias_mm"], key="dbias")

        load_ffnW("dWv", dW["dWv"], 1)
        E("gpsimd", lambda e: e.memset(VdG[0][:], 1.0), sem="gP", deps=[("ctxevd", NJ - 1)], key=("vdz", 0))
        E("gpsimd", lambda e: e.memset(VdG[1][:], 1.0), sem="gP", key=("vdz", 1))
        E("gpsimd", lambda e: e.memset(l_run[:], 0.0), sem="gP", key="lrz2")
        E("gpsimd", lambda e: e.memset(od_acc[:], 0.0), sem="gP", key="odz2")
        SLOT_BANK = [(4, 0), (5, 0), (4, 64), (5, 64)]
        for Jg in range(NJ):
            xn1_group(Jg, "f1", (2, 3),
                      start_dep=("f0", "ap", NT - 1) if Jg == 0 else None,
                      war=(lambda Jg_, tt: ("pdev", Jg_ - 1, tt % 2) if Jg_ >= 1 else ("f0", "con", NJ - 1)))
            for kc in range(8):
                bank = kc % 2
                for dcn in range(8):
                    E("tensor", lambda e, kc=kc, dcn=dcn, bank=bank: e.matmul(
                        out=psf(bank, 512), lhsT=ffnW[0][:, dcn * KB + kc * 128:dcn * KB + (kc + 1) * 128],
                        rhs=ch(xnTg, 512)[:, dcn, :], start=(dcn == 0), stop=(dcn == 7)),
                      sem="tM" if dcn == 7 else None,
                      deps=[("f1", "grp", Jg), ("dWk", "done"),
                            ("f1ke", Jg, kc - 2) if kc >= 2 else (("vdev", Jg - 1) if Jg >= 1 else ("f0", "con", NJ - 1))],
                      key=("f1km", Jg, kc) if dcn == 7 else None)
                E("scalar", lambda e, kc=kc, bank=bank: e.activation(
                    out=KdTg[:, kc * 512:(kc + 1) * 512], in_=psf(bank, 512),
                    func=AF.Identity, bias=bcols[:, 0 + kc:1 + kc], scale=1.0),
                  sem="sS", deps=[("f1km", Jg, kc), ("fbev", 0),
                                  ("sdmm", Jg - 1, kc) if Jg >= 1 else None],
                  key=("f1ke", Jg, kc))
            for tt in range(4):
                ti = Jg * 4 + tt
                for dcn in range(8):
                    for hf in range(2):
                        E("tensor", lambda e, dcn=dcn, hf=hf, tt=tt: e.matmul(
                            out=psf(hf, 512), lhsT=ch(xnTg, 512)[:, dcn, tt * 128:(tt + 1) * 128],
                            rhs=ffnW[1][:, dcn * KB + hf * 512:dcn * KB + (hf + 1) * 512],
                            start=(dcn == 0), stop=(dcn == 7)),
                          sem="tM" if (dcn == 7 and hf == 1) else None,
                          deps=[("f1", "grp", Jg), ("dWv", "done"), ("f1ke", Jg, 7),
                                ("vde", Jg, tt - 1) if tt >= 1 else None],
                          key=("vdm", Jg, tt) if (dcn == 7 and hf == 1) else None)
                E("vector", lambda e, tt=tt, Jg=Jg: e.tensor_tensor(
                    out=VdG[Jg % 2][:].rearrange("p (t g n) -> p t g n", g=16, n=65)[:, tt, :, 0:64],
                    in0=PS[:, 0:1024].rearrange("p (g n) -> p g n", n=64),
                    in1=dbvb[:].rearrange("p (g n) -> p g n", n=64), op=ALU.add),
                  sem="vV", deps=[("vdm", Jg, tt), ("fdbvb",), ("vdz", Jg % 2),
                                  ("odup", Jg - 2) if Jg >= 2 else None],
                  key=("vde", Jg, tt))
            w.mark(("vdev", Jg), "vV")
            w.mark(("f1", "con", Jg), "tM")
            for pp in range(8):
                bank, b0 = SLOT_BANK[pp % 4]
                E("tensor", lambda e, pp=pp, bank=bank, b0=b0: e.matmul(
                    out=PS[b0:b0 + 32, bank * 512:(bank + 1) * 512],
                    lhsT=QdBD[:, pp * 32:(pp + 1) * 32], rhs=KdTg[:, pp * 512:(pp + 1) * 512],
                    start=True, stop=True, tile_position=(0, b0) if b0 else None),
                  sem="tM", deps=[("f1ke", Jg, pp), ("qdbd", pp),
                                  "dbias" if (Jg == 0 and pp <= 1) else None,
                                  ("sdex", Jg, pp - 2) if pp >= 2 else (("sdex", Jg - 1, pp + 6) if Jg >= 1 else None)],
                  key=("sdmm", Jg, pp))
                E("scalar", lambda e, pp=pp, bank=bank, b0=b0: e.activation(
                    out=PdB[pp % 4][:], in_=PS[b0:b0 + 32, bank * 512:(bank + 1) * 512],
                    func=AF.Exp, bias=bias16d_t[b0:b0 + 32, pp:pp + 1], scale=SCALE),
                  sem="sS", deps=[("sdmm", Jg, pp), "dbias",
                                  ("pdtp", Jg, pp - 4) if pp >= 4 else
                                  (("pdtp", Jg - 1, pp + 4) if Jg >= 1 else None),
                                  ("sdls", Jg, pp - 4) if pp >= 4 else
                                  (("sdls", Jg - 1, pp + 4) if Jg >= 1 else None)],
                  key=("sdex", Jg, pp))
                E("vector", lambda e, pp=pp: e.reduce_sum(out=lpart[:, 0:1], in_=PdB[pp % 4][:], axis=AX),
                  sem="vV", deps=[("sdex", Jg, pp), ("sdlu", Jg, pp - 1) if pp >= 1 else
                                  (("sdlu", Jg - 1, 7) if Jg >= 1 else "lrz2")],
                  key=("sdls", Jg, pp))
                E("vector", lambda e, pp=pp: e.tensor_add(out=l_run[:, pp:pp + 1], in0=l_run[:, pp:pp + 1],
                                                          in1=lpart[:, 0:1]),
                  sem="vV", deps=[("sdls", Jg, pp)], key=("sdlu", Jg, pp))
                for it in range(4):
                    tb = 2 + it // 2
                    E("tensor", lambda e, it=it, pp=pp, tb=tb: e.transpose(
                        out=psb(tb, (it % 2) * 256 + pp * 32, 32),
                        in_=PdB[pp % 4][:, it * 128:(it + 1) * 128], identity=ident[0:32, 0:32]),
                      sem="tM" if it == 3 else None,
                      deps=[("sdex", Jg, pp),
                            ("f1", "ev", Jg * 4 + 2), ("f1", "ev", Jg * 4 + 3),
                            ("pdev", Jg - 1, 0) if (Jg >= 1 and pp == 0) else None,
                            ("pdev", Jg - 1, 1) if (Jg >= 1 and pp == 0) else None],
                      key=("pdtp", Jg, pp) if it == 3 else None)
            for half in range(2):
                E("vector", lambda e, half=half: e.tensor_copy(
                    out=PdTG[:, half * 512:(half + 1) * 512], in_=psb(2 + half, 0, 512)),
                  sem="vV", deps=[("pdtp", Jg, 7)], key=("pdev", Jg, half))
            w.mark(("pdtrd", Jg), "vV")
            for h in range(16):
                obank = 6 + h // 8
                for it in range(4):
                    E("tensor", lambda e, h=h, it=it, Jg=Jg, obank=obank: e.matmul(
                        out=PS[0:16, obank * 512 + (h % 8) * 64:obank * 512 + (h % 8) * 64 + 64],
                        lhsT=PdTG[:, (it // 2) * 512 + (it % 2) * 256 + (h // 2) * 32 + (h % 2) * 16:
                                   (it // 2) * 512 + (it % 2) * 256 + (h // 2) * 32 + (h % 2) * 16 + 16],
                        rhs=VdG[Jg % 2][:].rearrange("p (t g n) -> p t g n", g=16, n=65)[:, it, h, 0:64],
                        start=(it == 0), stop=(it == 3)),
                      sem="tM" if it == 3 else None,
                      deps=[("pdev", Jg, it // 2), ("vde", Jg, it),
                            ("odup", Jg - 1) if (Jg >= 1 and h == 0 and it == 0) else None],
                      key=("pvod", Jg, h) if it == 3 else None)
            E("vector", lambda e: e.tensor_add(out=od_acc[:], in0=od_acc[:], in1=PS[0:16, 6 * 512:8 * 512]),
              sem="vV", deps=[("pvod", Jg, 7), ("pvod", Jg, 15), "odz2",
                              ("odup", Jg - 1) if Jg >= 1 else None],
              key=("odup", Jg))

        E("vector", lambda e: e.reciprocal(out=rl32[:], in_=l_run[:]),
          sem="vV", deps=[("sdlu", NJ - 1, 7)], key="drl")
        qb_dma("gpsimd", lambda e: e.dma_start(out=rlB[:], in_=rl32[16:32, :]), ("drlB",), extra=["drl"])
        for h in range(16):
            pp, s2 = h // 2, h % 2
            E("vector", lambda e, h=h, pp=pp, s2=s2: e.tensor_scalar_mul(
                out=od_acc[:, h * 64:(h + 1) * 64], in0=od_acc[:, h * 64:(h + 1) * 64],
                scalar1=(rl32 if s2 == 0 else rlB)[0:16, pp:pp + 1]),
              sem="vV", deps=[("odup", NJ - 1), "drl", ("drlB",)], key=("odn", h))
        E("vector", lambda e: e.tensor_copy(out=h_sb[0:16, :], in_=od_acc[:]),
          sem="vV", deps=[("odn", 15), "fsqTev", ("qdtev", 7)], key="odcast")
        for oc in range(8):
            E("tensor", lambda e, oc=oc: e.transpose(out=psb(3, oc * 16, 16), in_=h_sb[0:16, oc * 128:(oc + 1) * 128],
                                                     identity=ident[0:16, 0:16]),
              sem="tM" if oc == 7 else None,
              deps=["odcast", "ident", ("pdev", NJ - 1, 0), ("pdev", NJ - 1, 1)],
              key="odtr" if oc == 7 else None)
        E("vector", lambda e: e.tensor_copy(out=odTb[:], in_=psb(3, 0, 128)), sem="vV",
          deps=["odtr"], key="odtev")
        load_ffnW("dWo", dW["dWo"], 0, free_ev=("f1km", NJ - 1, 7))
        for oc in range(8):
            for dcn in range(8):
                E("tensor", lambda e, oc=oc, dcn=dcn: e.matmul(
                    out=psf(0, 16), lhsT=ffnW[0][:, dcn * KB + oc * 128:dcn * KB + (oc + 1) * 128],
                    rhs=odTb[:, dcn * 16:(dcn + 1) * 16], start=(dcn == 0), stop=(dcn == 7)),
                  sem="tM" if dcn == 7 else None,
                  deps=[("dWo", "done"), "odtev", ("odwe", oc - 1) if oc >= 1 else None],
                  key=("odwm", oc) if dcn == 7 else None)
            E("vector", lambda e, oc=oc: e.scalar_tensor_tensor(
                out=oT[:, oc * 16:(oc + 1) * 16], in0=psf(0, 16), scalar=1.0,
                in1=sqTb[:, oc * 16:(oc + 1) * 16], op0=ALU.mult, op1=ALU.add),
              sem="vV", deps=[("odwm", oc), ("fsqb", oc)], key=("odwe", oc))
        for oc in range(8):
            E("tensor", lambda e, oc=oc: e.transpose(out=PS[0:16, 2 * 512 + oc * 128:2 * 512 + (oc + 1) * 128],
                                                     in_=oT[:, oc * 16:(oc + 1) * 16], identity=identf[:]),
              sem="tM" if oc == 7 else None,
              deps=[("odwe", oc), "identf", ("odwm", 7)],
              key="otokr" if oc == 7 else None)
        E("vector", lambda e: e.tensor_copy(out=o_tok[:], in_=PS[0:16, 2 * 512:2 * 512 + 1024]),
          sem="vV", deps=["otokr"], key="otok")
        qb_dma("gpsimd", lambda e: e.dma_start(out=dcc1i.ap(), in_=o_tok[:]), ("cc1w",), extra=["otok"])
        E("gpsimd", lambda e: e.collective_compute(
            "AllGather", ALU.bypass, replica_groups=[list(range(NCORES))],
            ins=[dcc1i.ap().opt()], outs=[dcc1o.ap().opt()]),
          sem="cc", inc=1, deps=[("cc1w",)], key="cc1")
        qb_dma("gpsimd", lambda e: e.dma_start(out=Oall[:], in_=dcc1o.ap()), ("oall",),
               extra=["cc1", ("odwm", 7)])

        if upto == "F":
            for ti in range(NT):
                E("sync", lambda e, ti=ti: e.dma_start(out=xt[ti % 2][:], in_=dx1[ti * 128:(ti + 1) * 128, :]),
                  sem="qF%d" % (ti % 2),
                  deps=[("x1st", NT - 2), ("x1st", NT - 1), ("f1", "ap", NT - 1),
                        ("fdbg", ti - 2) if ti >= 2 else None], key=("fdbgl", ti))
                E("gpsimd", lambda e, ti=ti: e.dma_start(out=dout[ti * 128:(ti + 1) * 128, :], in_=xt[ti % 2][:]),
                  sem="qO" if ti % 2 == 0 else "qO2", deps=[("fdbgl", ti)], key=("fdbg", ti))
            w.barrier_wait("gpsimd", [("fdbg", NT - 2), ("fdbg", NT - 1), ("oall",)])
            w.barrier_wait("gpsimd", [("tap", nm) for nm in dtaps])
            return

        for rep in range(8):
            qb_dma("sync", lambda e, rep=rep: e.dma_start(out=Gt[16 * rep:16 * rep + 16, :], in_=dlng),
                   ("gld", rep), extra=PP_DONE if rep == 0 else [])
        for rep in range(8):
            qb_dma("sync", lambda e, rep=rep: e.dma_start(out=Bt[16 * rep:16 * rep + 16, :], in_=dlnb),
                   ("bld", rep), extra=[("odwm", 7)] if rep == 0 else [])
        ln_stats(100, Oall, "lh", lnst[:, 8:9], lnst[:, 9:10], extra0=[("oall",), ("l2", "rstd", NT - 1)])
        E("scalar", lambda e: e.activation(out=sqf32[:], in_=Oall[:], func=AF.Identity,
                                           bias=lnst[:, 9:10], scale=lnst[:, 8:9]),
          sem="sS", deps=[("lhbias", 100)], key="hap")
        E("vector", lambda e: e.tensor_mul(out=sqf32[:], in0=sqf32[:], in1=Gt[:]),
          sem="vV", deps=["hap", ("gld", 7)], key="hg")
        E("vector", lambda e: e.tensor_tensor(out=h_sb[:], in0=sqf32[:], in1=Bt[:], op=ALU.add),
          sem="vV", deps=["hg", ("bld", 7), "odcast"], key="hb")
        qb_dma("sync", lambda e: e.dma_start(out=esel_sb[:], in_=desel), ("eselld",))
        E("vector", lambda e: e.tensor_copy(out=esel_bf[:], in_=esel_sb[:]),
          sem="vV", deps=[("eselld",)], key="eselbf")
        for half in range(2):
            E("tensor", lambda e, half=half: e.matmul(
                out=PS[0:16, 5 * 512 + half * 512:5 * 512 + (half + 1) * 512],
                lhsT=esel_bf[:], rhs=h_sb[:, half * 512:(half + 1) * 512], start=True, stop=True),
              sem="tM" if half == 1 else None,
              deps=["hb", "eselbf", "otokr", ("f0", "tr", NT - 1)],
              key="hselmm" if half == 1 else None)
        E("vector", lambda e: e.tensor_copy(out=hsel[:], in_=PS[0:16, 5 * 512:5 * 512 + 1024]),
          sem="vV", deps=["hselmm"], key="hselev")
        for dcn in range(8):
            E("tensor", lambda e, dcn=dcn: e.transpose(out=psb(6, dcn * 16, 16),
                                                       in_=hsel[0:16, dcn * 128:(dcn + 1) * 128],
                                                       identity=ident[0:16, 0:16]),
              sem="tM" if dcn == 7 else None,
              deps=["hselev", "ident", "hselmm"],
              key="htr" if dcn == 7 else None)
        E("vector", lambda e: e.tensor_copy(out=hTe[:], in_=psb(6, 0, 128)), sem="vV",
          deps=["htr"], key="hTev")
        for i, (dsrc, dst, wdt) in enumerate(((deb1, b1T, eb1c), (debg, bgT, ebgc), (deb2, b2T, eb2c))):
            nr = 32 if i < 2 else 8
            for le in range(2):
                qb_dma("sync", lambda e, dsrc=dsrc, le=le, nr=nr, wdt=wdt: e.dma_start(
                    out=wdt[0:nr, le * 128:(le + 1) * 128],
                    in_=dsrc[le].rearrange("(a b) -> a b", b=128)),
                    ("ebld", i, le))
                E("tensor", lambda e, le=le, nr=nr, wdt=wdt: e.transpose(
                    out=PS[0:128, 3 * 512 + le * 32:3 * 512 + le * 32 + 32][:, 0:nr],
                    in_=wdt[0:nr, le * 128:(le + 1) * 128],
                    identity=(identf[0:32, 0:32] if nr == 32 else identf[0:8, 0:8])),
                  sem="tM", deps=[("ebld", i, le), "identf",
                                  "htr" if (i == 0 and le == 0) else (("ebev", i - 1, 1) if (i >= 1 and le == 0) else None),
                                  ("ebev", i, le - 1) if le >= 1 else None],
                  key=("ebtr", i, le))
                E("vector", lambda e, i=i, le=le, nr=nr, dst=dst: e.tensor_copy(
                    out=dst[:, le * nr:(le + 1) * nr],
                    in_=PS[0:128, 3 * 512 + le * 32:3 * 512 + le * 32 + nr]),
                  sem="vV", deps=[("ebtr", i, le)], key=("ebev", i, le))
        for le in range(2):
            for q4 in range(4):
                for mat, (dsrc, wbuf, bT, psbank, osb) in enumerate(
                        ((deW1, W1bfh, b1T, (0 if q4 % 2 == 0 else 3), u_sb),
                         (deWg, Wgbfh, bgT, (1 if q4 % 2 == 0 else 7), g_sb))):
                    tagq = ("ew", le, q4, mat)
                    if q4 >= 1:
                        lddeps = [("ew", le, q4 - 1, mat, "mmz")]
                    elif le >= 1:
                        lddeps = [("w2", 0, 2 + mat, "mmz"), ("ew", 0, 3, mat, "mmz")]
                    elif mat == 0:
                        lddeps = [("odwm", 7), ("Jwo", NJ - 1)] + PP_DONE
                    else:
                        lddeps = [("f1", "ap", NT - 1), ("x1st", NT - 1), ("x1st", NT - 2),
                                  ("vdm", NJ - 1, 3)] + PP_DONE
                    E("sync", lambda e, g=(mat * 2 + le) * 4 + q4, wbuf=wbuf: e.dma_start(
                        out=wbuf[:].rearrange("p (a n) -> p a n", n=KB), in_=dewb1[g]),
                      sem="qW%d" % mat, deps=lddeps, key=(*tagq, "cast", 7))
                    for fc8 in range(8):
                        fc = q4 * 8 + fc8
                        for dcn in range(8):
                            E("tensor", lambda e, fc8=fc8, fc=fc, dcn=dcn, wbuf=wbuf, psbank=psbank, le=le: e.matmul(
                                out=PS[:, psbank * 512 + fc * 8:psbank * 512 + fc * 8 + 8],
                                lhsT=wbuf[:, dcn * KB + fc8 * 128:dcn * KB + (fc8 + 1) * 128],
                                rhs=hTe[:, dcn * 16 + le * 8:dcn * 16 + le * 8 + 8],
                                start=(dcn == 0), stop=(dcn == 7)),
                              sem="tM" if dcn == 7 else None,
                              deps=[(*tagq, "cast", 7), "hTev",
                                    ("exsil", le, q4 - 2, mat) if (q4 >= 2 and fc8 == 0) else None,
                                    (("exsil", le - 1, 2 + q4, mat) if (le >= 1 and q4 < 2 and fc8 == 0) else None),
                                    ("ebev", 2, 1) if (le == 0 and q4 < 2 and fc8 == 0) else None],
                              key=(*tagq, "mm", fc8) if dcn == 7 else None)
                    w.mark((*tagq, "mmz"), "tM")
                    for fc8 in range(8):
                        fc = q4 * 8 + fc8
                        E("scalar", lambda e, fc=fc, fc8=fc8, psbank=psbank, bT=bT, le=le, osb=osb: e.activation(
                            out=osb[:, fc * 8:fc * 8 + 8], in_=PS[:, psbank * 512 + fc * 8:psbank * 512 + fc * 8 + 8],
                            func=AF.Silu, bias=bT[:, le * 32 + fc:le * 32 + fc + 1], scale=1.0),
                          sem="sS", deps=[(*tagq, "mm", 7), ("ebev", mat, 1)],
                          key=("exsil", le, q4, mat) if fc8 == 7 else None)
            for q4 in range(4):
                E("vector", lambda e, q4=q4: e.tensor_mul(out=ug[:, q4 * 64:(q4 + 1) * 64],
                                                          in0=u_sb[:, q4 * 64:(q4 + 1) * 64],
                                                          in1=g_sb[:, q4 * 64:(q4 + 1) * 64]),
                  sem="vV", deps=[("exsil", le, q4, 0), ("exsil", le, q4, 1)], key=("ugm", le, q4))
            for wv2 in range(4):
                tagw = ("w2", le, wv2)
                gwv = le * 4 + wv2
                wbuf2 = W2bfw if gwv % 2 == 0 else ffnW[0]
                if gwv >= 2:
                    ple, pwv2 = divmod(gwv - 2, 4)
                    lddeps = [("w2", ple, pwv2, "mmz")]
                    if gwv in (4, 5):
                        lddeps.append(("ew", 1, 3, gwv - 4, "mmz"))
                elif gwv == 0:
                    lddeps = [("vdm", NJ - 1, 3), ("ew", 0, 3, 0, "mmz")]
                else:
                    lddeps = [("odwm", 7), "hap", "hg", "hb", ("ew", 0, 3, 1, "mmz")]
                E("sync", lambda e, g2=gwv, wbuf2=wbuf2: e.dma_start(
                    out=wbuf2[:].rearrange("p (a n) -> p a n", n=KB), in_=dewb2[g2]),
                  sem="qW%d" % (gwv % 2), deps=lddeps, key=(*tagw, "cast", 7))
                yb = 2 if wv2 % 2 == 0 else 5
                for dq8 in range(8):
                    for fcw in range(8):
                        E("tensor", lambda e, dq8=dq8, fcw=fcw, wv2=wv2, le=le, yb=yb, wbuf2=wbuf2: e.matmul(
                            out=PS[:, yb * 512 + dq8 * 8:yb * 512 + dq8 * 8 + 8],
                            lhsT=wbuf2[:, fcw * KB + dq8 * 128:fcw * KB + (dq8 + 1) * 128],
                            rhs=ug[:, (wv2 * 8 + fcw) * 8:(wv2 * 8 + fcw) * 8 + 8],
                            start=(fcw == 0), stop=(fcw == 7)),
                          sem="tM" if fcw == 7 else None,
                          deps=[(*tagw, "cast", 7), ("ugm", le, wv2) if dq8 == 0 else None,
                                ("ugm", le, 3) if dq8 == 0 else None,
                                ("yac", le, wv2 - 2) if (wv2 >= 2 and dq8 == 0) else None,
                                (("yac", le - 1, 2 + wv2) if (le >= 1 and wv2 < 2 and dq8 == 0) else None),
                                (("ytr", le - 1) if (le >= 1 and wv2 == 0 and dq8 == 0) else None),
                                ("selmm" if False else None)],
                          key=(*tagw, "mm", dq8) if fcw == 7 else None)
                w.mark((*tagw, "mmz"), "tM")
                if wv2 == 0:
                    E("vector", lambda e, yb=yb: e.tensor_copy(out=y_acc[:], in_=PS[:, yb * 512:yb * 512 + 64]),
                      sem="vV", deps=[(*tagw, "mm", 7)], key=("yac", le, 0))
                else:
                    E("vector", lambda e, yb=yb: e.tensor_add(out=y_acc[:], in0=y_acc[:], in1=PS[:, yb * 512:yb * 512 + 64]),
                      sem="vV", deps=[(*tagw, "mm", 7), ("yac", le, wv2 - 1)], key=("yac", le, wv2))
            for oc in range(8):
                E("vector", lambda e, oc=oc, le=le: e.tensor_scalar_add(
                    out=y_acc[:, oc * 8:(oc + 1) * 8], in0=y_acc[:, oc * 8:(oc + 1) * 8],
                    scalar1=b2T[:, le * 8 + oc:le * 8 + oc + 1]),
                  sem="vV", deps=[("yac", le, 3), ("ebev", 2, 1)], key=("yb2", le, oc))
            for oc in range(8):
                E("tensor", lambda e, oc=oc, le=le: e.transpose(
                    out=PS[0:8, 4 * 512 + oc * 128:4 * 512 + (oc + 1) * 128],
                    in_=y_acc[:, oc * 8:(oc + 1) * 8], identity=identf[:]),
                  sem="tM" if oc == 7 else None,
                  deps=[("yb2", le, oc), "identf", ("w2", le, 3, "mmz"),
                        ("ytev", le - 1) if (le >= 1 and oc == 0) else None],
                  key=("ytr", le) if oc == 7 else None)
            E("vector", lambda e, le=le: e.tensor_copy(out=Yle[le][0:8, :],
                                                       in_=PS[0:8, 4 * 512:4 * 512 + 1024]),
              sem="vV", deps=[("ytr", le)], key=("ytev", le))
        qb_dma("gpsimd", lambda e: e.dma_start(out=dcc2i.ap()[0:8, :], in_=Yle[0][0:8, :]), ("cc2w0",),
               extra=[("ytev", 0), ("ytev", 1)])
        qb_dma("gpsimd", lambda e: e.dma_start(out=dcc2i.ap()[8:16, :], in_=Yle[1][0:8, :]), ("cc2w",))
        E("gpsimd", lambda e: e.collective_compute(
            "AllGather", ALU.bypass, replica_groups=[list(range(NCORES))],
            ins=[dcc2i.ap().opt()], outs=[dcc2o.ap().opt()]),
          sem="cc", inc=1, deps=[("cc2w",)], key="cc2")
        qb_dma("gpsimd", lambda e: e.dma_start(out=Yall[:], in_=dcc2o.ap()), ("yall",), extra=["cc2"])
        qb_dma("sync", lambda e: e.dma_start(out=bsel_sb[:], in_=dbsel), ("bselld",))
        qb_dma("sync", lambda e: e.dma_start(out=cbob[:], in_=dB["cbo"][0:1, :].broadcast_to([128, KB])),
               ("cbobld",), extra=[("sdex", NJ - 1, 7)])
        for half in range(2):
            E("tensor", lambda e, half=half: e.matmul(
                out=PS[0:16, 4 * 512 + half * 512:4 * 512 + (half + 1) * 512],
                lhsT=bsel_sb[:], rhs=Yall[:, half * 512:(half + 1) * 512], start=True, stop=True),
              sem="tM" if half == 1 else None,
              deps=[("yall",), ("bselld",), ("ytr", 1), ("ytev", 1)],
              key="selmm" if half == 1 else None)
        E("vector", lambda e: e.tensor_add(out=o2_tok[:], in0=o_tok[:], in1=PS[0:16, 4 * 512:4 * 512 + 1024]),
          sem="vV", deps=["selmm", "otok"], key="o2t")

        if upto == "H":
            tap("o2tok", lambda: o2_tok[:], ["o2t"])
            for ti in range(NT):
                E("sync", lambda e, ti=ti: e.dma_start(out=xt[ti % 2][:], in_=dx1[ti * 128:(ti + 1) * 128, :]),
                  sem="qF%d" % (ti % 2),
                  deps=[("x1st", NT - 2), ("x1st", NT - 1), ("ew", 1, 3, 1, "mmz"), "o2t",
                        ("fdbg", ti - 2) if ti >= 2 else None], key=("fdbgl", ti))
                E("gpsimd", lambda e, ti=ti: e.dma_start(out=dout[ti * 128:(ti + 1) * 128, :], in_=xt[ti % 2][:]),
                  sem="qO" if ti % 2 == 0 else "qO2", deps=[("fdbgl", ti)], key=("fdbg", ti))
            w.barrier_wait("gpsimd", [("fdbg", NT - 2), ("fdbg", NT - 1), ("oall",), ("yall",)])
            w.barrier_wait("gpsimd", [("tap", nm) for nm in dtaps])
            return

        for oc in range(8):
            E("tensor", lambda e, oc=oc: e.transpose(out=PS[0:128, 5 * 512 + oc * 16:5 * 512 + (oc + 1) * 16],
                                                     in_=o2_tok[0:16, oc * 128:(oc + 1) * 128],
                                                     identity=identf[0:16, 0:16]),
              sem="tM" if oc == 7 else None, deps=["o2t", "identf", "selmm"],
              key="o2tr" if oc == 7 else None)
        E("vector", lambda e: e.tensor_copy(out=o2T[:], in_=PS[0:128, 5 * 512:5 * 512 + 128]),
          sem="vV", deps=["o2tr"], key="o2Tev")
        E("vector", lambda e: e.tensor_copy(out=o2Tb[:], in_=o2T[:]), sem="vV",
          deps=["o2Tev"], key="o2Tb")
        load_ffnW("cWk", dW["cWk"], 1, free_ev=("w2", 1, 3, "mmz"))
        for oc in range(8):
            kb_ = oc % 2
            for dcn in range(8):
                E("tensor", lambda e, oc=oc, dcn=dcn, kb_=kb_: e.matmul(
                    out=psf(kb_, 16), lhsT=ffnW[1][:, dcn * KB + oc * 128:dcn * KB + (oc + 1) * 128],
                    rhs=o2Tb[:, dcn * 16:(dcn + 1) * 16], start=(dcn == 0), stop=(dcn == 7)),
                  sem="tM" if dcn == 7 else None,
                  deps=[("cWk", "done"), "o2Tb", ("kcev", oc - 2) if oc >= 2 else None],
                  key=("kcmm", oc) if dcn == 7 else None)
            E("scalar", lambda e, oc=oc, kb_=kb_: e.activation(out=KcT[:, oc * 16:(oc + 1) * 16], in_=psf(kb_, 16),
                                                      func=AF.Identity, bias=bcols[:, 32 + oc:33 + oc], scale=1.0),
              sem="sS", deps=[("kcmm", oc), ("fbev", 4)], key=("kcev", oc))
        load_ffnW("cWv", dW["cWv"], 1, free_ev=("kcmm", 7))
        for oc in range(8):
            kb_ = oc % 2
            for dcn in range(8):
                E("tensor", lambda e, oc=oc, dcn=dcn, kb_=kb_: e.matmul(
                    out=psf(kb_, 16), lhsT=ffnW[1][:, dcn * KB + oc * 128:dcn * KB + (oc + 1) * 128],
                    rhs=o2Tb[:, dcn * 16:(dcn + 1) * 16], start=(dcn == 0), stop=(dcn == 7)),
                  sem="tM" if dcn == 7 else None,
                  deps=[("cWv", "done"), "o2Tb", ("vcev", oc - 2) if oc >= 2 else ("kcev", 7)],
                  key=("vcmm", oc) if dcn == 7 else None)
            E("vector", lambda e, oc=oc, kb_=kb_: e.tensor_scalar_add(out=VcT[:, oc * 16:(oc + 1) * 16], in0=psf(kb_, 16),
                                                             scalar1=bcols[:, 40 + oc:41 + oc]),
              sem="vV", deps=[("vcmm", oc), ("fbev", 5)], key=("vcev", oc))
        for oc in range(8):
            E("tensor", lambda e, oc=oc: e.transpose(out=PS[0:16, 5 * 512 + oc * 128:5 * 512 + (oc + 1) * 128],
                                                     in_=VcT[:, oc * 16:(oc + 1) * 16], identity=identf[:]),
              sem="tM" if oc == 7 else None, deps=[("vcev", oc), "identf", ("vcmm", 7), "o2tr"],
              key="vctr" if oc == 7 else None)
        E("vector", lambda e: e.tensor_copy(out=Vctok[:], in_=PS[0:16, 5 * 512:5 * 512 + 1024]),
          sem="vV", deps=["vctr", "fsqcast"], key="vctok")
        E("gpsimd", lambda e: e.memset(VcOnes[:], 1.0), sem="gP", deps=[("cc1w",)], key="vcoz")
        for h in range(16):
            E("vector", lambda e, h=h: e.tensor_copy(out=VcOnes[0:16, h * 65:h * 65 + 64],
                                                     in_=Vctok[0:16, h * 64:(h + 1) * 64]),
              sem="vV", deps=["vctok", "vcoz"], key=("vco", h))
        load_ffnW("cWq", dW["cWq"], 0, free_ev="o2t")
        load_ffnW("cWo", dW["cWo"], 1, free_ev=("vcmm", 7))
        if upto == "I0":
            for ti in range(NT):
                E("sync", lambda e, ti=ti: e.dma_start(out=xt[ti % 2][:], in_=dx1[ti * 128:(ti + 1) * 128, :]),
                  sem="qF%d" % (ti % 2),
                  deps=[("x1st", NT - 2), ("x1st", NT - 1), ("vco", 15), ("cWo", "done"), ("cWq", "done"),
                        ("fdbg", ti - 2) if ti >= 2 else None], key=("fdbgl", ti))
                E("gpsimd", lambda e, ti=ti: e.dma_start(out=dout[ti * 128:(ti + 1) * 128, :], in_=xt[ti % 2][:]),
                  sem="qO" if ti % 2 == 0 else "qO2", deps=[("fdbgl", ti)], key=("fdbg", ti))
            w.barrier_wait("gpsimd", [("fdbg", NT - 2), ("fdbg", NT - 1), ("oall",), ("yall",)])
            w.barrier_wait("gpsimd", [("tap", nm) for nm in dtaps])
            return
        E("gpsimd", lambda e: e.memset(nrm[0:2, 16:32], 0.0), sem="gP", deps=[("f0knu", NJ - 1, 7), "dksqrt"], key="kcnz")
        for c2 in range(8):
            E("scalar", lambda e, c2=c2: e.activation(out=sqsc[:, 0:16], in_=KcT[:, c2 * 16:(c2 + 1) * 16],
                                                      func=AF.Square, bias=0.0, scale=1.0),
              sem="sS", deps=[("kcev", c2), ("kcnm", c2 - 1) if c2 >= 1 else None], key=("kcsq", c2))
            E("tensor", lambda e: e.matmul(out=psf(6, 16, rows=2), lhsT=bonesb[:], rhs=sqsc[:, 0:16],
                                           start=True, stop=True),
              sem="tM", deps=[("kcsq", c2), "bones", ("kcnr", c2 - 1) if c2 >= 1 else "vctok"],
              key=("kcnm", c2))
            E("vector", lambda e, c2=c2: e.reduce_max(out=nrm[0:2, 16 + c2:17 + c2], in_=psf(6, 16, rows=2), axis=AX),
              sem="vV", deps=[("kcnm", c2), "kcnz"], key=("kcnr", c2))
        E("scalar", lambda e: e.activation(out=nrm[0:2, 16:24], in_=nrm[0:2, 16:24], func=AF.Sqrt, bias=0.0, scale=1.0),
          sem="sS", deps=[("kcnr", 7)], key="kcsqrt")
        qb_dma("gpsimd", lambda e: e.dma_start(out=qnacc[0:1, 96:104], in_=nrm[0:1, 16:24]), ("kcf", 0), extra=["kcsqrt"])
        qb_dma("gpsimd", lambda e: e.dma_start(out=qnacc[0:1, 104:112], in_=nrm[1:2, 16:24]), ("kcf", 1))
        E("gpsimd", lambda e: e.memset(KcBD[:], 0.0), sem="gP", deps=[("kcev", 7)], key="kcbdz")
        for c2 in range(8):
            E("vector", lambda e, c2=c2: e.tensor_copy(out=KcBD[0:64, c2 * 80:c2 * 80 + 16],
                                                       in_=KcT[0:64, c2 * 16:(c2 + 1) * 16]),
              sem="vV", deps=[("kcev", c2), "kcbdz"])
            E("vector", lambda e, c2=c2: e.tensor_copy(out=KcBD[64:128, c2 * 80 + 64:c2 * 80 + 80],
                                                       in_=KcT[64:128, c2 * 16:(c2 + 1) * 16]),
              sem="vV", deps=[("kcev", c2), "kcbdz"], key=("kcbd", c2))
        for Jg in range(NJ):
            xn1_group(Jg, "i1", (0, 1),
                      start_dep=("ew", 1, 3, 1, "mmz") if Jg == 0 else None,
                      war=(lambda Jg_, tt: ("qcev", Jg_ - 1, 6 + tt % 2) if Jg_ >= 1 else "hTev"))
            for oc in range(8):
                bank = oc % 2
                for dcn in range(8):
                    E("tensor", lambda e, oc=oc, dcn=dcn, bank=bank: e.matmul(
                        out=psf(bank, 512), lhsT=ffnW[0][:, dcn * KB + oc * 128:dcn * KB + (oc + 1) * 128],
                        rhs=ch(xnTg, 512)[:, dcn, :], start=(dcn == 0), stop=(dcn == 7)),
                      sem="tM" if dcn == 7 else None,
                      deps=[("i1", "grp", Jg), ("cWq", "done"),
                            ("qcev", Jg, oc - 2) if oc >= 2 else
                            (("i1att", Jg - 1) if Jg >= 1 else ("vcev", 7))],
                      key=("qcmm", Jg, oc) if dcn == 7 else None)
                E("scalar", lambda e, oc=oc, bank=bank: e.activation(
                    out=QcT[:, oc * 512:(oc + 1) * 512], in_=psf(bank, 512),
                    func=AF.Identity, bias=bcols[:, 24 + oc:25 + oc], scale=1.0),
                  sem="sS", deps=[("qcmm", Jg, oc), ("fbev", 3),
                                  ("qcnm", Jg, oc - 1) if oc >= 1 else (("scmm", Jg - 1, 7) if Jg >= 1 else None)],
                  key=("qcev", Jg, oc))
                E("scalar", lambda e, oc=oc: e.activation(out=sqsc[:], in_=QcT[:, oc * 512:(oc + 1) * 512],
                                                          func=AF.Square, bias=0.0, scale=1.0),
                  sem="sS", deps=[("qcev", Jg, oc)], key=("qcsq", Jg, oc))
                E("tensor", lambda e: e.matmul(out=psf(7, 512, rows=2), lhsT=bonesb[:], rhs=sqsc[:],
                                               start=True, stop=True),
                  sem="tM", deps=[("qcsq", Jg, oc), "bones",
                                  ("qcnr", Jg, oc - 1) if oc >= 1 else
                                  (("cexp", Jg - 1, 7) if Jg >= 1 else ("kcnm", 7))],
                  key=("qcnm", Jg, oc))
                E("vector", lambda e, oc=oc: e.reduce_max(out=nrm[0:2, 24 + oc:25 + oc], in_=psf(7, 512, rows=2), axis=AX),
                  sem="vV", deps=[("qcnm", Jg, oc)], key=("qcnr", Jg, oc))
            w.mark(("i1", "con", Jg), "tM")
            E("scalar", lambda e: e.activation(out=nrm[0:2, 24:32], in_=nrm[0:2, 24:32], func=AF.Sqrt, bias=0.0, scale=1.0),
              sem="sS", deps=[("qcnr", Jg, 7)], key=("qcsqrt", Jg))
            qb_dma("gpsimd", lambda e: e.dma_start(out=qnacc[0:1, 112:120], in_=nrm[0:1, 24:32]),
                   ("qcf", Jg, 0), extra=[("qcsqrt", Jg)])
            qb_dma("gpsimd", lambda e: e.dma_start(out=qnacc[0:1, 120:128], in_=nrm[1:2, 24:32]), ("qcf", Jg, 1))
            for h in range(16):
                qc_ = 112 + (h % 2) * 8 + h // 2
                kc_ = 96 + (h % 2) * 8 + h // 2
                E("vector", lambda e, qc_=qc_, kc_=kc_, h=h: e.tensor_tensor(
                    out=qnacc[0:1, 68 + h:69 + h], in0=qnacc[0:1, qc_:qc_ + 1], in1=qnacc[0:1, kc_:kc_ + 1], op=ALU.mult),
                  sem="vV", deps=[("qcf", Jg, 1), ("kcf", 1), ("cbias", Jg - 1) if Jg >= 1 else None],
                  key=("cmh0", Jg, h))
                E("vector", lambda e, h=h: e.tensor_scalar_mul(out=qnacc[0:1, 68 + h:69 + h],
                                                               in0=qnacc[0:1, 68 + h:69 + h], scalar1=-SCALE * 1.02),
                  sem="vV", deps=[("cmh0", Jg, h)], key=("cmh", Jg, h))
            qb_dma("gpsimd", lambda e: e.dma_start(out=mrow2[0:1, :],
                                                   in_=qnacc[0:1, 68:84].rearrange("a (b c) -> a b c", c=2)[:, :, 0],
                                                   allow_slow_non_contiguous=True),
                   ("cmr", Jg, 0), extra=[("cmh", Jg, 15)])
            qb_dma("gpsimd", lambda e: e.dma_start(out=mrow2[1:2, :],
                                                   in_=qnacc[0:1, 68:84].rearrange("a (b c) -> a b c", c=2)[:, :, 1],
                                                   allow_slow_non_contiguous=True),
                   ("cmr", Jg, 1))
            E("tensor", lambda e: e.matmul(out=psf(7, 8), lhsT=ind2[:], rhs=mrow2[:], start=True, stop=True),
              sem="tM", deps=[("cmr", Jg, 1), ("find2",), ("qcnm", Jg, 7)], key=("cbmm", Jg))
            E("vector", lambda e: e.tensor_copy(out=bias16c[:, 0:8], in_=psf(7, 8)), sem="vV",
              deps=[("cbmm", Jg)], key=("cbias", Jg))
            for pc2 in range(8):
                sb_ = 6 if pc2 % 2 == 0 else 7
                ob = 2 if pc2 % 2 == 0 else 4
                E("tensor", lambda e, pc2=pc2, sb_=sb_: e.matmul(
                    out=psf(sb_, 512, rows=80), lhsT=KcBD[:, pc2 * 80:(pc2 + 1) * 80],
                    rhs=QcT[:, pc2 * 512:(pc2 + 1) * 512], start=True, stop=True),
                  sem="tM", deps=[("qcev", Jg, pc2), ("kcbd", pc2), ("cbmm", Jg),
                                  ("cexp", Jg, pc2 - 2) if pc2 >= 2 else
                                  (("cexp", Jg - 1, pc2 + 6) if Jg >= 1 else None)],
                  key=("scmm", Jg, pc2))
                for s2 in range(2):
                    slot = (2 * pc2 + s2) % 4
                    E("scalar", lambda e, s2=s2, sb_=sb_, slot=slot, pc2=pc2: e.activation(
                        out=PcB[slot][:], in_=PS[64 * s2:64 * s2 + 16, sb_ * 512:(sb_ + 1) * 512],
                        func=AF.Exp, bias=bias16c[64 * s2:64 * s2 + 16, pc2:pc2 + 1], scale=SCALE),
                      sem="sS", deps=[("scmm", Jg, pc2), ("cbias", Jg),
                                      ("cpv", Jg, pc2 - 2) if pc2 >= 2 else
                                      (("cpv", Jg - 1, pc2 + 6) if Jg >= 1 else None)],
                      key=("cexp", Jg, pc2) if s2 == 1 else None)
                for s2 in range(2):
                    h = 2 * pc2 + s2
                    slot = (2 * pc2 + s2) % 4
                    E("tensor", lambda e, s2=s2, h=h, slot=slot, ob=ob: e.matmul(
                        out=psf(ob + s2, 512, rows=65), lhsT=VcOnes[0:16, h * 65:h * 65 + 65],
                        rhs=PcB[slot][:], start=True, stop=True),
                      sem="tM" if s2 == 1 else None,
                      deps=[("cexp", Jg, pc2), ("vco", 15),
                            ("cctxp", Jg, pc2 - 2) if pc2 >= 2 else
                            (("cctxp", Jg - 1, 7) if Jg >= 1 else None),
                            ("outev", Jg * 4 - 1) if (pc2 < 2 and Jg >= 1) else None],
                      key=("cpv", Jg, pc2) if s2 == 1 else None)
                lrow = lstage if pc2 % 2 == 0 else sqf32
                if pc2 >= 2:
                    clwar = ("crl", Jg, pc2 - 2, 1)
                elif Jg >= 1:
                    clwar = ("crl", Jg - 1, pc2 + 6, 1)
                else:
                    clwar = ("drl", NJ - 1, 7, 1) if pc2 == 0 else ("lhbias", 100)
                E("vector", lambda e, ob=ob, lrow=lrow: e.reciprocal(
                    out=lrow[64:65, 0:1024], in_=PS[64:65, ob * 512:(ob + 2) * 512]),
                  sem="vV", deps=[("cpv", Jg, pc2), clwar], key=("clrec", Jg, pc2))
                for s2 in range(2):
                    E("tensor", lambda e, s2=s2, ob=ob, lrow=lrow: e.matmul(
                        out=PS[64:128, (ob + s2) * 512:(ob + s2 + 1) * 512],
                        lhsT=ones64[64:65, 0:64],
                        rhs=lrow[64:65, s2 * 512:(s2 + 1) * 512],
                        start=True, stop=True, tile_position=(64, 64)),
                      sem="tM", deps=[("clrec", Jg, pc2), "ones64"],
                      key=("crl", Jg, pc2, s2))
                for s2 in range(2):
                    hc = (2 * pc2 + s2) % 4
                    E("scalar", lambda e, ob=ob, s2=s2, hc=hc: e.copy(
                        out=rlb[0:64, hc * 512:(hc + 1) * 512],
                        in_=PS[64:128, (ob + s2) * 512:(ob + s2 + 1) * 512]),
                      sem="sS", deps=[("crl", Jg, pc2, s2)], key=("crle", Jg, pc2, s2))
                cT = ctxcT[Jg % 2]
                for s2 in range(2):
                    h = 2 * pc2 + s2
                    hc = (2 * pc2 + s2) % 4
                    E("vector", lambda e, h=h, ob=ob, s2=s2, hc=hc, cT=cT: e.tensor_tensor(
                        out=ch(cT, 512)[64 * (h % 2):64 * (h % 2) + 64, h // 2, :],
                        in0=PS[0:64, (ob + s2) * 512:(ob + s2 + 1) * 512],
                        in1=rlb[:, hc * 512:(hc + 1) * 512], op=ALU.mult),
                      sem="vV",
                      deps=[("crle", Jg, pc2, s2),
                            ("a2mm", Jg * 4 - 1) if (Jg >= 1 and pc2 == 0 and s2 == 0) else None],
                      key=("cctxp", Jg, pc2) if s2 == 1 else None)
            w.mark(("cctxd", Jg), "vV")
            cT = ctxcT[Jg % 2]
            for tt in range(4):
                ti = Jg * 4 + tt
                ab0 = 2 if tt % 2 == 0 else 4
                for dcn in range(8):
                    for hf in range(2):
                        E("tensor", lambda e, dcn=dcn, hf=hf, tt=tt, ab0=ab0, cT=cT: e.matmul(
                            out=psf(ab0 + hf, 512),
                            lhsT=ch(cT, 512)[:, dcn, tt * 128:(tt + 1) * 128],
                            rhs=ffnW[1][:, dcn * KB + hf * 512:dcn * KB + (hf + 1) * 512],
                            start=(dcn == 0), stop=(dcn == 7)),
                          sem="tM" if (dcn == 7 and hf == 1) else None,
                          deps=[("cctxd", Jg), ("cWo", "done"),
                                ("outev", ti - 2) if tt >= 2 else (("outev", Jg * 4 - 1) if (tt == 0 and Jg >= 1) else None)],
                          key=("a2mm", ti) if (dcn == 7 and hf == 1) else None)
                xe = xt[2 + ti % 2]
                E("sync", lambda e, ti=ti, xe=xe: e.dma_start(out=xe[:], in_=dx1[ti * 128:(ti + 1) * 128, :]),
                  sem="qE%d" % (ti % 2),
                  deps=[*(X1ALL if ti < 2 else []), ("fout", ti - 2) if ti >= 2 else ("ew", 1, 3, 1, "mmz")],
                  key=("oxld", ti))
                E("gpsimd", lambda e, xe=xe: e.tensor_tensor(out=xe[:], in0=xe[:], in1=cbob[:], op=ALU.add),
                  sem="gP", deps=[("oxld", ti), ("cbobld",)], key=("oadd1", ti))
                E("vector", lambda e, xe=xe, ab0=ab0: e.tensor_add(out=xe[:, 0:512], in0=xe[:, 0:512], in1=psf(ab0, 512)),
                  sem="vV", deps=[("a2mm", ti), ("oadd1", ti)], key=("oadd2", ti))
                E("vector", lambda e, xe=xe, ab0=ab0: e.tensor_add(out=xe[:, 512:1024], in0=xe[:, 512:1024],
                                                                   in1=psf(ab0 + 1, 512)),
                  sem="vV", deps=[("oadd2", ti)], key=("outev", ti))
                E("gpsimd", lambda e, ti=ti, xe=xe: e.dma_start(out=dout[ti * 128:(ti + 1) * 128, :], in_=xe[:]),
                  sem="qO" if ti % 2 == 0 else "qO2", deps=[("outev", ti)], key=("fout", ti))
            w.mark(("i1att", Jg), "tM")
        if upto in ("IL1", "IL2", "ILq", "ILs"):
            lastev = {"IL1": ("pcn", NJ - 1, 3), "IL2": ("ctxcev", NT - 1),
                      "ILq": ("qcev", NJ - 1, 7), "ILs": ("scmm", NJ - 1, 3)}[upto]
            for ti in range(NT):
                E("sync", lambda e, ti=ti: e.dma_start(out=xt[ti % 2][:], in_=dx1[ti * 128:(ti + 1) * 128, :]),
                  sem="qF%d" % (ti % 2),
                  deps=[("x1st", NT - 2), ("x1st", NT - 1), lastev,
                        ("fdbg", ti - 2) if ti >= 2 else None], key=("fdbgl", ti))
                E("gpsimd", lambda e, ti=ti: e.dma_start(out=dout[ti * 128:(ti + 1) * 128, :], in_=xt[ti % 2][:]),
                  sem="qO" if ti % 2 == 0 else "qO2", deps=[("fdbgl", ti)], key=("fdbg", ti))
            w.barrier_wait("gpsimd", [("fdbg", NT - 2), ("fdbg", NT - 1), ("oall",), ("yall",)])
            w.barrier_wait("gpsimd", [("tap", nm) for nm in dtaps])
            return

        tap("qdt", lambda: QdT[:], [("qdtev", 7)])
        tap("biasd", lambda: bias16d_t[:], ["dbias"])
        tap("lrun", lambda: l_run[:], [("sdlu", NJ - 1, 7)])
        tap("odacc", lambda: od_acc[:], [("odn", 15)])
        tap("odtb", lambda: odTb[:], ["odtev"])
        tap("otok", lambda: o_tok[:], ["otok"])
        tap("oall", lambda: Oall[:], [("oall",)])


        w.barrier_wait("gpsimd", [("fout", NT - 2), ("fout", NT - 1), ("oall",), ("yall",)])
        w.barrier_wait("gpsimd", [("tap", nm) for nm in dtaps])

    with nc.Block() as block:
        @block.sync
        def _(eng):
            walk(Walk("sync", eng, sems))

        @block.gpsimd
        def _(eng):
            walk(Walk("gpsimd", eng, sems))

        @block.vector
        def _(eng):
            walk(Walk("vector", eng, sems))

        @block.scalar
        def _(eng):
            walk(Walk("scalar", eng, sems))

        @block.tensor
        def _(eng):
            walk(Walk("tensor", eng, sems))

    for g in sem_guards:
        g.__exit__(None, None, None)
    PS_guard.__exit__(None, None, None)
    return nc



def _inputs_for_core(c, kw):
    m = dict(
        x=kw["L0_attn_x"][c],
        wq=kw["L0_attn_wq"], wk=kw["L0_attn_wk"], wv=kw["L0_attn_wv"], wo=kw["L0_attn_wo"],
        slotq=kw["L0_ffn_slot_query"][0],
        dWq=kw["L0_ffn_d_Wq"], dWk=kw["L0_ffn_d_Wk"], dWv=kw["L0_ffn_d_Wv"], dWo=kw["L0_ffn_d_Wo"],
        cWq=kw["L0_ffn_c_Wq"], cWk=kw["L0_ffn_c_Wk"], cWv=kw["L0_ffn_c_Wv"], cWo=kw["L0_ffn_c_Wo"],
        dbq=kw["L0_ffn_d_bq"][None, :], dbk=kw["L0_ffn_d_bk"][None, :],
        dbv=kw["L0_ffn_d_bv"][None, :], dbo=kw["L0_ffn_d_bo"][None, :],
        cbq=kw["L0_ffn_c_bq"][None, :], cbk=kw["L0_ffn_c_bk"][None, :],
        cbv=kw["L0_ffn_c_bv"][None, :], cbo=kw["L0_ffn_c_bo"][None, :],
        lng=kw["L0_ffn_e_ln_g"], lnb=kw["L0_ffn_e_ln_b"],
        eW1=kw["L0_ffn_e_W1"][2 * c:2 * c + 2],
        eWg=kw["L0_ffn_e_Wg"][2 * c:2 * c + 2],
        eW2=kw["L0_ffn_e_W2"][2 * c:2 * c + 2],
        eb1=kw["L0_ffn_e_b1"][2 * c:2 * c + 2],
        ebg=kw["L0_ffn_e_bg"][2 * c:2 * c + 2],
        eb2=kw["L0_ffn_e_b2"][2 * c:2 * c + 2],
    )
    esel = np.zeros((128, 16), np.float32)
    for le in range(2):
        for r in range(8):
            esel[r * 16 + 2 * c + le, le * 8 + r] = 1.0
    m["esel"] = esel
    ind2 = np.zeros((2, 128), np.float32)
    ind2[0, :] = (np.arange(128) % 32 < 16)
    ind2[1, :] = (np.arange(128) % 32 >= 16)
    m["ind2"] = ind2
    sel = np.zeros((128, 16), np.float32)
    for cc in range(8):
        for le in range(2):
            sel[16 * cc + 8 * le + c, 2 * cc + le] = 1.0
    m["bsel"] = sel
    for k in m:
        m[k] = np.ascontiguousarray(m[k], dtype=np.float32)
    return m


def kernel(**kw):
    S = kw["L0_attn_x"].shape[1]
    nc = build_program(S)
    in_maps = [_inputs_for_core(c, kw) for c in range(NCORES)]
    res = run_bass_kernel_spmd(nc, in_maps, core_ids=list(range(NCORES)))
    out = np.stack([res.results[c]["out"] for c in range(NCORES)], axis=0)
    return out.astype(np.float32)


# version 08
# speedup vs baseline: 1.2475x; 1.2475x over previous
import numpy as np
import ml_dtypes
import concourse.bass as bass
import concourse.mybir as mybir
from concourse.bass_utils import run_bass_kernel_spmd

F32 = mybir.dt.float32
BF16 = mybir.dt.bfloat16
AF = mybir.ActivationFunctionType
ALU = mybir.AluOpType
AX = mybir.AxisListType.X

NCORES = 8
D = 1024
NH = 16
HD = 64
F = 4096
NS = 16
EPS = 1e-5
SCALE = 0.125

B0SET = [0, 1, 2, 3, 8, 9, 10, 11]
B64SET = [4, 5, 6, 7, 12, 13, 14, 15]
PAIRS = list(zip(B0SET, B64SET))


def qcol(h):
    if h in B0SET:
        return B0SET.index(h) * 128
    return B64SET.index(h) * 128 + 64

KB = 1024


def _ceil(a, b):
    return (a + b - 1) // b


class Walk:

    lenient = False

    def __init__(self, kind, eng, sems):
        self.kind = kind
        self.eng = eng
        self.sems = sems
        self.ctr = {k: 0 for k in sems}
        self.waited = {k: 0 for k in sems}
        self.ev = {}

    def _wait(self, sem, val):
        if val > self.waited[sem]:
            self.eng.wait_ge(self.sems[sem], val)
            self.waited[sem] = val

    def _dep(self, dep):
        if isinstance(dep, tuple) and len(dep) == 2 and isinstance(dep[0], str) \
                and dep[0] in self.sems and isinstance(dep[1], int):
            return dep
        return self.ev[dep]

    def emit(self, kind, f=None, sem=None, deps=(), key=None, inc=None):
        if inc is None:
            inc = 16 if (sem is not None and sem.startswith("q")) else 1
        if self.kind == kind:
            for dep in deps:
                if dep is None:
                    continue
                try:
                    s, v = self._dep(dep)
                except KeyError:
                    if self.lenient:
                        continue
                    raise
                self._wait(s, v)
            inst = f(self.eng)
            if sem is not None:
                inst.then_inc(self.sems[sem], inc)
        if sem is not None:
            self.ctr[sem] += inc
            if key is not None:
                self.ev[key] = (sem, self.ctr[sem])
        elif key is not None:
            raise ValueError("key requires sem")

    def mark(self, key, sem):
        self.ev[key] = (sem, self.ctr[sem])

    def barrier_wait(self, kind, keys):
        if self.kind == kind:
            for dep in keys:
                if dep is None:
                    continue
                s, v = self._dep(dep)
                self._wait(s, v)


def build_program(S, taps=(), upto="I"):
    Walk.lenient = (upto != "I")
    NT = S // 128
    NJ = S // 512
    nc = bass.Bass("TRN2", target_bir_lowering=False, debug=False)

    dx = nc.dram_tensor("x", [S, D], F32, kind="ExternalInput").ap()
    dwq = nc.dram_tensor("wq", [D, D], F32, kind="ExternalInput").ap()
    dwk = nc.dram_tensor("wk", [256, D], F32, kind="ExternalInput").ap()
    dwv = nc.dram_tensor("wv", [256, D], F32, kind="ExternalInput").ap()
    dwo = nc.dram_tensor("wo", [D, D], F32, kind="ExternalInput").ap()
    dsq = nc.dram_tensor("slotq", [NS, D], F32, kind="ExternalInput").ap()
    dW = {}
    for nm in ("dWq", "dWk", "dWv", "dWo", "cWq", "cWk", "cWv", "cWo"):
        dW[nm] = nc.dram_tensor(nm, [D, D], F32, kind="ExternalInput").ap()
    dB = {}
    for nm in ("dbq", "dbk", "dbv", "dbo", "cbq", "cbk", "cbv", "cbo"):
        dB[nm] = nc.dram_tensor(nm, [1, D], F32, kind="ExternalInput").ap()
    dlng = nc.dram_tensor("lng", [NS, D], F32, kind="ExternalInput").ap()
    dlnb = nc.dram_tensor("lnb", [NS, D], F32, kind="ExternalInput").ap()
    deW1 = nc.dram_tensor("eW1", [2, D, F], F32, kind="ExternalInput").ap()
    deWg = nc.dram_tensor("eWg", [2, D, F], F32, kind="ExternalInput").ap()
    deW2 = nc.dram_tensor("eW2", [2, F, D], F32, kind="ExternalInput").ap()
    deb1 = nc.dram_tensor("eb1", [2, F], F32, kind="ExternalInput").ap()
    debg = nc.dram_tensor("ebg", [2, F], F32, kind="ExternalInput").ap()
    deb2 = nc.dram_tensor("eb2", [2, D], F32, kind="ExternalInput").ap()
    dbsel = nc.dram_tensor("bsel", [128, 16], F32, kind="ExternalInput").ap()
    dind2 = nc.dram_tensor("ind2", [2, 128], F32, kind="ExternalInput").ap()
    desel = nc.dram_tensor("esel", [128, 16], F32, kind="ExternalInput").ap()
    dout = nc.dram_tensor("out", [S, D], F32, kind="ExternalOutput").ap()

    dx1 = nc.dram_tensor("x1d", [S, D], F32).ap()
    dqt = nc.dram_tensor("qtd", [8, 128, S], BF16).ap()
    dewb1 = nc.dram_tensor("ewb1", [16, 128, 8, KB], BF16).ap()
    dewb2 = nc.dram_tensor("ewb2", [8, 128, 8, KB], BF16).ap()
    dewbF = nc.dram_tensor("ewbF", [8, 128, 8, KB], BF16).ap()
    dlrow = nc.dram_tensor("lrowd", [8, KB], F32).ap()
    drl = nc.dram_tensor("rld", [16, 512], F32).ap()
    dcc1i = nc.dram_tensor("cc1i", [NS, D], F32)
    dcc1o = nc.dram_tensor("cc1o", [128, D], F32)
    dcc2i = nc.dram_tensor("cc2i", [NS, D], F32)
    dcc2o = nc.dram_tensor("cc2o", [128, D], F32)

    dtaps = {}
    for name, shape, dt_ in taps:
        dtaps[name] = nc.dram_tensor("tap_" + name, list(shape), dt_, kind="ExternalOutput").ap()

    SBASE = (nc.sbuf_base + 31) // 32 * 32
    cur = [0]

    def alloc_at(name, shape, dtype, offset):
        return nc.alloc_sbuf_tensor_at(name, list(shape), dtype, offset=SBASE + offset)

    def sb(name, shape, dtype, at=None):
        per = int(np.prod(shape[1:])) * mybir.dt.size(dtype)
        if at is None:
            at = cur[0]
            cur[0] = (at + per + 31) // 32 * 32
        return alloc_at(name, shape, dtype, at)

    ident = sb("ident", [128, 128], BF16)
    identf = sb("identf", [128, 128], F32)
    trimask = sb("trimask", [128, 128], BF16)
    st_s = sb("st_s", [128, NT], F32)
    st_b = sb("st_b", [128, NT], F32)
    lnst = sb("lnst", [128, 12], F32)
    bias16 = sb("bias16", [128, 16], F32)
    bias16p = sb("bias16p", [128, 8], F32)
    bias16c = sb("bias16c", [128, 16], F32)
    bias16d_t = sb("bias16d_t", [128, 8], F32)
    bcols = sb("bcols", [128, 48], F32)
    nrm = sb("nrm", [128, 64], F32)
    qnacc = sb("qnacc", [128, 128], F32)
    lstage_off = cur[0]
    lstage = sb("lstage", [128, KB], F32)
    xnsc = sb("xnsc", [128, KB], BF16)
    sqsc = sb("sqsc", [128, 512], BF16)
    sqT = sb("sqT", [128, 128], BF16)
    sqTb = sb("sqTb", [128, 128], F32)
    odTb = sb("odTb", [128, 128], BF16)
    oT = sb("oT", [128, 128], F32)
    o2T = sb("o2T", [128, 128], F32)
    o2Tb = sb("o2Tb", [128, 128], BF16)
    QdT = sb("QdT", [128, 128], BF16)
    QdBD = sb("QdBD", [128, 8 * 32], BF16)

    KcT = sb("KcT", [128, 128], BF16)
    VcT = sb("VcT", [128, 128], F32)
    VcOnes = sb("VcOnes", [16, 16 * 65], BF16)
    onesc = sb("onesc", [1, 128], F32)
    ug = sb("ug", [128, 256], BF16)
    u_sb = sb("u_sb", [128, 256], BF16)
    g_sb = sb("g_sb", [128, 256], BF16)
    y_acc = sb("y_acc", [128, 64], F32)
    bsel_sb = sb("bsel_sb", [128, 16], F32)
    l_run = sb("l_run", [32, 8], F32)
    bonesb = sb("bonesb", [128, 2], BF16)
    rl32 = sb("rl32", [32, 8], F32)
    rlB = sb("rlB", [16, 8], F32)
    lpart = sb("lpart", [32, 2], F32)
    ind2 = sb("ind2", [2, 128], F32)
    mrow2 = sb("mrow2", [2, 8], F32)
    bstg = sb("bstg", [8, 128], F32)

    A1 = cur[0]; cur[0] += 16 * KB + 512
    A3 = cur[0]; cur[0] += 8 * KB
    A4 = cur[0]; cur[0] += 16 * KB
    A5 = cur[0]; cur[0] += 16 * KB
    A6 = cur[0]; cur[0] += 8 * KB
    A7 = cur[0]; cur[0] += 8 * KB
    A8 = cur[0]; cur[0] += 16 * KB
    A9 = cur[0]; cur[0] += 32 * KB
    A10 = cur[0]; cur[0] += 8 * KB
    A11 = cur[0]; cur[0] += 40 * KB
    AKT = cur[0]; cur[0] += max(4 * S, 16 * KB)
    ASQ = cur[0]; cur[0] += 4 * KB

    assert cur[0] <= 212832, f"SBUF over budget: {cur[0]} bytes/partition"

    V_sb = alloc_at("V_sb", [128, NT * 260], BF16, A1)
    W1bfh = alloc_at("W1bfh", [128, 8192], BF16, A1)
    xnTg = alloc_at("xnTg", [128, 8 * 512], BF16, A3)
    QTJ = [alloc_at(f"QTJ{i}", [128, 8 * 512], BF16, A4 + i * 8 * KB) for i in range(2)]
    stage1 = alloc_at("stage1", [128, 4096], F32, A4)
    QcT = alloc_at("QcT", [128, 8 * KB], BF16, A4)
    ctxT = [alloc_at(f"ctxT{i}", [128, 8 * 512], BF16, A5 + i * 8 * KB) for i in range(2)]
    stage2 = alloc_at("stage2", [128, 4096], F32, A5)
    ctxcT = [alloc_at(f"ctxcT{i}", [128, 8 * 512], BF16, A5 + i * 8 * KB) for i in range(2)]
    PT = [alloc_at(f"PT{i}", [128, KB], BF16, A6 + i * 2 * KB) for i in range(4)]
    PTpair = [alloc_at(f"PTp{i}", [128, 2 * KB], BF16, A6 + i * 4 * KB) for i in range(2)]
    KdTg = alloc_at("KdTg", [128, 8 * 512], BF16, A6)
    PcB = [alloc_at(f"PcB{i}", [16, 512], BF16, A6 + i * KB) for i in range(4)]
    KcBD = alloc_at("KcBD", [128, 8 * 80], BF16, A6 + 4 * KB)
    rlb = alloc_at("rlb", [64, 2048], F32, A7)
    ones64 = alloc_at("ones64", [128, 64], F32, A7)
    dbvb = alloc_at("dbvb", [128, KB], F32, A5)
    cbob = alloc_at("cbob", [128, KB], F32, A11)
    xt = [alloc_at(f"xt{i}", [128, KB], F32, A8 + i * 4 * KB) for i in range(4)]
    Wgbfh = alloc_at("Wgbfh", [128, 8192], BF16, A8)
    ffnW = [alloc_at(f"ffnW{i}", [128, 8 * KB], BF16, A9 + i * 16 * KB) for i in range(2)]
    Oall = alloc_at("Oall", [128, KB], F32, A9)
    Yall = alloc_at("Yall", [128, KB], F32, A9 + 4 * KB)
    Gt = alloc_at("Gt", [128, KB], F32, A9 + 8 * KB)
    Bt = alloc_at("Bt", [128, KB], F32, A9 + 12 * KB)
    W2bfw = alloc_at("W2bfw", [128, 8 * KB], BF16, A9 + 16 * KB)
    sqf32 = alloc_at("sqf32", [128, KB], F32, ASQ)
    o_tok = alloc_at("o_tok", [16, KB], F32, lstage_off)
    wstg = [alloc_at(f"wstg{i}", [128, KB], F32, A10 + i * 4 * KB) for i in range(2)]
    pstg = [alloc_at(f"pstg{i}", [128, 4096], F32, A9 + i * 16 * KB) for i in range(2)]
    pwb = [alloc_at(f"pwb{i}", [128, 2048], BF16, A10 + i * 4 * KB) for i in range(2)]
    wqT = alloc_at("wqT", [128, 8 * KB], BF16, A11)
    wkT = alloc_at("wkT", [128, 2 * KB], BF16, A11 + 16 * KB)
    wvT = alloc_at("wvT", [128, 2 * KB], BF16, A11 + 20 * KB)
    woT = alloc_at("woT", [128, 8 * KB], BF16, A11 + 24 * KB)
    VdG = [alloc_at(f"VdG{i}", [128, 4 * 1040], BF16, A11 + i * 9 * KB) for i in range(2)]
    PdB = [alloc_at(f"PdB{i}", [32, 512], BF16, A11 + 18 * KB + i * KB) for i in range(4)]
    PdTG = alloc_at("PdTG", [128, 4 * 256], BF16, A11 + 22 * KB)
    od_acc = alloc_at("od_acc", [16, KB], F32, A11 + 24 * KB)
    b1T = alloc_at("b1T", [128, 64], F32, A11 + 28 * KB)
    bgT = alloc_at("bgT", [128, 64], F32, A11 + 28 * KB + 256)
    b2T = alloc_at("b2T", [128, 16], F32, A11 + 28 * KB + 512)
    eb1c2 = alloc_at("eb1c2", [32, 256], F32, A11 + 29 * KB)
    PcT = [alloc_at(f"PcT{i}", [16, 2048], BF16, A11 + 30 * KB + i * 4 * KB) for i in range(2)]
    Yle = [alloc_at(f"Yle{i}", [8, KB], F32, A6 + i * 4 * KB) for i in range(2)]
    hsel = alloc_at("hsel", [16, KB], BF16, AKT + 11 * KB)
    hTe = alloc_at("hTe", [128, 128], BF16, AKT + 15 * KB)
    esel_sb = alloc_at("esel_sb", [128, 16], F32, AKT + 15 * KB + 256)
    esel_bf = alloc_at("esel_bf", [128, 16], BF16, AKT + 15 * KB + 320)
    Pcn = alloc_at("Pcn", [128, 256], BF16, A11 + 38 * KB)
    Scm = alloc_at("Scm", [128, 16 * 3], F32, A11 + 38 * KB + 512)
    Sct = alloc_at("Sct", [128, 256], F32, A11 + 39 * KB)
    KT = alloc_at("KT", [128, 2 * S], BF16, AKT)
    o2_tok = alloc_at("o2_tok", [16, KB], F32, AKT)
    Vctok = alloc_at("Vctok", [16, KB], F32, AKT + 4 * KB)
    eb1c = alloc_at("eb1c", [32, 256], F32, AKT + 8 * KB)
    ebgc = alloc_at("ebgc", [32, 256], F32, AKT + 9 * KB)
    eb2c = alloc_at("eb2c", [8, 256], F32, AKT + 10 * KB)
    hT = alloc_at("hT", [128, KB], BF16, AKT + 11 * KB)
    h_sb = alloc_at("h_sb", [128, KB], BF16, AKT + 13 * KB)
    qstg = [alloc_at(f"qstg{i}", [128, 512], BF16, A5 + i * KB) for i in range(2)]

    PS_guard = nc.psum_tensor("PS", [128, 4096], F32)
    PS = PS_guard.__enter__()
    PSb = PS.ap().bitcast(BF16)

    def psf(bank, cols=512, rows=128, row0=0):
        return PS[row0:row0 + rows, bank * 512:bank * 512 + cols]

    def psfc(bank, c0, cols, rows=128, row0=0):
        return PS[row0:row0 + rows, bank * 512 + c0:bank * 512 + c0 + cols]

    def psb(bank, c0, cols, rows=128, row0=0):
        return PSb[row0:row0 + rows, bank * 1024 + c0:bank * 1024 + c0 + cols]

    sem_names = ["tM", "vV", "sS", "gP", "cc", "qO", "qM",
                 "qW0", "qW1", "qX0", "qX1", "qQ0", "qQ1", "qT0", "qT1",
                 "qR0", "qR1", "qS0", "qS1", "qLG", "qRB", "qLD0", "qLD1", "qRL",
                 "qF0", "qF1", "qB", "qC", "qE0", "qE1", "qO2",
                 "qPW0", "qPW1", "qPB0", "qPB1"]
    sems = {}
    sem_guards = []
    for nm in sem_names:
        g = nc.semaphore(nm)
        sems[nm] = g.__enter__()
        sem_guards.append(g)

    def ch(t, n):
        return t[:].rearrange("p (c n) -> p c n", n=n)

    def walk(w: Walk):
        E = w.emit

        def tap(name, ap_f, deps):
            if name in dtaps:
                E("gpsimd", lambda e: e.dma_start(out=dtaps[name], in_=ap_f()), sem="qM",
                  deps=deps, key=("tap", name))

        E("gpsimd", lambda e: e.memset(ident[:], 0.0), sem="gP", key="identz")
        E("gpsimd", lambda e: e.affine_select(out=ident[:], in_=ident[:], compare_op=ALU.not_equal,
                                              fill=1.0, base=0, pattern=[[-1, 128]], channel_multiplier=1),
          sem="gP", deps=["identz"], key="ident")
        E("gpsimd", lambda e: e.memset(identf[:], 0.0), sem="gP", deps=["ident"], key="identfz")
        E("gpsimd", lambda e: e.affine_select(out=identf[:], in_=identf[:], compare_op=ALU.not_equal,
                                              fill=1.0, base=0, pattern=[[-1, 128]], channel_multiplier=1),
          sem="gP", deps=["identfz"], key="identf")
        E("gpsimd", lambda e: e.memset(trimask[:], 1.0), sem="gP", deps=["identf"], key="trimaskz")
        E("gpsimd", lambda e: e.affine_select(out=trimask[:], in_=trimask[:], compare_op=ALU.is_ge,
                                              fill=0.0, base=0, pattern=[[1, 128]], channel_multiplier=-1),
          sem="gP", deps=["trimaskz"], key="trimask")
        E("gpsimd", lambda e: e.memset(onesc[:], 1.0), sem="gP", key="onesc")
        E("gpsimd", lambda e: e.memset(V_sb[:], 1.0), sem="gP", key="vones")
        E("gpsimd", lambda e: e.memset(nrm[:, 0:2], 0.0), sem="gP", key="bz0")
        E("gpsimd", lambda e: e.memset(nrm[0:64, 0:1], 1.0), sem="gP", deps=["bz0"])
        E("gpsimd", lambda e: e.memset(nrm[64:128, 1:2], 1.0), sem="gP", deps=["bz0"])
        E("gpsimd", lambda e: e.memset(bonesb[:], 0.0), sem="gP", key="bz1")
        E("gpsimd", lambda e: e.memset(bonesb[0:64, 0:1], 1.0), sem="gP", deps=["bz1"])
        E("gpsimd", lambda e: e.memset(bonesb[64:128, 1:2], 1.0), sem="gP", deps=["bz1"], key="bones")
        E("gpsimd", lambda e: e.memset(qnacc[:, 0:16], 0.0), sem="gP", key="qnz")
        E("gpsimd", lambda e: e.memset(nrm[0:2, 40:48], 0.0), sem="gP", key="qnz2")
        E("gpsimd", lambda e: e.memset(ones64[64:65, 0:64], 1.0), sem="gP", key="ones64")

        prev_tag = [None]

        def prep_attn_w(tag, dsrc, nrows, dstT, cols_per_chunk, permute_q=False):
            nrc = nrows // 128
            pt = prev_tag[0]
            for rc in range(nrc):
                stg = wstg[rc % 2]
                E("sync", lambda e, rc=rc, stg=stg: e.dma_start(out=stg[:], in_=dsrc[rc * 128:(rc + 1) * 128, :]),
                  sem="qW%d" % (rc % 2),
                  deps=[("wprep_cast", tag, rc - 2) if rc >= 2 else (("wprep_done", pt) if pt else None)],
                  key=("wprep_ld", tag, rc))
                E("vector", lambda e, stg=stg: e.tensor_copy(out=xnsc[:], in_=stg[:]),
                  sem="vV",
                  deps=[("wprep_ld", tag, rc),
                        ("wprep_tr", tag, rc - 1) if rc >= 1 else (("wprep_done", pt) if pt else None)],
                  key=("wprep_cast", tag, rc))
                for dcn in range(8):
                    E("tensor", lambda e, dcn=dcn: e.transpose(out=psb(7, dcn * 128, 128),
                                                               in_=xnsc[:, dcn * 128:(dcn + 1) * 128],
                                                               identity=ident[:]),
                      sem="tM" if dcn == 7 else None,
                      deps=[("wprep_cast", tag, rc), "ident",
                            ("wprep_ev", tag, rc - 1) if rc >= 1 else (("wprep_done", pt) if pt else None)],
                      key=("wprep_tr", tag, rc) if dcn == 7 else None)
                if permute_q:
                    for s2 in range(2):
                        h = 2 * rc + s2
                        dc_ = qcol(h)
                        E("vector", lambda e, s2=s2, dc_=dc_: e.tensor_copy(
                            out=ch(dstT, cols_per_chunk)[:, :, dc_:dc_ + 64],
                            in_=psb(7, 0, 1024).rearrange("p (c n) -> p c n", n=128)[:, :, s2 * 64:(s2 + 1) * 64]),
                          sem="vV" if s2 == 1 else None, deps=[("wprep_tr", tag, rc)],
                          key=("wprep_ev", tag, rc) if s2 == 1 else None)
                else:
                    E("vector", lambda e, rc=rc: e.tensor_copy(
                        out=ch(dstT, cols_per_chunk)[:, :, rc * 128:(rc + 1) * 128],
                        in_=psb(7, 0, 1024).rearrange("p (c n) -> p c n", n=128)),
                      sem="vV", deps=[("wprep_tr", tag, rc)], key=("wprep_ev", tag, rc))
            w.mark(("wprep_done", tag), "vV")
            prev_tag[0] = tag

        prep_attn_w("wq", dwq, D, wqT, KB, permute_q=True)
        prep_attn_w("wk", dwk, 256, wkT, 256)
        prep_attn_w("wv", dwv, 256, wvT, 256)
        prep_attn_w("wo", dwo, D, woT, KB)

        def ln_stats(ti, xbuf, kpref, sdst, bdst, extra0=()):
            K = lambda nm: (kpref, nm, ti)
            E("vector", lambda e: e.reduce_sum(out=lnst[:, 0:1], in_=xbuf[:], axis=AX),
              sem="vV", deps=[*extra0], key=K("sum"))
            E("vector", lambda e: e.tensor_mul(out=sqf32[:], in0=xbuf[:], in1=xbuf[:]), sem="vV", key=K("sq"))
            E("vector", lambda e: e.reduce_sum(out=lnst[:, 1:2], in_=sqf32[:], axis=AX),
              sem="vV", deps=[K("sq")], key=K("sumsq"))
            E("vector", lambda e: e.tensor_scalar_mul(out=lnst[:, 2:3], in0=lnst[:, 0:1], scalar1=1.0 / D),
              sem="vV", deps=[K("sum")], key=K("mean"))
            E("vector", lambda e: e.tensor_scalar_mul(out=lnst[:, 3:4], in0=lnst[:, 1:2], scalar1=1.0 / D),
              sem="vV", deps=[K("sumsq")], key=K("ms"))
            E("vector", lambda e: e.tensor_mul(out=lnst[:, 4:5], in0=lnst[:, 2:3], in1=lnst[:, 2:3]),
              sem="vV", deps=[K("mean")], key=K("m2"))
            E("vector", lambda e: e.tensor_sub(out=lnst[:, 3:4], in0=lnst[:, 3:4], in1=lnst[:, 4:5]),
              sem="vV", deps=[K("ms"), K("m2")], key=K("v0"))
            E("vector", lambda e: e.tensor_scalar_add(out=lnst[:, 3:4], in0=lnst[:, 3:4], scalar1=EPS),
              sem="vV", deps=[K("v0")], key=K("var"))
            E("scalar", lambda e: e.activation(out=lnst[:, 5:6], in_=lnst[:, 3:4], func=AF.Ln, bias=0.0, scale=1.0),
              sem="sS", deps=[K("var")], key=K("ln"))
            E("scalar", lambda e: e.activation(out=sdst, in_=lnst[:, 5:6], func=AF.Exp, bias=0.0, scale=-0.5),
              sem="sS", deps=[K("ln")], key=K("rstd"))
            E("vector", lambda e: e.tensor_mul(out=lnst[:, 6:7], in0=lnst[:, 2:3], in1=sdst),
              sem="vV", deps=[K("rstd")], key=K("mr"))
            E("vector", lambda e: e.tensor_scalar_mul(out=bdst, in0=lnst[:, 6:7], scalar1=-1.0),
              sem="vV", deps=[K("mr"), ("xnapply", ti - 2) if (kpref == "l1" and ti >= 2) else None],
              key=(kpref + "bias", ti))

        def ln_tile(ti, xbuf):
            p = ti % 2
            sdst = lnst[:, 8 + 2 * p:9 + 2 * p]
            bdst = lnst[:, 9 + 2 * p:10 + 2 * p]
            ln_stats(ti, xbuf, "l1", sdst, bdst,
                     extra0=[("xld", ti), ("l1", "rstd", ti - 1) if ti >= 1 else None,
                             ("tap", "lnst0") if ("lnst0" in dtaps and ti == 2) else None])
            E("scalar", lambda e: e.activation(out=xnsc[:], in_=xbuf[:], func=AF.Identity, bias=bdst, scale=sdst),
              sem="sS",
              deps=[("l1bias", ti), ("xntr", ti - 1) if ti >= 1 else ("wprep_done", "wo"),
                    ("tap", "xnsc0") if ("xnsc0" in dtaps and ti == 1) else None],
              key=("xnapply", ti))

        def xn_transpose(ti, dstcol):
            bank = 6 if ti % 2 == 0 else 7
            for dcn in range(8):
                E("tensor", lambda e, dcn=dcn, bank=bank: e.transpose(
                    out=psb(bank, dcn * 128, 128), in_=xnsc[:, dcn * 128:(dcn + 1) * 128], identity=ident[:]),
                  sem="tM" if dcn == 7 else None,
                  deps=[("xnapply", ti), "ident",
                        ("xntrev", ti - 2) if ti >= 2 else ("wprep_done", "wo"),
                        ("tap", "xng0") if ("xng0" in dtaps and ti == 4) else None],
                  key=("xntr", ti) if dcn == 7 else None)
            E("scalar", lambda e, bank=bank, dstcol=dstcol: e.copy(
                out=ch(xnTg, 512)[:, :, dstcol:dstcol + 128],
                in_=psb(bank, 0, 1024).rearrange("p (c n) -> p c n", n=128)),
              sem="sS", deps=[("xntr", ti)], key=("xntrev", ti))

        for Jg in range(NJ):
            for tt in range(4):
                ti = Jg * 4 + tt
                xbuf = xt[ti % 2]
                E("sync", lambda e, ti=ti, xbuf=xbuf: e.dma_start(out=xbuf[:], in_=dx[ti * 128:(ti + 1) * 128, :]),
                  sem="qX%d" % (ti % 2), deps=[("xnapply", ti - 2)] if ti >= 2 else [], key=("xld", ti))
                ln_tile(ti, xbuf)
                if ti == 0:
                    tap("xnsc0", lambda: xnsc[:], [("xnapply", 0)])
                    tap("lnst0", lambda: lnst[:], [("xnapply", 0)])
                xn_transpose(ti, tt * 128)
                if ti == 3:
                    tap("xng0", lambda: xnTg[:], [("xntrev", 3)])
                for dcn in range(8):
                    E("tensor", lambda e, dcn=dcn, tt=tt: e.matmul(
                        out=psf(4, 256),
                        lhsT=ch(xnTg, 512)[:, dcn, tt * 128:(tt + 1) * 128],
                        rhs=ch(wvT, 256)[:, dcn, :],
                        start=(dcn == 0), stop=(dcn == 7)),
                      sem="tM" if dcn == 7 else None,
                      deps=[("xntrev", ti), ("wprep_done", "wv"),
                            ("vev", ti - 1) if ti >= 1 else "vones"],
                      key=("vmm", ti) if dcn == 7 else None)
                E("scalar", lambda e, ti=ti: e.copy(
                    out=V_sb[:].rearrange("p (t g n) -> p t g n", g=4, n=65)[:, ti, :, 0:64],
                    in_=psf(4, 256).rearrange("p (g n) -> p g n", n=64)),
                  sem="sS", deps=[("vmm", ti), "vones"], key=("vev", ti))
            for qc in range(8):
                bank = 2 + qc % 2
                for dcn in range(8):
                    E("tensor", lambda e, dcn=dcn, qc=qc, bank=bank: e.matmul(
                        out=psf(bank, 512),
                        lhsT=ch(wqT, KB)[:, dcn, qc * 128:(qc + 1) * 128],
                        rhs=ch(xnTg, 512)[:, dcn, :],
                        start=(dcn == 0), stop=(dcn == 7)),
                      sem="tM" if dcn == 7 else None,
                      deps=[("xntrev", Jg * 4 + 3), ("wprep_done", "wq"),
                            ("qev", Jg, qc - 2) if qc >= 2 else (("qev", Jg - 1, 6 + qc) if Jg >= 1 else None)],
                      key=("qmm", Jg, qc) if dcn == 7 else None)
                stq = qstg[qc % 2]
                E("scalar", lambda e, bank=bank, stq=stq: e.copy(out=stq[:], in_=psf(bank, 512)),
                  sem="sS", deps=[("qmm", Jg, qc), ("qdma", Jg, qc - 2) if qc >= 2 else
                                  (("qdma", Jg - 1, 6 + qc) if Jg >= 1 else None)],
                  key=("qev", Jg, qc))
                E("gpsimd", lambda e, qc=qc, Jg=Jg, stq=stq: e.dma_start(
                    out=dqt[qc, :, Jg * 512:(Jg + 1) * 512], in_=stq[:]),
                  sem="qQ%d" % (qc % 2), deps=[("qev", Jg, qc)], key=("qdma", Jg, qc))
                E("scalar", lambda e, stq=stq: e.activation(out=sqsc[:], in_=stq[:], func=AF.Square, bias=0.0, scale=1.0),
                  sem="sS",
                  deps=[("qev", Jg, qc),
                        ("qnmm", Jg, qc - 1) if qc >= 1 else (("knmm", Jg - 1, 1) if Jg >= 1 else None)],
                  key=("qsq", Jg, qc))
                E("tensor", lambda e: e.matmul(out=psf(5, 512, rows=2), lhsT=bonesb[:], rhs=sqsc[:],
                                               start=True, stop=True),
                  sem="tM", deps=[("qsq", Jg, qc), "bones",
                                  ("qnred0", Jg, qc - 1) if qc >= 1 else (("knred0", Jg - 1, 1) if Jg >= 1 else None)],
                  key=("qnmm", Jg, qc))
                E("vector", lambda e, qc=qc: e.reduce_max(out=qnacc[0:2, 16 + qc:17 + qc],
                                                          in_=psf(5, 512, rows=2), axis=AX),
                  sem="vV", deps=[("qnmm", Jg, qc), ("qnupd", Jg - 1, qc) if Jg >= 1 else "qnz"],
                  key=("qnred0", Jg, qc))
                E("vector", lambda e, qc=qc: e.tensor_max(out=qnacc[0:2, qc:qc + 1],
                                                          in0=qnacc[0:2, qc:qc + 1],
                                                          in1=qnacc[0:2, 16 + qc:17 + qc]),
                  sem="vV", deps=[("qnred0", Jg, qc), "qnz"], key=("qnupd", Jg, qc))
            for kc in range(2):
                bank = 2 + kc
                for dcn in range(8):
                    E("tensor", lambda e, dcn=dcn, kc=kc, bank=bank: e.matmul(
                        out=psf(bank, 512),
                        lhsT=ch(wkT, 256)[:, dcn, kc * 128:(kc + 1) * 128],
                        rhs=ch(xnTg, 512)[:, dcn, :],
                        start=(dcn == 0), stop=(dcn == 7)),
                      sem="tM" if dcn == 7 else None,
                      deps=[("xntrev", Jg * 4 + 3), ("wprep_done", "wk"), ("qev", Jg, 6 + kc)],
                      key=("kmm", Jg, kc) if dcn == 7 else None)
                E("scalar", lambda e, kc=kc, Jg=Jg, bank=bank: e.copy(
                    out=KT[:, kc * S + Jg * 512:kc * S + (Jg + 1) * 512], in_=psf(bank, 512)),
                  sem="sS", deps=[("kmm", Jg, kc)], key=("kev", Jg, kc))
                E("scalar", lambda e, kc=kc, Jg=Jg: e.activation(
                    out=sqsc[:], in_=KT[:, kc * S + Jg * 512:kc * S + (Jg + 1) * 512],
                    func=AF.Square, bias=0.0, scale=1.0),
                  sem="sS", deps=[("kev", Jg, kc),
                                  ("knmm", Jg, kc - 1) if kc >= 1 else ("qnmm", Jg, 7)],
                  key=("ksq", Jg, kc))
                E("tensor", lambda e: e.matmul(out=psf(5, 512, rows=2), lhsT=bonesb[:], rhs=sqsc[:],
                                               start=True, stop=True),
                  sem="tM", deps=[("ksq", Jg, kc), "bones",
                                  ("knred0", Jg, kc - 1) if kc >= 1 else ("qnred0", Jg, 7)],
                  key=("knmm", Jg, kc))
                E("vector", lambda e, kc=kc: e.reduce_max(out=qnacc[0:2, 24 + kc:25 + kc],
                                                          in_=psf(5, 512, rows=2), axis=AX),
                  sem="vV", deps=[("knmm", Jg, kc), ("knupd", Jg - 1, kc) if Jg >= 1 else "qnz"],
                  key=("knred0", Jg, kc))
                E("vector", lambda e, kc=kc: e.tensor_max(out=qnacc[0:2, 8 + kc:9 + kc],
                                                          in0=qnacc[0:2, 8 + kc:9 + kc],
                                                          in1=qnacc[0:2, 24 + kc:25 + kc]),
                  sem="vV", deps=[("knred0", Jg, kc), "qnz"], key=("knupd", Jg, kc))

        E("scalar", lambda e: e.activation(out=qnacc[0:2, 32:40], in_=qnacc[0:2, 0:8], func=AF.Sqrt,
                                           bias=0.0, scale=1.0),
          sem="sS", deps=[("qnupd", NJ - 1, 7)], key="qsqrt")
        E("scalar", lambda e: e.activation(out=qnacc[0:2, 40:42], in_=qnacc[0:2, 8:10], func=AF.Sqrt,
                                           bias=0.0, scale=1.0),
          sem="sS", deps=[("knupd", NJ - 1, 1)], key="ksqrt")
        E("gpsimd", lambda e: e.dma_start(out=qnacc[0:1, 48:58], in_=qnacc[0:1, 32:42]),
          sem="qM", deps=["qsqrt", "ksqrt"])
        E("gpsimd", lambda e: e.dma_start(out=qnacc[0:1, 58:68], in_=qnacc[1:2, 32:42]),
          sem="qM", deps=["qsqrt", "ksqrt"], key="nflat")
        for slot in range(8):
            for s2 in range(2):
                h = PAIRS[slot][s2]
                qcol_ = 48 + (h % 2) * 10 + h // 2
                kcol_ = 48 + ((h >> 2) % 2) * 10 + 8 + (h >> 2) // 2
                col = slot * 2 + s2
                E("vector", lambda e, qcol_=qcol_, kcol_=kcol_, col=col: e.tensor_tensor(
                    out=nrm[0:1, 32 + col:33 + col], in0=qnacc[0:1, qcol_:qcol_ + 1],
                    in1=qnacc[0:1, kcol_:kcol_ + 1], op=ALU.mult),
                  sem="vV", deps=["nflat"], key=("mh0", col))
                E("vector", lambda e, col=col: e.tensor_scalar_mul(
                    out=nrm[0:1, 32 + col:33 + col], in0=nrm[0:1, 32 + col:33 + col], scalar1=-SCALE * 1.02),
                  sem="vV", deps=[("mh0", col)], key=("mh", col))
        E("tensor", lambda e: e.matmul(out=psf(5, 16), lhsT=onesc[0:1, :], rhs=nrm[0:1, 32:48],
                                       start=True, stop=True),
          sem="tM", deps=[("mh", 15), "onesc", ("knred0", NJ - 1, 1)], key="mhbc")
        E("vector", lambda e: e.tensor_copy(out=bias16[:], in_=psf(5, 16)), sem="vV",
          deps=["mhbc"], key="bias16")
        for pi_ in range(8):
            E("vector", lambda e, pi_=pi_: e.tensor_tensor(
                out=bias16p[:, pi_:pi_ + 1], in0=bias16[:, 2 * pi_:2 * pi_ + 1],
                in1=bias16[:, 2 * pi_ + 1:2 * pi_ + 2], op=ALU.min),
              sem="vV", deps=["bias16"], key="bias16pz" if pi_ == 7 else None)
        tap("qt", lambda: dqt, [("qdma", NJ - 1, 7)])
        tap("kt", lambda: KT[:], [("kev", NJ - 1, 1)])
        tap("vsb", lambda: V_sb[:], [("vev", NT - 1)])
        tap("bias16", lambda: bias16[:], ["bias16"])
        tap("wqT", lambda: wqT[:], [("wprep_done", "wq")])

        def ln2_tile(ti, xbuf):
            ln_stats(ti, xbuf, "l2", st_s[:, ti:ti + 1], st_b[:, ti:ti + 1],
                     extra0=[("x1ev", ti), ("l2", "rstd", ti - 1) if ti >= 1 else None])

        FNAMES = ("dWq", "dWk", "dWv", "dWo", "cWq", "cWk", "cWv", "cWo")
        pp_pieces = []
        _sc = [0, 0]

        def _add_small(src, dst, skey):
            vs = _sc[0] % 8
            pp_pieces.append(dict(big=False, sl=vs % 2, vq=vs // 2, src=src,
                                  dsts=[(dst, 0)], skey=skey))
            _sc[0] += 1

        for m, nm in enumerate(FNAMES):
            for rc in range(8):
                _add_small(dW[nm][rc * 128:(rc + 1) * 128, :], dewbF[m][:, rc:rc + 1, :], ("pwsF", nm, rc))
        for mat, dsrc in ((0, deW1), (1, deWg)):
            for le in range(2):
                for rcb in range(8):
                    dsts = [(dewb1[(mat * 2 + le) * 4 + q4][:, rcb:rcb + 1, :], q4) for q4 in range(4)]
                    pp_pieces.append(dict(big=True, sl=_sc[1] % 2, vq=None,
                                          src=dsrc[le, rcb * 128:(rcb + 1) * 128, :],
                                          dsts=dsts, skey=("pws1", mat, le, rcb)))
                    _sc[1] += 1
        for le in range(2):
            for wv2 in range(4):
                for rc in range(8):
                    _add_small(deW2[le, wv2 * KB + rc * 128:wv2 * KB + (rc + 1) * 128, :],
                               dewb2[le * 4 + wv2][:, rc:rc + 1, :], ("pws2", le, wv2, rc))

        pp_state = dict(loads=0, casts=0,
                        lastload={0: None, 1: None},
                        lastcast={},
                        laststore={0: ("wprep_cast", "wo", 6), 1: ("wprep_cast", "wo", 7)})

        def pp_quota(t):
            q = min(max(t - 8, 0) * 6, 64)
            q += min(max(t - 19, 0) * 2, 32)
            q += min(max(t - 35, 0) * 4, 64)
            return q

        def pp_emit_n(nl, ncst):
            st = pp_state
            while st["casts"] < min(ncst, len(pp_pieces)):
                k = st["casts"]
                pc = pp_pieces[k]
                sl = pc["sl"]
                ckeys = []
                if pc["big"]:
                    for h in range(2):
                        E("gpsimd", lambda e, h=h, sl=sl: e.tensor_copy(
                            out=pwb[h][:], in_=pstg[sl][:, h * 2048:(h + 1) * 2048]),
                          sem="gP", deps=[st["lastload"][sl], st["laststore"][h]],
                          key=("pwc", k, h))
                        ckeys.append(("pwc", k, h))
                    st["lastcast"][("p", sl)] = ckeys
                    for dst, q4 in pc["dsts"]:
                        h = q4 // 2
                        col = (q4 % 2) * KB
                        E("gpsimd", lambda e, dst=dst, h=h, col=col: e.dma_start(
                            out=dst, in_=pwb[h][:, col:col + KB].rearrange("p (o n) -> p o n", o=1)),
                          sem="qPB%d" % h, deps=[("pwc", k, h)], key=(*pc["skey"], q4))
                        st["laststore"][h] = (*pc["skey"], q4)
                else:
                    h = sl
                    vq = pc["vq"]
                    E("gpsimd", lambda e, h=h, sl=sl, vq=vq: e.tensor_copy(
                        out=pwb[h][:, 0:KB], in_=pstg[sl][:, vq * KB:(vq + 1) * KB]),
                      sem="gP", deps=[st["lastload"][sl], st["laststore"][h]],
                      key=("pwc", k, 0))
                    st["lastcast"][("p", sl, vq)] = [("pwc", k, 0)]
                    dst, _ = pc["dsts"][0]
                    E("gpsimd", lambda e, dst=dst, h=h: e.dma_start(
                        out=dst, in_=pwb[h][:, 0:KB].rearrange("p (o n) -> p o n", o=1)),
                      sem="qPB%d" % h, deps=[("pwc", k, 0)], key=(*pc["skey"], 0))
                    st["laststore"][h] = (*pc["skey"], 0)
                st["casts"] = k + 1
            while st["loads"] < min(nl, len(pp_pieces)):
                k = st["loads"]
                pc = pp_pieces[k]
                sl = pc["sl"]
                if pc["big"]:
                    rs = [("p", sl)] + [("p", sl, q) for q in range(4)]
                    tgt = pstg[sl][:]
                else:
                    rs = [("p", sl), ("p", sl, pc["vq"])]
                    tgt = pstg[sl][:, pc["vq"] * KB:(pc["vq"] + 1) * KB]
                deps = []
                for r in rs:
                    deps += st["lastcast"].get(r, [])
                E("gpsimd", lambda e, tgt=tgt, src=pc["src"]: e.dma_start(out=tgt, in_=src),
                  sem="qPW%d" % sl, deps=deps, key=("pwl", k))
                st["lastload"][sl] = ("pwl", k)
                st["loads"] = k + 1

        def pp_emit(t):
            pp_emit_n(pp_quota(t), pp_quota(t - 1))

        def kinfo(h):
            g = h >> 2
            return g // 2, 64 * (g % 2)

        def qinfo(h):
            if h in B0SET:
                return B0SET.index(h), 0
            return B64SET.index(h), 64

        for J in range(NJ):
            qb = QTJ[J % 2]
            for qcc in range(8):
                E("sync", lambda e, J=J, qb=qb, qcc=qcc: e.dma_start(
                    out=qb[:, qcc * 512:(qcc + 1) * 512], in_=dqt[qcc, :, J * 512:(J + 1) * 512]),
                  sem="qT%d" % (J % 2),
                  deps=[("qdma", J, 6), ("qdma", J, 7), ("scJlast", J - 2) if J >= 2 else None],
                  key=("qtj", J) if qcc == 7 else None)
            ntt = 4 * J + 4
            nbt = _ceil(ntt, 2)
            for pi in range(8):
                ha, hb = PAIRS[pi]
                ob = 4 if pi % 2 == 0 else 6

                def emit_pv(tv, pi=pi, ha=ha, hb=hb, ob=ob, ntt=ntt, J=J):
                    cv = max(0, (tv - 4 * J) * 128)
                    am = tv >= 4 * J
                    for s2 in range(2):
                        h = (ha, hb)[s2]
                        g = h >> 2
                        ptb = PTpair[tv % 2]
                        E("tensor", lambda e, s2=s2, tv=tv, cv=cv, g=g, ptb=ptb, ob=ob, ntt=ntt: e.matmul(
                            out=psfc(ob + s2, cv, 512 - cv, rows=65),
                            lhsT=V_sb[:].rearrange("p (t gg n) -> p t gg n", gg=4, n=65)[:, tv, g, :],
                            rhs=ptb[:, s2 * 512 + cv:(s2 + 1) * 512],
                            start=(tv == 0), stop=(tv == ntt - 1)),
                          sem="tM" if s2 == 1 else None,
                          deps=[("exp", J, pi, tv),
                                ("mskd", J, pi, tv) if am else None,
                                ("ctxevp", J, pi - 2) if (tv == 0 and pi >= 2) else None],
                          key=("pv", J, pi, tv) if s2 == 1 else None)
                    w.mark(("pvdone", J, pi, tv), "tM")

                for ts in range(ntt):
                    it = ts
                    c0 = max(0, (it - 4 * J) * 128)
                    first_dep = None
                    extra_dep = None
                    if ts == 0:
                        if pi >= 2:
                            first_dep = ("ctxevp", J, pi - 2)
                        elif J >= 1:
                            first_dep = ("Jwo", J - 1)
                            extra_dep = ("x1ev", 4 * J - 1)
                        else:
                            first_dep = "bias16"
                    for s2 in range(2):
                        h = (ha, hb)[s2]
                        kc2, kr = kinfo(h)
                        qc2, qr = qinfo(h)
                        E("tensor", lambda e, kc2=kc2, kr=kr, qc2=qc2, qr=qr, it=it, s2=s2, c0=c0, ts=ts, qb=qb: e.matmul(
                            out=psfc((ts % 2) * 2 + s2, c0, 512 - c0),
                            lhsT=KT[kr:kr + 64, kc2 * S + it * 128:kc2 * S + (it + 1) * 128],
                            rhs=ch(qb, 512)[qr:qr + 64, qc2, c0:512],
                            start=True, stop=True),
                          sem="tM" if s2 == 1 else None,
                          deps=[("qtj", J), ("kev", NJ - 1, 1), "bias16",
                                ("vev", NT - 1) if (J == 0 and pi == 0 and ts == 0) else None,
                                ("exp", J, pi, ts - 2) if ts >= 2 else None,
                                first_dep, extra_dep],
                          key=("sc", J, pi, ts) if s2 == 1 else None)
                    E("scalar", lambda e, ptp=PTpair[ts % 2], pi=pi, ts=ts: e.activation(
                        out=ptp[:, 0:1024], in_=PS[:, (ts % 2) * 1024:(ts % 2) * 1024 + 1024],
                        func=AF.Exp, bias=bias16p[:, pi:pi + 1], scale=SCALE),
                      sem="sS",
                      deps=[("sc", J, pi, ts), "bias16pz",
                            ("pvdone", J, pi, ts - 2) if ts >= 2 else first_dep,
                            ("mskd", J, pi, ts - 2) if (ts >= 2 and ts - 2 >= 4 * J) else None],
                      key=("exph", J, pi, ts, 1))
                    w.mark(("exph", J, pi, ts, 0), "sS")
                    w.mark(("exp", J, pi, ts), "sS")
                    anymask = it >= 4 * J
                    if anymask:
                        cb = (it - 4 * J) * 128
                        for s2 in range(2):
                            ptb = PTpair[ts % 2]
                            E("gpsimd", lambda e, cb=cb, s2=s2, ptb=ptb: e.tensor_mul(
                                out=ptb[:, s2 * 512 + cb:s2 * 512 + cb + 128],
                                in0=ptb[:, s2 * 512 + cb:s2 * 512 + cb + 128], in1=trimask[:]),
                              sem="gP", deps=[("exph", J, pi, ts, s2), "trimask"],
                              key=("msk", J, pi, ts))
                        w.mark(("mskd", J, pi, ts), "gP")
                    if ts >= 1:
                        emit_pv(ts - 1)
                if ntt >= 1:
                    emit_pv(ntt - 1)
                if pi % 2 == 1:
                    cT = ctxT[J % 2]
                    for pj in (pi - 1, pi):
                        obj = 4 if pj % 2 == 0 else 6
                        if pj == pi:
                            lwar = ("drl", J, pi - 1, 1)
                        elif pi >= 3:
                            lwar = ("drl", J, pi - 2, 1)
                        else:
                            lwar = ("drl", J - 1, 7, 1) if J >= 1 else None
                        E("vector", lambda e, obj=obj: e.reciprocal(
                            out=lstage[64:65, 0:1024], in_=PS[64:65, obj * 512:(obj + 2) * 512]),
                          sem="vV", deps=[("pvdone", J, pj, ntt - 1), lwar],
                          key=("dlrec", J, pj))
                        for s2 in range(2):
                            E("tensor", lambda e, s2=s2, obj=obj: e.matmul(
                                out=PS[64:128, (obj + s2) * 512:(obj + s2 + 1) * 512],
                                lhsT=ones64[64:65, 0:64],
                                rhs=lstage[64:65, s2 * 512:(s2 + 1) * 512],
                                start=True, stop=True, tile_position=(64, 64)),
                              sem="tM",
                              deps=[("dlrec", J, pj), "ones64"],
                              key=("drl", J, pj, s2))
                        for s2 in range(2):
                            hc = (pj * 2 + s2) % 4
                            E("vector", lambda e, obj=obj, s2=s2, hc=hc: e.tensor_copy(
                                out=rlb[0:64, hc * 512:(hc + 1) * 512],
                                in_=PS[64:128, (obj + s2) * 512:(obj + s2 + 1) * 512]),
                              sem="vV", deps=[("drl", J, pj, s2)], key=("drle", J, pj, s2))
                        for s2 in range(2):
                            h = PAIRS[pj][s2]
                            hc = (pj * 2 + s2) % 4
                            E("vector", lambda e, h=h, obj=obj, s2=s2, hc=hc, cT=cT: e.tensor_tensor(
                                out=ch(cT, 512)[64 * (h % 2):64 * (h % 2) + 64, h // 2, :],
                                in0=PS[0:64, (obj + s2) * 512:(obj + s2 + 1) * 512],
                                in1=rlb[:, hc * 512:(hc + 1) * 512],
                                op=ALU.mult),
                              sem="vV",
                              deps=[("drle", J, pj, s2),
                                    ("Jwo", J - 2) if (J >= 2 and pj == pi - 1 and s2 == 0) else None],
                              key=("ctxevp", J, pj) if s2 == 1 else None)
                pp_emit(J * 8 + pi + 1)
            w.mark(("ctxevd", J), "vV")
            w.mark(("scJlast", J), "tM")
            cT = ctxT[J % 2]
            for tt in range(4):
                ti = J * 4 + tt
                wb = 0 if tt % 2 == 0 else 2
                for dcn in range(8):
                    for hf in range(2):
                        E("tensor", lambda e, dcn=dcn, hf=hf, tt=tt, wb=wb, cT=cT: e.matmul(
                            out=psf(wb + hf, 512),
                            lhsT=ch(cT, 512)[:, dcn, tt * 128:(tt + 1) * 128],
                            rhs=ch(woT, KB)[:, dcn, hf * 512:(hf + 1) * 512],
                            start=(dcn == 0), stop=(dcn == 7)),
                          sem="tM" if (dcn == 7 and hf == 1) else None,
                          deps=[("ctxevd", J), ("wprep_done", "wo"),
                                ("x1ev", ti - 2) if tt >= 2 else (("x1ev", J * 4 - 1) if (tt == 0 and J >= 1) else None)],
                          key=("womm", ti) if (dcn == 7 and hf == 1) else None)
                xre = xt[2 + ti % 2]
                E("sync", lambda e, ti=ti, xre=xre: e.dma_start(out=xre[:], in_=dx[ti * 128:(ti + 1) * 128, :]),
                  sem="qR%d" % (ti % 2),
                  deps=[("x1st", ti - 2) if ti >= 2 else None,
                        ("l2", "sq", ti - 2) if ti >= 2 else None], key=("xreld", ti))
                E("vector", lambda e, wb=wb, xre=xre: e.tensor_add(out=xre[:], in0=xre[:],
                                                                   in1=PS[:, wb * 512:(wb + 2) * 512]),
                  sem="vV", deps=[("womm", ti), ("xreld", ti)], key=("x1ev", ti))
                E("gpsimd", lambda e, ti=ti, xre=xre: e.dma_start(out=dx1[ti * 128:(ti + 1) * 128, :], in_=xre[:]),
                  sem="qS%d" % (ti % 2), deps=[("x1ev", ti)], key=("x1st", ti))
                ln2_tile(ti, xre)
            w.mark(("Jwo", J), "tM")

        tap("ctx0", lambda: ctxT[0][:], [("ctxevd", 0)])
        tap("x1", lambda: dx1, [("x1st", NT - 1)])

        FD = ("Jwo", NJ - 1)
        X1ALL = [("x1st", NT - 2), ("x1st", NT - 1)]

        for _k in range(len(pp_pieces)):
            pp_emit_n(_k + 1, _k)
        pp_emit_n(len(pp_pieces), len(pp_pieces))
        PP_DONE = [pp_state["laststore"][0], pp_state["laststore"][1]]
        FIDX = {nm: i for i, nm in enumerate(FNAMES)}
        _slot_first = [True, True]

        def load_ffnW(tag, dsrc, slot, free_ev=None):
            deps = [free_ev]
            if _slot_first[slot]:
                deps += PP_DONE
                _slot_first[slot] = False
            E("sync", lambda e, m=FIDX[tag], slot=slot: e.dma_start(
                out=ffnW[slot][:].rearrange("p (a n) -> p a n", n=KB), in_=dewbF[m]),
              sem="qW%d" % slot, deps=deps, key=(tag, "cast", 7))
            w.mark((tag, "cast", 6), "qW%d" % slot)
            w.mark((tag, "done"), "qW%d" % slot)

        qBprev = [None]

        def qb_dma(kind_eng, f, key, extra=()):
            E(kind_eng, f, sem="qB" if kind_eng == "sync" else "qC", deps=[qBprev[0], *extra], key=key)
            qBprev[0] = key

        BIAS_ORDER = ("dbk", "dbq", "dbo", "cbq", "cbk", "cbv")
        for i, nm in enumerate(BIAS_ORDER):
            qb_dma("sync", lambda e, nm=nm: e.dma_start(
                out=bstg[:], in_=dB[nm].rearrange("a (b c) -> (a b) c", c=128)),
                ("fbld", i), extra=[("fbev", i - 1) if i >= 1 else None])
            E("tensor", lambda e: e.transpose(out=PS[0:128, 2 * 512:2 * 512 + 8], in_=bstg[:], identity=identf[0:8, 0:8]),
              sem="tM", deps=[("fbld", i), "identf", FD, ("x1ev", NT - 1), ("x1ev", NT - 2),
                              ("fbev", i - 1) if i >= 1 else None],
              key=("fbtr", i))
            E("vector", lambda e, i=i: e.tensor_copy(out=bcols[:, i * 8:(i + 1) * 8], in_=PS[0:128, 2 * 512:2 * 512 + 8]),
              sem="vV", deps=[("fbtr", i)], key=("fbev", i))
        qb_dma("sync", lambda e: e.dma_start(out=Vctok[:], in_=dsq), ("fsqld",))
        qb_dma("sync", lambda e: e.dma_start(out=ind2[:], in_=dind2), ("find2",))
        qb_dma("sync", lambda e: e.dma_start(out=dbvb[:], in_=dB["dbv"][0:1, :].broadcast_to([128, KB])), ("fdbvb",))
        E("vector", lambda e: e.tensor_copy(out=h_sb[0:16, :], in_=Vctok[:]),
          sem="vV", deps=[("fsqld",)], key="fsqcast")
        for oc in range(8):
            E("tensor", lambda e, oc=oc: e.transpose(out=psb(3, oc * 16, 16), in_=h_sb[0:16, oc * 128:(oc + 1) * 128],
                                                     identity=ident[0:16, 0:16]),
              sem="tM" if oc == 7 else None, deps=["fsqcast", "ident", FD, ("fbtr", 5),
                                                    ("x1ev", NT - 1), ("x1ev", NT - 2)],
              key="fsqtrb" if oc == 7 else None)
        E("vector", lambda e: e.tensor_copy(out=sqT[:], in_=psb(3, 0, 128)), sem="vV",
          deps=["fsqtrb"], key="fsqTev")
        for oc in range(8):
            E("tensor", lambda e, oc=oc: e.transpose(out=PS[0:128, 4 * 512 + oc * 16:4 * 512 + (oc + 1) * 16],
                                                     in_=Vctok[0:16, oc * 128:(oc + 1) * 128], identity=identf[0:16, 0:16]),
              sem="tM" if oc == 7 else None, deps=[("fsqld",), "identf", FD, "fsqtrb"],
              key="fsqtrf" if oc == 7 else None)
        for oc in range(8):
            E("vector", lambda e, oc=oc: e.tensor_scalar_add(
                out=sqTb[:, oc * 16:(oc + 1) * 16], in0=PS[0:128, 4 * 512 + oc * 16:4 * 512 + (oc + 1) * 16],
                scalar1=bcols[:, 16 + oc:17 + oc]),
              sem="vV", deps=["fsqtrf", ("fbev", 2)], key=("fsqb", oc))

        load_ffnW("dWq", dW["dWq"], 0)
        for oc in range(8):
            for dcn in range(8):
                E("tensor", lambda e, oc=oc, dcn=dcn: e.matmul(
                    out=psf(0, 16), lhsT=ffnW[0][:, dcn * KB + oc * 128:dcn * KB + (oc + 1) * 128],
                    rhs=sqT[:, dcn * 16:(dcn + 1) * 16], start=(dcn == 0), stop=(dcn == 7)),
                  sem="tM" if dcn == 7 else None,
                  deps=[("dWq", "done"), "fsqTev", FD, ("qdtev", oc - 1) if oc >= 1 else "fsqtrf"],
                  key=("qdtmm", oc) if dcn == 7 else None)
            E("scalar", lambda e, oc=oc: e.activation(out=QdT[:, oc * 16:(oc + 1) * 16], in_=psf(0, 16),
                                                      func=AF.Identity, bias=bcols[:, 8 + oc:9 + oc], scale=1.0),
              sem="sS", deps=[("qdtmm", oc), ("fbev", 1)], key=("qdtev", oc))
        E("gpsimd", lambda e: e.memset(QdBD[:], 0.0), sem="gP", deps=[("qdtev", 7)], key="qdbdz")
        for pp in range(8):
            E("vector", lambda e, pp=pp: e.tensor_copy(out=QdBD[0:64, pp * 32:pp * 32 + 16],
                                                       in_=QdT[0:64, pp * 16:(pp + 1) * 16]),
              sem="vV", deps=[("qdtev", pp), "qdbdz"])
            E("vector", lambda e, pp=pp: e.tensor_copy(out=QdBD[64:128, pp * 32 + 16:pp * 32 + 32],
                                                       in_=QdT[64:128, pp * 16:(pp + 1) * 16]),
              sem="vV", deps=[("qdtev", pp), "qdbdz"], key=("qdbd", pp))
        for oc in range(8):
            E("scalar", lambda e, oc=oc: e.activation(out=sqsc[:, 0:16], in_=QdT[:, oc * 16:(oc + 1) * 16],
                                                      func=AF.Square, bias=0.0, scale=1.0),
              sem="sS", deps=[("qdtev", oc), ("qdnm", oc - 1) if oc >= 1 else None], key=("qdsq", oc))
            E("tensor", lambda e: e.matmul(out=psf(2, 16, rows=2), lhsT=bonesb[:], rhs=sqsc[:, 0:16],
                                           start=True, stop=True),
              sem="tM", deps=[("qdsq", oc), "bones", ("qdnr", oc - 1) if oc >= 1 else ("fbtr", 5)],
              key=("qdnm", oc))
            E("vector", lambda e, oc=oc: e.reduce_max(out=nrm[0:2, 48 + oc:49 + oc], in_=psf(2, 16, rows=2), axis=AX),
              sem="vV", deps=[("qdnm", oc)], key=("qdnr", oc))

        load_ffnW("dWk", dW["dWk"], 0, free_ev=("qdtmm", 7))

        def xn1_group(Jg, pref, bankpair, start_dep=None, war=None):
            for tt in range(4):
                ti = Jg * 4 + tt
                xbuf = xt[ti % 2]
                E("sync", lambda e, ti=ti, xbuf=xbuf: e.dma_start(out=xbuf[:], in_=dx1[ti * 128:(ti + 1) * 128, :]),
                  sem="qR%d" % (ti % 2),
                  deps=[*(X1ALL if ti < 2 else []), (pref, "ap", ti - 2) if ti >= 2 else start_dep],
                  key=(pref, "ld", ti))
                E("scalar", lambda e, ti=ti, xbuf=xbuf: e.activation(
                    out=xnsc[:], in_=xbuf[:], func=AF.Identity,
                    bias=st_b[:, ti:ti + 1], scale=st_s[:, ti:ti + 1]),
                  sem="sS", deps=[(pref, "ld", ti), ("l2", "rstd", NT - 1), ("l2bias", NT - 1),
                                  (pref, "tr", ti - 1) if ti >= 1 else start_dep],
                  key=(pref, "ap", ti))
                bank = bankpair[tt % 2]
                for dcn in range(8):
                    E("tensor", lambda e, dcn=dcn, bank=bank: e.transpose(
                        out=psb(bank, dcn * 128, 128), in_=xnsc[:, dcn * 128:(dcn + 1) * 128], identity=ident[:]),
                      sem="tM" if dcn == 7 else None,
                      deps=[(pref, "ap", ti), "ident", (pref, "ev", ti - 2) if ti >= 2 else FD,
                            war(Jg, tt) if (war is not None and tt < 2) else None],
                      key=(pref, "tr", ti) if dcn == 7 else None)
                if pref == "i1":
                    E("scalar", lambda e, bank=bank, tt=tt: e.copy(
                        out=ch(xnTg, 512)[:, :, tt * 128:(tt + 1) * 128],
                        in_=psb(bank, 0, 1024).rearrange("p (c n) -> p c n", n=128)),
                      sem="sS", deps=[(pref, "tr", ti), (pref, "con", Jg - 1) if (Jg >= 1 and tt == 0) else
                                      (start_dep if (Jg == 0 and tt == 0) else None)],
                      key=(pref, "ev", ti))
                else:
                    E("vector", lambda e, bank=bank, tt=tt: e.tensor_copy(
                        out=ch(xnTg, 512)[:, :, tt * 128:(tt + 1) * 128],
                        in_=psb(bank, 0, 1024).rearrange("p (c n) -> p c n", n=128)),
                      sem="vV", deps=[(pref, "tr", ti), (pref, "con", Jg - 1) if (Jg >= 1 and tt == 0) else
                                      (start_dep if (Jg == 0 and tt == 0) else None)],
                      key=(pref, "ev", ti))
            w.mark((pref, "grp", Jg), "sS" if pref == "i1" else "vV")

        for Jg in range(NJ):
            xn1_group(Jg, "f0", (6, 7), start_dep=FD)
            for kc in range(8):
                bank = kc % 2
                for dcn in range(8):
                    E("tensor", lambda e, kc=kc, dcn=dcn, bank=bank: e.matmul(
                        out=psf(bank, 512), lhsT=ffnW[0][:, dcn * KB + kc * 128:dcn * KB + (kc + 1) * 128],
                        rhs=ch(xnTg, 512)[:, dcn, :], start=(dcn == 0), stop=(dcn == 7)),
                      sem="tM" if dcn == 7 else None,
                      deps=[("f0", "grp", Jg), ("dWk", "done"),
                            ("f0ks", Jg, kc - 2) if kc >= 2 else (("f0ks", Jg - 1, 6 + kc) if Jg >= 1 else None)],
                      key=("f0km", Jg, kc) if dcn == 7 else None)
                E("scalar", lambda e, bank=bank: e.activation(out=sqsc[:], in_=psf(bank, 512),
                                                              func=AF.Square, bias=0.0, scale=1.0),
                  sem="sS", deps=[("f0km", Jg, kc), ("f0knm", Jg, kc - 1) if kc >= 1 else
                                  (("f0knm", Jg - 1, 7) if Jg >= 1 else ("qdnm", 7))],
                  key=("f0ks", Jg, kc))
                E("tensor", lambda e: e.matmul(out=psf(2, 512, rows=2), lhsT=bonesb[:], rhs=sqsc[:],
                                               start=True, stop=True),
                  sem="tM", deps=[("f0ks", Jg, kc), "bones",
                                  ("f0knr", Jg, kc - 1) if kc >= 1 else (("f0knr", Jg - 1, 7) if Jg >= 1 else ("qdnr", 7))],
                  key=("f0knm", Jg, kc))
                E("vector", lambda e, kc=kc: e.reduce_max(out=nrm[0:2, 56 + kc:57 + kc], in_=psf(2, 512, rows=2), axis=AX),
                  sem="vV", deps=[("f0knm", Jg, kc), ("f0knu", Jg - 1, kc) if Jg >= 1 else "qnz"],
                  key=("f0knr", Jg, kc))
                E("vector", lambda e, kc=kc: e.tensor_max(out=nrm[0:2, 40 + kc:41 + kc],
                                                          in0=nrm[0:2, 40 + kc:41 + kc], in1=nrm[0:2, 56 + kc:57 + kc]),
                  sem="vV", deps=[("f0knr", Jg, kc), "qnz2"], key=("f0knu", Jg, kc))
            w.mark(("f0", "con", Jg), "tM")

        E("scalar", lambda e: e.activation(out=nrm[0:2, 48:56], in_=nrm[0:2, 48:56], func=AF.Sqrt, bias=0.0, scale=1.0),
          sem="sS", deps=[("qdnr", 7), ("f0ks", NJ - 1, 7)], key="dqsqrt")
        E("scalar", lambda e: e.activation(out=nrm[0:2, 40:48], in_=nrm[0:2, 40:48], func=AF.Sqrt, bias=0.0, scale=1.0),
          sem="sS", deps=[("f0knu", NJ - 1, 7), "dqsqrt"], key="dksqrt")
        qb_dma("gpsimd", lambda e: e.dma_start(out=qnacc[0:1, 96:104], in_=nrm[0:1, 48:56]), ("dnf", 0), extra=["dqsqrt", "dksqrt"])
        qb_dma("gpsimd", lambda e: e.dma_start(out=qnacc[0:1, 104:112], in_=nrm[1:2, 48:56]), ("dnf", 1))
        qb_dma("gpsimd", lambda e: e.dma_start(out=qnacc[0:1, 112:120], in_=nrm[0:1, 40:48]), ("dnf", 2))
        qb_dma("gpsimd", lambda e: e.dma_start(out=qnacc[0:1, 120:128], in_=nrm[1:2, 40:48]), ("dnf", 3))
        for h in range(16):
            qc_ = 96 + (h % 2) * 8 + h // 2
            kc_ = 112 + (h % 2) * 8 + h // 2
            E("vector", lambda e, qc_=qc_, kc_=kc_, h=h: e.tensor_tensor(
                out=qnacc[0:1, 68 + h:69 + h], in0=qnacc[0:1, qc_:qc_ + 1], in1=qnacc[0:1, kc_:kc_ + 1], op=ALU.mult),
              sem="vV", deps=[("dnf", 3)], key=("dmh0", h))
            E("vector", lambda e, h=h: e.tensor_scalar_mul(out=qnacc[0:1, 68 + h:69 + h],
                                                           in0=qnacc[0:1, 68 + h:69 + h], scalar1=-SCALE * 1.02),
              sem="vV", deps=[("dmh0", h)], key=("dmh", h))
        qb_dma("gpsimd", lambda e: e.dma_start(out=mrow2[0:1, :], in_=qnacc[0:1, 68:84].rearrange("a (b c) -> a b c", c=2)[:, :, 0],
                                               allow_slow_non_contiguous=True),
               ("dmr", 0), extra=[("dmh", 15)])
        qb_dma("gpsimd", lambda e: e.dma_start(out=mrow2[1:2, :], in_=qnacc[0:1, 68:84].rearrange("a (b c) -> a b c", c=2)[:, :, 1],
                                               allow_slow_non_contiguous=True),
               ("dmr", 1))
        E("tensor", lambda e: e.matmul(out=psf(4, 8), lhsT=ind2[:], rhs=mrow2[:], start=True, stop=True),
          sem="tM", deps=[("dmr", 1), ("find2",), ("f0", "con", NJ - 1)], key="dbias_mm")
        E("vector", lambda e: e.tensor_copy(out=bias16d_t[:], in_=psf(4, 8)), sem="vV",
          deps=["dbias_mm"], key="dbias")

        load_ffnW("dWv", dW["dWv"], 1)
        E("gpsimd", lambda e: e.memset(VdG[0][:], 1.0), sem="gP", deps=[("ctxevd", NJ - 1)], key=("vdz", 0))
        E("gpsimd", lambda e: e.memset(VdG[1][:], 1.0), sem="gP", key=("vdz", 1))
        E("gpsimd", lambda e: e.memset(l_run[:], 0.0), sem="gP", key="lrz2")
        E("gpsimd", lambda e: e.memset(od_acc[:], 0.0), sem="gP", key="odz2")
        SLOT_BANK = [(4, 0), (5, 0), (4, 64), (5, 64)]
        for Jg in range(NJ):
            xn1_group(Jg, "f1", (2, 3),
                      start_dep=("f0", "ap", NT - 1) if Jg == 0 else None,
                      war=(lambda Jg_, tt: ("pdev", Jg_ - 1, tt % 2) if Jg_ >= 1 else ("f0", "con", NJ - 1)))
            for kc in range(8):
                bank = kc % 2
                for dcn in range(8):
                    E("tensor", lambda e, kc=kc, dcn=dcn, bank=bank: e.matmul(
                        out=psf(bank, 512), lhsT=ffnW[0][:, dcn * KB + kc * 128:dcn * KB + (kc + 1) * 128],
                        rhs=ch(xnTg, 512)[:, dcn, :], start=(dcn == 0), stop=(dcn == 7)),
                      sem="tM" if dcn == 7 else None,
                      deps=[("f1", "grp", Jg), ("dWk", "done"),
                            ("f1ke", Jg, kc - 2) if kc >= 2 else (("vdev", Jg - 1) if Jg >= 1 else ("f0", "con", NJ - 1))],
                      key=("f1km", Jg, kc) if dcn == 7 else None)
                E("scalar", lambda e, kc=kc, bank=bank: e.activation(
                    out=KdTg[:, kc * 512:(kc + 1) * 512], in_=psf(bank, 512),
                    func=AF.Identity, bias=bcols[:, 0 + kc:1 + kc], scale=1.0),
                  sem="sS", deps=[("f1km", Jg, kc), ("fbev", 0),
                                  ("sdmm", Jg - 1, kc) if Jg >= 1 else None],
                  key=("f1ke", Jg, kc))
            for tt in range(4):
                ti = Jg * 4 + tt
                for dcn in range(8):
                    for hf in range(2):
                        E("tensor", lambda e, dcn=dcn, hf=hf, tt=tt: e.matmul(
                            out=psf(hf, 512), lhsT=ch(xnTg, 512)[:, dcn, tt * 128:(tt + 1) * 128],
                            rhs=ffnW[1][:, dcn * KB + hf * 512:dcn * KB + (hf + 1) * 512],
                            start=(dcn == 0), stop=(dcn == 7)),
                          sem="tM" if (dcn == 7 and hf == 1) else None,
                          deps=[("f1", "grp", Jg), ("dWv", "done"), ("f1ke", Jg, 7),
                                ("vde", Jg, tt - 1) if tt >= 1 else None],
                          key=("vdm", Jg, tt) if (dcn == 7 and hf == 1) else None)
                E("vector", lambda e, tt=tt, Jg=Jg: e.tensor_tensor(
                    out=VdG[Jg % 2][:].rearrange("p (t g n) -> p t g n", g=16, n=65)[:, tt, :, 0:64],
                    in0=PS[:, 0:1024].rearrange("p (g n) -> p g n", n=64),
                    in1=dbvb[:].rearrange("p (g n) -> p g n", n=64), op=ALU.add),
                  sem="vV", deps=[("vdm", Jg, tt), ("fdbvb",), ("vdz", Jg % 2),
                                  ("odup", Jg - 2) if Jg >= 2 else None],
                  key=("vde", Jg, tt))
            w.mark(("vdev", Jg), "vV")
            w.mark(("f1", "con", Jg), "tM")
            for pp in range(8):
                bank, b0 = SLOT_BANK[pp % 4]
                E("tensor", lambda e, pp=pp, bank=bank, b0=b0: e.matmul(
                    out=PS[b0:b0 + 32, bank * 512:(bank + 1) * 512],
                    lhsT=QdBD[:, pp * 32:(pp + 1) * 32], rhs=KdTg[:, pp * 512:(pp + 1) * 512],
                    start=True, stop=True, tile_position=(0, b0) if b0 else None),
                  sem="tM", deps=[("f1ke", Jg, pp), ("qdbd", pp),
                                  "dbias" if (Jg == 0 and pp <= 1) else None,
                                  ("sdex", Jg, pp - 2) if pp >= 2 else (("sdex", Jg - 1, pp + 6) if Jg >= 1 else None)],
                  key=("sdmm", Jg, pp))
                E("scalar", lambda e, pp=pp, bank=bank, b0=b0: e.activation(
                    out=PdB[pp % 4][:], in_=PS[b0:b0 + 32, bank * 512:(bank + 1) * 512],
                    func=AF.Exp, bias=bias16d_t[b0:b0 + 32, pp:pp + 1], scale=SCALE),
                  sem="sS", deps=[("sdmm", Jg, pp), "dbias",
                                  ("pdtp", Jg, pp - 4) if pp >= 4 else
                                  (("pdtp", Jg - 1, pp + 4) if Jg >= 1 else None),
                                  ("sdls", Jg, pp - 4) if pp >= 4 else
                                  (("sdls", Jg - 1, pp + 4) if Jg >= 1 else None)],
                  key=("sdex", Jg, pp))
                E("vector", lambda e, pp=pp: e.reduce_sum(out=lpart[:, 0:1], in_=PdB[pp % 4][:], axis=AX),
                  sem="vV", deps=[("sdex", Jg, pp), ("sdlu", Jg, pp - 1) if pp >= 1 else
                                  (("sdlu", Jg - 1, 7) if Jg >= 1 else "lrz2")],
                  key=("sdls", Jg, pp))
                E("vector", lambda e, pp=pp: e.tensor_add(out=l_run[:, pp:pp + 1], in0=l_run[:, pp:pp + 1],
                                                          in1=lpart[:, 0:1]),
                  sem="vV", deps=[("sdls", Jg, pp)], key=("sdlu", Jg, pp))
                for it in range(4):
                    tb = 2 + it // 2
                    E("tensor", lambda e, it=it, pp=pp, tb=tb: e.transpose(
                        out=psb(tb, (it % 2) * 256 + pp * 32, 32),
                        in_=PdB[pp % 4][:, it * 128:(it + 1) * 128], identity=ident[0:32, 0:32]),
                      sem="tM" if it == 3 else None,
                      deps=[("sdex", Jg, pp),
                            ("f1", "ev", Jg * 4 + 2), ("f1", "ev", Jg * 4 + 3),
                            ("pdev", Jg - 1, 0) if (Jg >= 1 and pp == 0) else None,
                            ("pdev", Jg - 1, 1) if (Jg >= 1 and pp == 0) else None],
                      key=("pdtp", Jg, pp) if it == 3 else None)
            for half in range(2):
                E("vector", lambda e, half=half: e.tensor_copy(
                    out=PdTG[:, half * 512:(half + 1) * 512], in_=psb(2 + half, 0, 512)),
                  sem="vV", deps=[("pdtp", Jg, 7)], key=("pdev", Jg, half))
            w.mark(("pdtrd", Jg), "vV")
            for h in range(16):
                obank = 6 + h // 8
                for it in range(4):
                    E("tensor", lambda e, h=h, it=it, Jg=Jg, obank=obank: e.matmul(
                        out=PS[0:16, obank * 512 + (h % 8) * 64:obank * 512 + (h % 8) * 64 + 64],
                        lhsT=PdTG[:, (it // 2) * 512 + (it % 2) * 256 + (h // 2) * 32 + (h % 2) * 16:
                                   (it // 2) * 512 + (it % 2) * 256 + (h // 2) * 32 + (h % 2) * 16 + 16],
                        rhs=VdG[Jg % 2][:].rearrange("p (t g n) -> p t g n", g=16, n=65)[:, it, h, 0:64],
                        start=(it == 0), stop=(it == 3)),
                      sem="tM" if it == 3 else None,
                      deps=[("pdev", Jg, it // 2), ("vde", Jg, it),
                            ("odup", Jg - 1) if (Jg >= 1 and h == 0 and it == 0) else None],
                      key=("pvod", Jg, h) if it == 3 else None)
            E("vector", lambda e: e.tensor_add(out=od_acc[:], in0=od_acc[:], in1=PS[0:16, 6 * 512:8 * 512]),
              sem="vV", deps=[("pvod", Jg, 7), ("pvod", Jg, 15), "odz2",
                              ("odup", Jg - 1) if Jg >= 1 else None],
              key=("odup", Jg))

        E("vector", lambda e: e.reciprocal(out=rl32[:], in_=l_run[:]),
          sem="vV", deps=[("sdlu", NJ - 1, 7)], key="drl")
        qb_dma("gpsimd", lambda e: e.dma_start(out=rlB[:], in_=rl32[16:32, :]), ("drlB",), extra=["drl"])
        for h in range(16):
            pp, s2 = h // 2, h % 2
            E("vector", lambda e, h=h, pp=pp, s2=s2: e.tensor_scalar_mul(
                out=od_acc[:, h * 64:(h + 1) * 64], in0=od_acc[:, h * 64:(h + 1) * 64],
                scalar1=(rl32 if s2 == 0 else rlB)[0:16, pp:pp + 1]),
              sem="vV", deps=[("odup", NJ - 1), "drl", ("drlB",)], key=("odn", h))
        E("vector", lambda e: e.tensor_copy(out=h_sb[0:16, :], in_=od_acc[:]),
          sem="vV", deps=[("odn", 15), "fsqTev", ("qdtev", 7)], key="odcast")
        for oc in range(8):
            E("tensor", lambda e, oc=oc: e.transpose(out=psb(3, oc * 16, 16), in_=h_sb[0:16, oc * 128:(oc + 1) * 128],
                                                     identity=ident[0:16, 0:16]),
              sem="tM" if oc == 7 else None,
              deps=["odcast", "ident", ("pdev", NJ - 1, 0), ("pdev", NJ - 1, 1)],
              key="odtr" if oc == 7 else None)
        E("vector", lambda e: e.tensor_copy(out=odTb[:], in_=psb(3, 0, 128)), sem="vV",
          deps=["odtr"], key="odtev")
        load_ffnW("dWo", dW["dWo"], 0, free_ev=("f1km", NJ - 1, 7))
        for oc in range(8):
            for dcn in range(8):
                E("tensor", lambda e, oc=oc, dcn=dcn: e.matmul(
                    out=psf(0, 16), lhsT=ffnW[0][:, dcn * KB + oc * 128:dcn * KB + (oc + 1) * 128],
                    rhs=odTb[:, dcn * 16:(dcn + 1) * 16], start=(dcn == 0), stop=(dcn == 7)),
                  sem="tM" if dcn == 7 else None,
                  deps=[("dWo", "done"), "odtev", ("odwe", oc - 1) if oc >= 1 else None],
                  key=("odwm", oc) if dcn == 7 else None)
            E("vector", lambda e, oc=oc: e.scalar_tensor_tensor(
                out=oT[:, oc * 16:(oc + 1) * 16], in0=psf(0, 16), scalar=1.0,
                in1=sqTb[:, oc * 16:(oc + 1) * 16], op0=ALU.mult, op1=ALU.add),
              sem="vV", deps=[("odwm", oc), ("fsqb", oc)], key=("odwe", oc))
        for oc in range(8):
            E("tensor", lambda e, oc=oc: e.transpose(out=PS[0:16, 2 * 512 + oc * 128:2 * 512 + (oc + 1) * 128],
                                                     in_=oT[:, oc * 16:(oc + 1) * 16], identity=identf[:]),
              sem="tM" if oc == 7 else None,
              deps=[("odwe", oc), "identf", ("odwm", 7)],
              key="otokr" if oc == 7 else None)
        E("vector", lambda e: e.tensor_copy(out=o_tok[:], in_=PS[0:16, 2 * 512:2 * 512 + 1024]),
          sem="vV", deps=["otokr"], key="otok")
        qb_dma("gpsimd", lambda e: e.dma_start(out=dcc1i.ap(), in_=o_tok[:]), ("cc1w",), extra=["otok"])
        E("gpsimd", lambda e: e.collective_compute(
            "AllGather", ALU.bypass, replica_groups=[list(range(NCORES))],
            ins=[dcc1i.ap().opt()], outs=[dcc1o.ap().opt()]),
          sem="cc", inc=1, deps=[("cc1w",)], key="cc1")
        qb_dma("gpsimd", lambda e: e.dma_start(out=Oall[:], in_=dcc1o.ap()), ("oall",),
               extra=["cc1", ("odwm", 7)])

        if upto == "F":
            for ti in range(NT):
                E("sync", lambda e, ti=ti: e.dma_start(out=xt[ti % 2][:], in_=dx1[ti * 128:(ti + 1) * 128, :]),
                  sem="qF%d" % (ti % 2),
                  deps=[("x1st", NT - 2), ("x1st", NT - 1), ("f1", "ap", NT - 1),
                        ("fdbg", ti - 2) if ti >= 2 else None], key=("fdbgl", ti))
                E("gpsimd", lambda e, ti=ti: e.dma_start(out=dout[ti * 128:(ti + 1) * 128, :], in_=xt[ti % 2][:]),
                  sem="qO" if ti % 2 == 0 else "qO2", deps=[("fdbgl", ti)], key=("fdbg", ti))
            w.barrier_wait("gpsimd", [("fdbg", NT - 2), ("fdbg", NT - 1), ("oall",)])
            w.barrier_wait("gpsimd", [("tap", nm) for nm in dtaps])
            return

        for rep in range(8):
            qb_dma("sync", lambda e, rep=rep: e.dma_start(out=Gt[16 * rep:16 * rep + 16, :], in_=dlng),
                   ("gld", rep), extra=PP_DONE if rep == 0 else [])
        for rep in range(8):
            qb_dma("sync", lambda e, rep=rep: e.dma_start(out=Bt[16 * rep:16 * rep + 16, :], in_=dlnb),
                   ("bld", rep), extra=[("odwm", 7)] if rep == 0 else [])
        ln_stats(100, Oall, "lh", lnst[:, 8:9], lnst[:, 9:10], extra0=[("oall",), ("l2", "rstd", NT - 1)])
        E("scalar", lambda e: e.activation(out=sqf32[:], in_=Oall[:], func=AF.Identity,
                                           bias=lnst[:, 9:10], scale=lnst[:, 8:9]),
          sem="sS", deps=[("lhbias", 100)], key="hap")
        E("vector", lambda e: e.tensor_mul(out=sqf32[:], in0=sqf32[:], in1=Gt[:]),
          sem="vV", deps=["hap", ("gld", 7)], key="hg")
        E("vector", lambda e: e.tensor_tensor(out=h_sb[:], in0=sqf32[:], in1=Bt[:], op=ALU.add),
          sem="vV", deps=["hg", ("bld", 7), "odcast"], key="hb")
        qb_dma("sync", lambda e: e.dma_start(out=esel_sb[:], in_=desel), ("eselld",))
        E("vector", lambda e: e.tensor_copy(out=esel_bf[:], in_=esel_sb[:]),
          sem="vV", deps=[("eselld",)], key="eselbf")
        for half in range(2):
            E("tensor", lambda e, half=half: e.matmul(
                out=PS[0:16, 5 * 512 + half * 512:5 * 512 + (half + 1) * 512],
                lhsT=esel_bf[:], rhs=h_sb[:, half * 512:(half + 1) * 512], start=True, stop=True),
              sem="tM" if half == 1 else None,
              deps=["hb", "eselbf", "otokr", ("f0", "tr", NT - 1)],
              key="hselmm" if half == 1 else None)
        E("vector", lambda e: e.tensor_copy(out=hsel[:], in_=PS[0:16, 5 * 512:5 * 512 + 1024]),
          sem="vV", deps=["hselmm"], key="hselev")
        for dcn in range(8):
            E("tensor", lambda e, dcn=dcn: e.transpose(out=psb(6, dcn * 16, 16),
                                                       in_=hsel[0:16, dcn * 128:(dcn + 1) * 128],
                                                       identity=ident[0:16, 0:16]),
              sem="tM" if dcn == 7 else None,
              deps=["hselev", "ident", "hselmm"],
              key="htr" if dcn == 7 else None)
        E("vector", lambda e: e.tensor_copy(out=hTe[:], in_=psb(6, 0, 128)), sem="vV",
          deps=["htr"], key="hTev")
        for i, (dsrc, dst, wdt) in enumerate(((deb1, b1T, eb1c), (debg, bgT, ebgc), (deb2, b2T, eb2c))):
            nr = 32 if i < 2 else 8
            for le in range(2):
                qb_dma("sync", lambda e, dsrc=dsrc, le=le, nr=nr, wdt=wdt: e.dma_start(
                    out=wdt[0:nr, le * 128:(le + 1) * 128],
                    in_=dsrc[le].rearrange("(a b) -> a b", b=128)),
                    ("ebld", i, le))
                E("tensor", lambda e, le=le, nr=nr, wdt=wdt: e.transpose(
                    out=PS[0:128, 3 * 512 + le * 32:3 * 512 + le * 32 + 32][:, 0:nr],
                    in_=wdt[0:nr, le * 128:(le + 1) * 128],
                    identity=(identf[0:32, 0:32] if nr == 32 else identf[0:8, 0:8])),
                  sem="tM", deps=[("ebld", i, le), "identf",
                                  "htr" if (i == 0 and le == 0) else (("ebev", i - 1, 1) if (i >= 1 and le == 0) else None),
                                  ("ebev", i, le - 1) if le >= 1 else None],
                  key=("ebtr", i, le))
                E("vector", lambda e, i=i, le=le, nr=nr, dst=dst: e.tensor_copy(
                    out=dst[:, le * nr:(le + 1) * nr],
                    in_=PS[0:128, 3 * 512 + le * 32:3 * 512 + le * 32 + nr]),
                  sem="vV", deps=[("ebtr", i, le)], key=("ebev", i, le))
        for le in range(2):
            for q4 in range(4):
                for mat, (dsrc, wbuf, bT, psbank, osb) in enumerate(
                        ((deW1, W1bfh, b1T, (0 if q4 % 2 == 0 else 3), u_sb),
                         (deWg, Wgbfh, bgT, (1 if q4 % 2 == 0 else 7), g_sb))):
                    tagq = ("ew", le, q4, mat)
                    if q4 >= 1:
                        lddeps = [("ew", le, q4 - 1, mat, "mmz")]
                    elif le >= 1:
                        lddeps = [("w2", 0, 2 + mat, "mmz"), ("ew", 0, 3, mat, "mmz")]
                    elif mat == 0:
                        lddeps = [("odwm", 7), ("Jwo", NJ - 1)] + PP_DONE
                    else:
                        lddeps = [("f1", "ap", NT - 1), ("x1st", NT - 1), ("x1st", NT - 2),
                                  ("vdm", NJ - 1, 3)] + PP_DONE
                    E("sync", lambda e, g=(mat * 2 + le) * 4 + q4, wbuf=wbuf: e.dma_start(
                        out=wbuf[:].rearrange("p (a n) -> p a n", n=KB), in_=dewb1[g]),
                      sem="qW%d" % mat, deps=lddeps, key=(*tagq, "cast", 7))
                    for fc8 in range(8):
                        fc = q4 * 8 + fc8
                        for dcn in range(8):
                            E("tensor", lambda e, fc8=fc8, fc=fc, dcn=dcn, wbuf=wbuf, psbank=psbank, le=le: e.matmul(
                                out=PS[:, psbank * 512 + fc * 8:psbank * 512 + fc * 8 + 8],
                                lhsT=wbuf[:, dcn * KB + fc8 * 128:dcn * KB + (fc8 + 1) * 128],
                                rhs=hTe[:, dcn * 16 + le * 8:dcn * 16 + le * 8 + 8],
                                start=(dcn == 0), stop=(dcn == 7)),
                              sem="tM" if dcn == 7 else None,
                              deps=[(*tagq, "cast", 7), "hTev",
                                    ("exsil", le, q4 - 2, mat) if (q4 >= 2 and fc8 == 0) else None,
                                    (("exsil", le - 1, 2 + q4, mat) if (le >= 1 and q4 < 2 and fc8 == 0) else None),
                                    ("ebev", 2, 1) if (le == 0 and q4 < 2 and fc8 == 0) else None],
                              key=(*tagq, "mm", fc8) if dcn == 7 else None)
                    w.mark((*tagq, "mmz"), "tM")
                    for fc8 in range(8):
                        fc = q4 * 8 + fc8
                        E("scalar", lambda e, fc=fc, fc8=fc8, psbank=psbank, bT=bT, le=le, osb=osb: e.activation(
                            out=osb[:, fc * 8:fc * 8 + 8], in_=PS[:, psbank * 512 + fc * 8:psbank * 512 + fc * 8 + 8],
                            func=AF.Silu, bias=bT[:, le * 32 + fc:le * 32 + fc + 1], scale=1.0),
                          sem="sS", deps=[(*tagq, "mm", 7), ("ebev", mat, 1)],
                          key=("exsil", le, q4, mat) if fc8 == 7 else None)
            for q4 in range(4):
                E("vector", lambda e, q4=q4: e.tensor_mul(out=ug[:, q4 * 64:(q4 + 1) * 64],
                                                          in0=u_sb[:, q4 * 64:(q4 + 1) * 64],
                                                          in1=g_sb[:, q4 * 64:(q4 + 1) * 64]),
                  sem="vV", deps=[("exsil", le, q4, 0), ("exsil", le, q4, 1)], key=("ugm", le, q4))
            for wv2 in range(4):
                tagw = ("w2", le, wv2)
                gwv = le * 4 + wv2
                wbuf2 = W2bfw if gwv % 2 == 0 else ffnW[0]
                if gwv >= 2:
                    ple, pwv2 = divmod(gwv - 2, 4)
                    lddeps = [("w2", ple, pwv2, "mmz")]
                    if gwv in (4, 5):
                        lddeps.append(("ew", 1, 3, gwv - 4, "mmz"))
                elif gwv == 0:
                    lddeps = [("vdm", NJ - 1, 3), ("ew", 0, 3, 0, "mmz")]
                else:
                    lddeps = [("odwm", 7), "hap", "hg", "hb", ("ew", 0, 3, 1, "mmz")]
                E("sync", lambda e, g2=gwv, wbuf2=wbuf2: e.dma_start(
                    out=wbuf2[:].rearrange("p (a n) -> p a n", n=KB), in_=dewb2[g2]),
                  sem="qW%d" % (gwv % 2), deps=lddeps, key=(*tagw, "cast", 7))
                yb = 2 if wv2 % 2 == 0 else 5
                for dq8 in range(8):
                    for fcw in range(8):
                        E("tensor", lambda e, dq8=dq8, fcw=fcw, wv2=wv2, le=le, yb=yb, wbuf2=wbuf2: e.matmul(
                            out=PS[:, yb * 512 + dq8 * 8:yb * 512 + dq8 * 8 + 8],
                            lhsT=wbuf2[:, fcw * KB + dq8 * 128:fcw * KB + (dq8 + 1) * 128],
                            rhs=ug[:, (wv2 * 8 + fcw) * 8:(wv2 * 8 + fcw) * 8 + 8],
                            start=(fcw == 0), stop=(fcw == 7)),
                          sem="tM" if fcw == 7 else None,
                          deps=[(*tagw, "cast", 7), ("ugm", le, wv2) if dq8 == 0 else None,
                                ("ugm", le, 3) if dq8 == 0 else None,
                                ("yac", le, wv2 - 2) if (wv2 >= 2 and dq8 == 0) else None,
                                (("yac", le - 1, 2 + wv2) if (le >= 1 and wv2 < 2 and dq8 == 0) else None),
                                (("ytr", le - 1) if (le >= 1 and wv2 == 0 and dq8 == 0) else None),
                                ("selmm" if False else None)],
                          key=(*tagw, "mm", dq8) if fcw == 7 else None)
                w.mark((*tagw, "mmz"), "tM")
                if wv2 == 0:
                    E("vector", lambda e, yb=yb: e.tensor_copy(out=y_acc[:], in_=PS[:, yb * 512:yb * 512 + 64]),
                      sem="vV", deps=[(*tagw, "mm", 7)], key=("yac", le, 0))
                else:
                    E("vector", lambda e, yb=yb: e.tensor_add(out=y_acc[:], in0=y_acc[:], in1=PS[:, yb * 512:yb * 512 + 64]),
                      sem="vV", deps=[(*tagw, "mm", 7), ("yac", le, wv2 - 1)], key=("yac", le, wv2))
            for oc in range(8):
                E("vector", lambda e, oc=oc, le=le: e.tensor_scalar_add(
                    out=y_acc[:, oc * 8:(oc + 1) * 8], in0=y_acc[:, oc * 8:(oc + 1) * 8],
                    scalar1=b2T[:, le * 8 + oc:le * 8 + oc + 1]),
                  sem="vV", deps=[("yac", le, 3), ("ebev", 2, 1)], key=("yb2", le, oc))
            for oc in range(8):
                E("tensor", lambda e, oc=oc, le=le: e.transpose(
                    out=PS[0:8, 4 * 512 + oc * 128:4 * 512 + (oc + 1) * 128],
                    in_=y_acc[:, oc * 8:(oc + 1) * 8], identity=identf[:]),
                  sem="tM" if oc == 7 else None,
                  deps=[("yb2", le, oc), "identf", ("w2", le, 3, "mmz"),
                        ("ytev", le - 1) if (le >= 1 and oc == 0) else None],
                  key=("ytr", le) if oc == 7 else None)
            E("vector", lambda e, le=le: e.tensor_copy(out=Yle[le][0:8, :],
                                                       in_=PS[0:8, 4 * 512:4 * 512 + 1024]),
              sem="vV", deps=[("ytr", le)], key=("ytev", le))
        qb_dma("gpsimd", lambda e: e.dma_start(out=dcc2i.ap()[0:8, :], in_=Yle[0][0:8, :]), ("cc2w0",),
               extra=[("ytev", 0), ("ytev", 1)])
        qb_dma("gpsimd", lambda e: e.dma_start(out=dcc2i.ap()[8:16, :], in_=Yle[1][0:8, :]), ("cc2w",))
        E("gpsimd", lambda e: e.collective_compute(
            "AllGather", ALU.bypass, replica_groups=[list(range(NCORES))],
            ins=[dcc2i.ap().opt()], outs=[dcc2o.ap().opt()]),
          sem="cc", inc=1, deps=[("cc2w",)], key="cc2")
        qb_dma("gpsimd", lambda e: e.dma_start(out=Yall[:], in_=dcc2o.ap()), ("yall",), extra=["cc2"])
        qb_dma("sync", lambda e: e.dma_start(out=bsel_sb[:], in_=dbsel), ("bselld",))
        qb_dma("sync", lambda e: e.dma_start(out=cbob[:], in_=dB["cbo"][0:1, :].broadcast_to([128, KB])),
               ("cbobld",), extra=[("sdex", NJ - 1, 7)])
        for half in range(2):
            E("tensor", lambda e, half=half: e.matmul(
                out=PS[0:16, 4 * 512 + half * 512:4 * 512 + (half + 1) * 512],
                lhsT=bsel_sb[:], rhs=Yall[:, half * 512:(half + 1) * 512], start=True, stop=True),
              sem="tM" if half == 1 else None,
              deps=[("yall",), ("bselld",), ("ytr", 1), ("ytev", 1)],
              key="selmm" if half == 1 else None)
        E("vector", lambda e: e.tensor_add(out=o2_tok[:], in0=o_tok[:], in1=PS[0:16, 4 * 512:4 * 512 + 1024]),
          sem="vV", deps=["selmm", "otok"], key="o2t")

        if upto == "H":
            tap("o2tok", lambda: o2_tok[:], ["o2t"])
            for ti in range(NT):
                E("sync", lambda e, ti=ti: e.dma_start(out=xt[ti % 2][:], in_=dx1[ti * 128:(ti + 1) * 128, :]),
                  sem="qF%d" % (ti % 2),
                  deps=[("x1st", NT - 2), ("x1st", NT - 1), ("ew", 1, 3, 1, "mmz"), "o2t",
                        ("fdbg", ti - 2) if ti >= 2 else None], key=("fdbgl", ti))
                E("gpsimd", lambda e, ti=ti: e.dma_start(out=dout[ti * 128:(ti + 1) * 128, :], in_=xt[ti % 2][:]),
                  sem="qO" if ti % 2 == 0 else "qO2", deps=[("fdbgl", ti)], key=("fdbg", ti))
            w.barrier_wait("gpsimd", [("fdbg", NT - 2), ("fdbg", NT - 1), ("oall",), ("yall",)])
            w.barrier_wait("gpsimd", [("tap", nm) for nm in dtaps])
            return

        for oc in range(8):
            E("tensor", lambda e, oc=oc: e.transpose(out=PS[0:128, 5 * 512 + oc * 16:5 * 512 + (oc + 1) * 16],
                                                     in_=o2_tok[0:16, oc * 128:(oc + 1) * 128],
                                                     identity=identf[0:16, 0:16]),
              sem="tM" if oc == 7 else None, deps=["o2t", "identf", "selmm"],
              key="o2tr" if oc == 7 else None)
        E("vector", lambda e: e.tensor_copy(out=o2T[:], in_=PS[0:128, 5 * 512:5 * 512 + 128]),
          sem="vV", deps=["o2tr"], key="o2Tev")
        E("vector", lambda e: e.tensor_copy(out=o2Tb[:], in_=o2T[:]), sem="vV",
          deps=["o2Tev"], key="o2Tb")
        load_ffnW("cWk", dW["cWk"], 1, free_ev=("w2", 1, 3, "mmz"))
        for oc in range(8):
            kb_ = oc % 2
            for dcn in range(8):
                E("tensor", lambda e, oc=oc, dcn=dcn, kb_=kb_: e.matmul(
                    out=psf(kb_, 16), lhsT=ffnW[1][:, dcn * KB + oc * 128:dcn * KB + (oc + 1) * 128],
                    rhs=o2Tb[:, dcn * 16:(dcn + 1) * 16], start=(dcn == 0), stop=(dcn == 7)),
                  sem="tM" if dcn == 7 else None,
                  deps=[("cWk", "done"), "o2Tb", ("kcev", oc - 2) if oc >= 2 else None],
                  key=("kcmm", oc) if dcn == 7 else None)
            E("scalar", lambda e, oc=oc, kb_=kb_: e.activation(out=KcT[:, oc * 16:(oc + 1) * 16], in_=psf(kb_, 16),
                                                      func=AF.Identity, bias=bcols[:, 32 + oc:33 + oc], scale=1.0),
              sem="sS", deps=[("kcmm", oc), ("fbev", 4)], key=("kcev", oc))
        load_ffnW("cWv", dW["cWv"], 1, free_ev=("kcmm", 7))
        for oc in range(8):
            kb_ = oc % 2
            for dcn in range(8):
                E("tensor", lambda e, oc=oc, dcn=dcn, kb_=kb_: e.matmul(
                    out=psf(kb_, 16), lhsT=ffnW[1][:, dcn * KB + oc * 128:dcn * KB + (oc + 1) * 128],
                    rhs=o2Tb[:, dcn * 16:(dcn + 1) * 16], start=(dcn == 0), stop=(dcn == 7)),
                  sem="tM" if dcn == 7 else None,
                  deps=[("cWv", "done"), "o2Tb", ("vcev", oc - 2) if oc >= 2 else ("kcev", 7)],
                  key=("vcmm", oc) if dcn == 7 else None)
            E("vector", lambda e, oc=oc, kb_=kb_: e.tensor_scalar_add(out=VcT[:, oc * 16:(oc + 1) * 16], in0=psf(kb_, 16),
                                                             scalar1=bcols[:, 40 + oc:41 + oc]),
              sem="vV", deps=[("vcmm", oc), ("fbev", 5)], key=("vcev", oc))
        for oc in range(8):
            E("tensor", lambda e, oc=oc: e.transpose(out=PS[0:16, 5 * 512 + oc * 128:5 * 512 + (oc + 1) * 128],
                                                     in_=VcT[:, oc * 16:(oc + 1) * 16], identity=identf[:]),
              sem="tM" if oc == 7 else None, deps=[("vcev", oc), "identf", ("vcmm", 7), "o2tr"],
              key="vctr" if oc == 7 else None)
        E("vector", lambda e: e.tensor_copy(out=Vctok[:], in_=PS[0:16, 5 * 512:5 * 512 + 1024]),
          sem="vV", deps=["vctr", "fsqcast"], key="vctok")
        E("gpsimd", lambda e: e.memset(VcOnes[:], 1.0), sem="gP", deps=[("cc1w",)], key="vcoz")
        for h in range(16):
            E("vector", lambda e, h=h: e.tensor_copy(out=VcOnes[0:16, h * 65:h * 65 + 64],
                                                     in_=Vctok[0:16, h * 64:(h + 1) * 64]),
              sem="vV", deps=["vctok", "vcoz"], key=("vco", h))
        load_ffnW("cWq", dW["cWq"], 0, free_ev="o2t")
        load_ffnW("cWo", dW["cWo"], 1, free_ev=("vcmm", 7))
        if upto == "I0":
            for ti in range(NT):
                E("sync", lambda e, ti=ti: e.dma_start(out=xt[ti % 2][:], in_=dx1[ti * 128:(ti + 1) * 128, :]),
                  sem="qF%d" % (ti % 2),
                  deps=[("x1st", NT - 2), ("x1st", NT - 1), ("vco", 15), ("cWo", "done"), ("cWq", "done"),
                        ("fdbg", ti - 2) if ti >= 2 else None], key=("fdbgl", ti))
                E("gpsimd", lambda e, ti=ti: e.dma_start(out=dout[ti * 128:(ti + 1) * 128, :], in_=xt[ti % 2][:]),
                  sem="qO" if ti % 2 == 0 else "qO2", deps=[("fdbgl", ti)], key=("fdbg", ti))
            w.barrier_wait("gpsimd", [("fdbg", NT - 2), ("fdbg", NT - 1), ("oall",), ("yall",)])
            w.barrier_wait("gpsimd", [("tap", nm) for nm in dtaps])
            return
        E("gpsimd", lambda e: e.memset(nrm[0:2, 16:32], 0.0), sem="gP", deps=[("f0knu", NJ - 1, 7), "dksqrt"], key="kcnz")
        for c2 in range(8):
            E("scalar", lambda e, c2=c2: e.activation(out=sqsc[:, 0:16], in_=KcT[:, c2 * 16:(c2 + 1) * 16],
                                                      func=AF.Square, bias=0.0, scale=1.0),
              sem="sS", deps=[("kcev", c2), ("kcnm", c2 - 1) if c2 >= 1 else None], key=("kcsq", c2))
            E("tensor", lambda e: e.matmul(out=psf(6, 16, rows=2), lhsT=bonesb[:], rhs=sqsc[:, 0:16],
                                           start=True, stop=True),
              sem="tM", deps=[("kcsq", c2), "bones", ("kcnr", c2 - 1) if c2 >= 1 else "vctok"],
              key=("kcnm", c2))
            E("vector", lambda e, c2=c2: e.reduce_max(out=nrm[0:2, 16 + c2:17 + c2], in_=psf(6, 16, rows=2), axis=AX),
              sem="vV", deps=[("kcnm", c2), "kcnz"], key=("kcnr", c2))
        E("scalar", lambda e: e.activation(out=nrm[0:2, 16:24], in_=nrm[0:2, 16:24], func=AF.Sqrt, bias=0.0, scale=1.0),
          sem="sS", deps=[("kcnr", 7)], key="kcsqrt")
        qb_dma("gpsimd", lambda e: e.dma_start(out=qnacc[0:1, 96:104], in_=nrm[0:1, 16:24]), ("kcf", 0), extra=["kcsqrt"])
        qb_dma("gpsimd", lambda e: e.dma_start(out=qnacc[0:1, 104:112], in_=nrm[1:2, 16:24]), ("kcf", 1))
        E("gpsimd", lambda e: e.memset(KcBD[:], 0.0), sem="gP", deps=[("kcev", 7)], key="kcbdz")
        for c2 in range(8):
            E("vector", lambda e, c2=c2: e.tensor_copy(out=KcBD[0:64, c2 * 80:c2 * 80 + 16],
                                                       in_=KcT[0:64, c2 * 16:(c2 + 1) * 16]),
              sem="vV", deps=[("kcev", c2), "kcbdz"])
            E("vector", lambda e, c2=c2: e.tensor_copy(out=KcBD[64:128, c2 * 80 + 64:c2 * 80 + 80],
                                                       in_=KcT[64:128, c2 * 16:(c2 + 1) * 16]),
              sem="vV", deps=[("kcev", c2), "kcbdz"], key=("kcbd", c2))
        for Jg in range(NJ):
            xn1_group(Jg, "i1", (0, 1),
                      start_dep=("ew", 1, 3, 1, "mmz") if Jg == 0 else None,
                      war=(lambda Jg_, tt: ("qcev", Jg_ - 1, 6 + tt % 2) if Jg_ >= 1 else "hTev"))
            for oc in range(8):
                bank = oc % 2
                for dcn in range(8):
                    E("tensor", lambda e, oc=oc, dcn=dcn, bank=bank: e.matmul(
                        out=psf(bank, 512), lhsT=ffnW[0][:, dcn * KB + oc * 128:dcn * KB + (oc + 1) * 128],
                        rhs=ch(xnTg, 512)[:, dcn, :], start=(dcn == 0), stop=(dcn == 7)),
                      sem="tM" if dcn == 7 else None,
                      deps=[("i1", "grp", Jg), ("cWq", "done"),
                            ("qcev", Jg, oc - 2) if oc >= 2 else
                            (("i1att", Jg - 1) if Jg >= 1 else ("vcev", 7))],
                      key=("qcmm", Jg, oc) if dcn == 7 else None)
                E("scalar", lambda e, oc=oc, bank=bank: e.activation(
                    out=QcT[:, oc * 512:(oc + 1) * 512], in_=psf(bank, 512),
                    func=AF.Identity, bias=bcols[:, 24 + oc:25 + oc], scale=1.0),
                  sem="sS", deps=[("qcmm", Jg, oc), ("fbev", 3),
                                  ("qcnm", Jg, oc - 1) if oc >= 1 else (("scmm", Jg - 1, 7) if Jg >= 1 else None)],
                  key=("qcev", Jg, oc))
                E("scalar", lambda e, oc=oc: e.activation(out=sqsc[:], in_=QcT[:, oc * 512:(oc + 1) * 512],
                                                          func=AF.Square, bias=0.0, scale=1.0),
                  sem="sS", deps=[("qcev", Jg, oc)], key=("qcsq", Jg, oc))
                E("tensor", lambda e: e.matmul(out=psf(7, 512, rows=2), lhsT=bonesb[:], rhs=sqsc[:],
                                               start=True, stop=True),
                  sem="tM", deps=[("qcsq", Jg, oc), "bones",
                                  ("qcnr", Jg, oc - 1) if oc >= 1 else
                                  (("cexp", Jg - 1, 7) if Jg >= 1 else ("kcnm", 7))],
                  key=("qcnm", Jg, oc))
                E("vector", lambda e, oc=oc: e.reduce_max(out=nrm[0:2, 24 + oc:25 + oc], in_=psf(7, 512, rows=2), axis=AX),
                  sem="vV", deps=[("qcnm", Jg, oc)], key=("qcnr", Jg, oc))
            w.mark(("i1", "con", Jg), "tM")
            E("scalar", lambda e: e.activation(out=nrm[0:2, 24:32], in_=nrm[0:2, 24:32], func=AF.Sqrt, bias=0.0, scale=1.0),
              sem="sS", deps=[("qcnr", Jg, 7)], key=("qcsqrt", Jg))
            qb_dma("gpsimd", lambda e: e.dma_start(out=qnacc[0:1, 112:120], in_=nrm[0:1, 24:32]),
                   ("qcf", Jg, 0), extra=[("qcsqrt", Jg)])
            qb_dma("gpsimd", lambda e: e.dma_start(out=qnacc[0:1, 120:128], in_=nrm[1:2, 24:32]), ("qcf", Jg, 1))
            for h in range(16):
                qc_ = 112 + (h % 2) * 8 + h // 2
                kc_ = 96 + (h % 2) * 8 + h // 2
                E("vector", lambda e, qc_=qc_, kc_=kc_, h=h: e.tensor_tensor(
                    out=qnacc[0:1, 68 + h:69 + h], in0=qnacc[0:1, qc_:qc_ + 1], in1=qnacc[0:1, kc_:kc_ + 1], op=ALU.mult),
                  sem="vV", deps=[("qcf", Jg, 1), ("kcf", 1), ("cbias", Jg - 1) if Jg >= 1 else None],
                  key=("cmh0", Jg, h))
                E("vector", lambda e, h=h: e.tensor_scalar_mul(out=qnacc[0:1, 68 + h:69 + h],
                                                               in0=qnacc[0:1, 68 + h:69 + h], scalar1=-SCALE * 1.02),
                  sem="vV", deps=[("cmh0", Jg, h)], key=("cmh", Jg, h))
            qb_dma("gpsimd", lambda e: e.dma_start(out=mrow2[0:1, :],
                                                   in_=qnacc[0:1, 68:84].rearrange("a (b c) -> a b c", c=2)[:, :, 0],
                                                   allow_slow_non_contiguous=True),
                   ("cmr", Jg, 0), extra=[("cmh", Jg, 15)])
            qb_dma("gpsimd", lambda e: e.dma_start(out=mrow2[1:2, :],
                                                   in_=qnacc[0:1, 68:84].rearrange("a (b c) -> a b c", c=2)[:, :, 1],
                                                   allow_slow_non_contiguous=True),
                   ("cmr", Jg, 1))
            E("tensor", lambda e: e.matmul(out=psf(7, 8), lhsT=ind2[:], rhs=mrow2[:], start=True, stop=True),
              sem="tM", deps=[("cmr", Jg, 1), ("find2",), ("qcnm", Jg, 7)], key=("cbmm", Jg))
            E("vector", lambda e: e.tensor_copy(out=bias16c[:, 0:8], in_=psf(7, 8)), sem="vV",
              deps=[("cbmm", Jg)], key=("cbias", Jg))
            for pc2 in range(8):
                sb_ = 6 if pc2 % 2 == 0 else 7
                ob = 2 if pc2 % 2 == 0 else 4
                E("tensor", lambda e, pc2=pc2, sb_=sb_: e.matmul(
                    out=psf(sb_, 512, rows=80), lhsT=KcBD[:, pc2 * 80:(pc2 + 1) * 80],
                    rhs=QcT[:, pc2 * 512:(pc2 + 1) * 512], start=True, stop=True),
                  sem="tM", deps=[("qcev", Jg, pc2), ("kcbd", pc2), ("cbmm", Jg),
                                  ("cexp", Jg, pc2 - 2) if pc2 >= 2 else
                                  (("cexp", Jg - 1, pc2 + 6) if Jg >= 1 else None)],
                  key=("scmm", Jg, pc2))
                for s2 in range(2):
                    slot = (2 * pc2 + s2) % 4
                    E("scalar", lambda e, s2=s2, sb_=sb_, slot=slot, pc2=pc2: e.activation(
                        out=PcB[slot][:], in_=PS[64 * s2:64 * s2 + 16, sb_ * 512:(sb_ + 1) * 512],
                        func=AF.Exp, bias=bias16c[64 * s2:64 * s2 + 16, pc2:pc2 + 1], scale=SCALE),
                      sem="sS", deps=[("scmm", Jg, pc2), ("cbias", Jg),
                                      ("cpv", Jg, pc2 - 2) if pc2 >= 2 else
                                      (("cpv", Jg - 1, pc2 + 6) if Jg >= 1 else None)],
                      key=("cexp", Jg, pc2) if s2 == 1 else None)
                for s2 in range(2):
                    h = 2 * pc2 + s2
                    slot = (2 * pc2 + s2) % 4
                    E("tensor", lambda e, s2=s2, h=h, slot=slot, ob=ob: e.matmul(
                        out=psf(ob + s2, 512, rows=65), lhsT=VcOnes[0:16, h * 65:h * 65 + 65],
                        rhs=PcB[slot][:], start=True, stop=True),
                      sem="tM" if s2 == 1 else None,
                      deps=[("cexp", Jg, pc2), ("vco", 15),
                            ("cctxp", Jg, pc2 - 2) if pc2 >= 2 else
                            (("cctxp", Jg - 1, 7) if Jg >= 1 else None),
                            ("outev", Jg * 4 - 1) if (pc2 < 2 and Jg >= 1) else None],
                      key=("cpv", Jg, pc2) if s2 == 1 else None)
                lrow = lstage if pc2 % 2 == 0 else sqf32
                if pc2 >= 2:
                    clwar = ("crl", Jg, pc2 - 2, 1)
                elif Jg >= 1:
                    clwar = ("crl", Jg - 1, pc2 + 6, 1)
                else:
                    clwar = ("drl", NJ - 1, 7, 1) if pc2 == 0 else ("lhbias", 100)
                E("vector", lambda e, ob=ob, lrow=lrow: e.reciprocal(
                    out=lrow[64:65, 0:1024], in_=PS[64:65, ob * 512:(ob + 2) * 512]),
                  sem="vV", deps=[("cpv", Jg, pc2), clwar], key=("clrec", Jg, pc2))
                for s2 in range(2):
                    E("tensor", lambda e, s2=s2, ob=ob, lrow=lrow: e.matmul(
                        out=PS[64:128, (ob + s2) * 512:(ob + s2 + 1) * 512],
                        lhsT=ones64[64:65, 0:64],
                        rhs=lrow[64:65, s2 * 512:(s2 + 1) * 512],
                        start=True, stop=True, tile_position=(64, 64)),
                      sem="tM", deps=[("clrec", Jg, pc2), "ones64"],
                      key=("crl", Jg, pc2, s2))
                for s2 in range(2):
                    hc = (2 * pc2 + s2) % 4
                    E("scalar", lambda e, ob=ob, s2=s2, hc=hc: e.copy(
                        out=rlb[0:64, hc * 512:(hc + 1) * 512],
                        in_=PS[64:128, (ob + s2) * 512:(ob + s2 + 1) * 512]),
                      sem="sS", deps=[("crl", Jg, pc2, s2)], key=("crle", Jg, pc2, s2))
                cT = ctxcT[Jg % 2]
                for s2 in range(2):
                    h = 2 * pc2 + s2
                    hc = (2 * pc2 + s2) % 4
                    E("vector", lambda e, h=h, ob=ob, s2=s2, hc=hc, cT=cT: e.tensor_tensor(
                        out=ch(cT, 512)[64 * (h % 2):64 * (h % 2) + 64, h // 2, :],
                        in0=PS[0:64, (ob + s2) * 512:(ob + s2 + 1) * 512],
                        in1=rlb[:, hc * 512:(hc + 1) * 512], op=ALU.mult),
                      sem="vV",
                      deps=[("crle", Jg, pc2, s2),
                            ("a2mm", Jg * 4 - 1) if (Jg >= 1 and pc2 == 0 and s2 == 0) else None],
                      key=("cctxp", Jg, pc2) if s2 == 1 else None)
            w.mark(("cctxd", Jg), "vV")
            cT = ctxcT[Jg % 2]
            for tt in range(4):
                ti = Jg * 4 + tt
                ab0 = 2 if tt % 2 == 0 else 4
                for dcn in range(8):
                    for hf in range(2):
                        E("tensor", lambda e, dcn=dcn, hf=hf, tt=tt, ab0=ab0, cT=cT: e.matmul(
                            out=psf(ab0 + hf, 512),
                            lhsT=ch(cT, 512)[:, dcn, tt * 128:(tt + 1) * 128],
                            rhs=ffnW[1][:, dcn * KB + hf * 512:dcn * KB + (hf + 1) * 512],
                            start=(dcn == 0), stop=(dcn == 7)),
                          sem="tM" if (dcn == 7 and hf == 1) else None,
                          deps=[("cctxd", Jg), ("cWo", "done"),
                                ("outev", ti - 2) if tt >= 2 else (("outev", Jg * 4 - 1) if (tt == 0 and Jg >= 1) else None)],
                          key=("a2mm", ti) if (dcn == 7 and hf == 1) else None)
                xe = xt[2 + ti % 2]
                E("sync", lambda e, ti=ti, xe=xe: e.dma_start(out=xe[:], in_=dx1[ti * 128:(ti + 1) * 128, :]),
                  sem="qE%d" % (ti % 2),
                  deps=[*(X1ALL if ti < 2 else []), ("fout", ti - 2) if ti >= 2 else ("ew", 1, 3, 1, "mmz")],
                  key=("oxld", ti))
                E("gpsimd", lambda e, xe=xe: e.tensor_tensor(out=xe[:], in0=xe[:], in1=cbob[:], op=ALU.add),
                  sem="gP", deps=[("oxld", ti), ("cbobld",)], key=("oadd1", ti))
                E("vector", lambda e, xe=xe, ab0=ab0: e.tensor_add(out=xe[:, 0:512], in0=xe[:, 0:512], in1=psf(ab0, 512)),
                  sem="vV", deps=[("a2mm", ti), ("oadd1", ti)], key=("oadd2", ti))
                E("vector", lambda e, xe=xe, ab0=ab0: e.tensor_add(out=xe[:, 512:1024], in0=xe[:, 512:1024],
                                                                   in1=psf(ab0 + 1, 512)),
                  sem="vV", deps=[("oadd2", ti)], key=("outev", ti))
                E("gpsimd", lambda e, ti=ti, xe=xe: e.dma_start(out=dout[ti * 128:(ti + 1) * 128, :], in_=xe[:]),
                  sem="qO" if ti % 2 == 0 else "qO2", deps=[("outev", ti)], key=("fout", ti))
            w.mark(("i1att", Jg), "tM")
        if upto in ("IL1", "IL2", "ILq", "ILs"):
            lastev = {"IL1": ("pcn", NJ - 1, 3), "IL2": ("ctxcev", NT - 1),
                      "ILq": ("qcev", NJ - 1, 7), "ILs": ("scmm", NJ - 1, 3)}[upto]
            for ti in range(NT):
                E("sync", lambda e, ti=ti: e.dma_start(out=xt[ti % 2][:], in_=dx1[ti * 128:(ti + 1) * 128, :]),
                  sem="qF%d" % (ti % 2),
                  deps=[("x1st", NT - 2), ("x1st", NT - 1), lastev,
                        ("fdbg", ti - 2) if ti >= 2 else None], key=("fdbgl", ti))
                E("gpsimd", lambda e, ti=ti: e.dma_start(out=dout[ti * 128:(ti + 1) * 128, :], in_=xt[ti % 2][:]),
                  sem="qO" if ti % 2 == 0 else "qO2", deps=[("fdbgl", ti)], key=("fdbg", ti))
            w.barrier_wait("gpsimd", [("fdbg", NT - 2), ("fdbg", NT - 1), ("oall",), ("yall",)])
            w.barrier_wait("gpsimd", [("tap", nm) for nm in dtaps])
            return

        tap("qdt", lambda: QdT[:], [("qdtev", 7)])
        tap("biasd", lambda: bias16d_t[:], ["dbias"])
        tap("lrun", lambda: l_run[:], [("sdlu", NJ - 1, 7)])
        tap("odacc", lambda: od_acc[:], [("odn", 15)])
        tap("odtb", lambda: odTb[:], ["odtev"])
        tap("otok", lambda: o_tok[:], ["otok"])
        tap("oall", lambda: Oall[:], [("oall",)])


        w.barrier_wait("gpsimd", [("fout", NT - 2), ("fout", NT - 1), ("oall",), ("yall",)])
        w.barrier_wait("gpsimd", [("tap", nm) for nm in dtaps])

    with nc.Block() as block:
        @block.sync
        def _(eng):
            walk(Walk("sync", eng, sems))

        @block.gpsimd
        def _(eng):
            walk(Walk("gpsimd", eng, sems))

        @block.vector
        def _(eng):
            walk(Walk("vector", eng, sems))

        @block.scalar
        def _(eng):
            walk(Walk("scalar", eng, sems))

        @block.tensor
        def _(eng):
            walk(Walk("tensor", eng, sems))

    for g in sem_guards:
        g.__exit__(None, None, None)
    PS_guard.__exit__(None, None, None)
    return nc



def _inputs_for_core(c, kw):
    m = dict(
        x=kw["L0_attn_x"][c],
        wq=kw["L0_attn_wq"], wk=kw["L0_attn_wk"], wv=kw["L0_attn_wv"], wo=kw["L0_attn_wo"],
        slotq=kw["L0_ffn_slot_query"][0],
        dWq=kw["L0_ffn_d_Wq"], dWk=kw["L0_ffn_d_Wk"], dWv=kw["L0_ffn_d_Wv"], dWo=kw["L0_ffn_d_Wo"],
        cWq=kw["L0_ffn_c_Wq"], cWk=kw["L0_ffn_c_Wk"], cWv=kw["L0_ffn_c_Wv"], cWo=kw["L0_ffn_c_Wo"],
        dbq=kw["L0_ffn_d_bq"][None, :], dbk=kw["L0_ffn_d_bk"][None, :],
        dbv=kw["L0_ffn_d_bv"][None, :], dbo=kw["L0_ffn_d_bo"][None, :],
        cbq=kw["L0_ffn_c_bq"][None, :], cbk=kw["L0_ffn_c_bk"][None, :],
        cbv=kw["L0_ffn_c_bv"][None, :], cbo=kw["L0_ffn_c_bo"][None, :],
        lng=kw["L0_ffn_e_ln_g"], lnb=kw["L0_ffn_e_ln_b"],
        eW1=kw["L0_ffn_e_W1"][2 * c:2 * c + 2],
        eWg=kw["L0_ffn_e_Wg"][2 * c:2 * c + 2],
        eW2=kw["L0_ffn_e_W2"][2 * c:2 * c + 2],
        eb1=kw["L0_ffn_e_b1"][2 * c:2 * c + 2],
        ebg=kw["L0_ffn_e_bg"][2 * c:2 * c + 2],
        eb2=kw["L0_ffn_e_b2"][2 * c:2 * c + 2],
    )
    esel = np.zeros((128, 16), np.float32)
    for le in range(2):
        for r in range(8):
            esel[r * 16 + 2 * c + le, le * 8 + r] = 1.0
    m["esel"] = esel
    ind2 = np.zeros((2, 128), np.float32)
    ind2[0, :] = (np.arange(128) % 32 < 16)
    ind2[1, :] = (np.arange(128) % 32 >= 16)
    m["ind2"] = ind2
    sel = np.zeros((128, 16), np.float32)
    for cc in range(8):
        for le in range(2):
            sel[16 * cc + 8 * le + c, 2 * cc + le] = 1.0
    m["bsel"] = sel
    for k in m:
        m[k] = np.ascontiguousarray(m[k], dtype=np.float32)
    return m


def kernel(**kw):
    S = kw["L0_attn_x"].shape[1]
    nc = build_program(S)
    in_maps = [_inputs_for_core(c, kw) for c in range(NCORES)]
    res = run_bass_kernel_spmd(nc, in_maps, core_ids=list(range(NCORES)))
    out = np.stack([res.results[c]["out"] for c in range(NCORES)], axis=0)
    return out.astype(np.float32)


# version 09 (final)
# speedup vs baseline: 1.2701x; 1.0182x over previous
import numpy as np
import ml_dtypes
import concourse.bass as bass
import concourse.mybir as mybir
from concourse.bass_utils import run_bass_kernel_spmd

F32 = mybir.dt.float32
BF16 = mybir.dt.bfloat16
AF = mybir.ActivationFunctionType
ALU = mybir.AluOpType
AX = mybir.AxisListType.X

NCORES = 8
D = 1024
NH = 16
HD = 64
F = 4096
NS = 16
EPS = 1e-5
SCALE = 0.125

B0SET = [0, 1, 2, 3, 8, 9, 10, 11]
B64SET = [4, 5, 6, 7, 12, 13, 14, 15]
PAIRS = list(zip(B0SET, B64SET))


def qcol(h):
    if h in B0SET:
        return B0SET.index(h) * 128
    return B64SET.index(h) * 128 + 64

KB = 1024


def _ceil(a, b):
    return (a + b - 1) // b


class Walk:

    lenient = False

    def __init__(self, kind, eng, sems):
        self.kind = kind
        self.eng = eng
        self.sems = sems
        self.ctr = {k: 0 for k in sems}
        self.waited = {k: 0 for k in sems}
        self.ev = {}

    def _wait(self, sem, val):
        if val > self.waited[sem]:
            self.eng.wait_ge(self.sems[sem], val)
            self.waited[sem] = val

    def _dep(self, dep):
        if isinstance(dep, tuple) and len(dep) == 2 and isinstance(dep[0], str) \
                and dep[0] in self.sems and isinstance(dep[1], int):
            return dep
        return self.ev[dep]

    def emit(self, kind, f=None, sem=None, deps=(), key=None, inc=None):
        if inc is None:
            inc = 16 if (sem is not None and sem.startswith("q")) else 1
        if self.kind == kind:
            for dep in deps:
                if dep is None:
                    continue
                try:
                    s, v = self._dep(dep)
                except KeyError:
                    if self.lenient:
                        continue
                    raise
                self._wait(s, v)
            inst = f(self.eng)
            if sem is not None:
                inst.then_inc(self.sems[sem], inc)
        if sem is not None:
            self.ctr[sem] += inc
            if key is not None:
                self.ev[key] = (sem, self.ctr[sem])
        elif key is not None:
            raise ValueError("key requires sem")

    def mark(self, key, sem):
        self.ev[key] = (sem, self.ctr[sem])

    def barrier_wait(self, kind, keys):
        if self.kind == kind:
            for dep in keys:
                if dep is None:
                    continue
                s, v = self._dep(dep)
                self._wait(s, v)


def build_program(S, taps=(), upto="I"):
    Walk.lenient = (upto != "I")
    NT = S // 128
    NJ = S // 512
    nc = bass.Bass("TRN2", target_bir_lowering=False, debug=False)

    dx = nc.dram_tensor("x", [S, D], F32, kind="ExternalInput").ap()
    dwq = nc.dram_tensor("wq", [D, D], F32, kind="ExternalInput").ap()
    dwk = nc.dram_tensor("wk", [256, D], F32, kind="ExternalInput").ap()
    dwv = nc.dram_tensor("wv", [256, D], F32, kind="ExternalInput").ap()
    dwo = nc.dram_tensor("wo", [D, D], F32, kind="ExternalInput").ap()
    dsq = nc.dram_tensor("slotq", [NS, D], F32, kind="ExternalInput").ap()
    dW = {}
    for nm in ("dWq", "dWk", "dWv", "dWo", "cWq", "cWk", "cWv", "cWo"):
        dW[nm] = nc.dram_tensor(nm, [D, D], F32, kind="ExternalInput").ap()
    dB = {}
    for nm in ("dbq", "dbk", "dbv", "dbo", "cbq", "cbk", "cbv", "cbo"):
        dB[nm] = nc.dram_tensor(nm, [1, D], F32, kind="ExternalInput").ap()
    dlng = nc.dram_tensor("lng", [NS, D], F32, kind="ExternalInput").ap()
    dlnb = nc.dram_tensor("lnb", [NS, D], F32, kind="ExternalInput").ap()
    deW1 = nc.dram_tensor("eW1", [2, D, F], F32, kind="ExternalInput").ap()
    deWg = nc.dram_tensor("eWg", [2, D, F], F32, kind="ExternalInput").ap()
    deW2 = nc.dram_tensor("eW2", [2, F, D], F32, kind="ExternalInput").ap()
    deb1 = nc.dram_tensor("eb1", [2, F], F32, kind="ExternalInput").ap()
    debg = nc.dram_tensor("ebg", [2, F], F32, kind="ExternalInput").ap()
    deb2 = nc.dram_tensor("eb2", [2, D], F32, kind="ExternalInput").ap()
    dbsel = nc.dram_tensor("bsel", [128, 16], F32, kind="ExternalInput").ap()
    dind2 = nc.dram_tensor("ind2", [2, 128], F32, kind="ExternalInput").ap()
    desel = nc.dram_tensor("esel", [128, 16], F32, kind="ExternalInput").ap()
    dout = nc.dram_tensor("out", [S, D], F32, kind="ExternalOutput").ap()

    dx1 = nc.dram_tensor("x1d", [S, D], F32).ap()
    dqt = nc.dram_tensor("qtd", [8, 128, S], BF16).ap()
    dewb1 = nc.dram_tensor("ewb1", [16, 128, 8, KB], BF16).ap()
    dewb2 = nc.dram_tensor("ewb2", [8, 128, 8, KB], BF16).ap()
    dewbF = nc.dram_tensor("ewbF", [8, 128, 8, KB], BF16).ap()
    dlrow = nc.dram_tensor("lrowd", [8, KB], F32).ap()
    drl = nc.dram_tensor("rld", [16, 512], F32).ap()
    dcc1i = nc.dram_tensor("cc1i", [NS, D], F32)
    dcc1o = nc.dram_tensor("cc1o", [128, D], F32)
    dcc2i = nc.dram_tensor("cc2i", [NS, D], F32)
    dcc2o = nc.dram_tensor("cc2o", [128, D], F32)

    dtaps = {}
    for name, shape, dt_ in taps:
        dtaps[name] = nc.dram_tensor("tap_" + name, list(shape), dt_, kind="ExternalOutput").ap()

    SBASE = (nc.sbuf_base + 31) // 32 * 32
    cur = [0]

    def alloc_at(name, shape, dtype, offset):
        return nc.alloc_sbuf_tensor_at(name, list(shape), dtype, offset=SBASE + offset)

    def sb(name, shape, dtype, at=None):
        per = int(np.prod(shape[1:])) * mybir.dt.size(dtype)
        if at is None:
            at = cur[0]
            cur[0] = (at + per + 31) // 32 * 32
        return alloc_at(name, shape, dtype, at)

    ident = sb("ident", [128, 128], BF16)
    identf = sb("identf", [128, 128], F32)
    trimask = sb("trimask", [128, 128], BF16)
    st_s = sb("st_s", [128, NT], F32)
    st_b = sb("st_b", [128, NT], F32)
    lnst = sb("lnst", [128, 12], F32)
    bias16 = sb("bias16", [128, 16], F32)
    bias16p = sb("bias16p", [128, 8], F32)
    bias16c = sb("bias16c", [128, 16], F32)
    bias16d_t = sb("bias16d_t", [128, 8], F32)
    bcols = sb("bcols", [128, 48], F32)
    nrm = sb("nrm", [128, 64], F32)
    qnacc = sb("qnacc", [128, 128], F32)
    lstage_off = cur[0]
    lstage = sb("lstage", [128, KB], F32)
    xnsc = sb("xnsc", [128, KB], BF16)
    sqsc = sb("sqsc", [128, 512], BF16)
    sqT = sb("sqT", [128, 128], BF16)
    sqTb = sb("sqTb", [128, 128], F32)
    odTb = sb("odTb", [128, 128], BF16)
    oT = sb("oT", [128, 128], F32)
    o2T = sb("o2T", [128, 128], F32)
    o2Tb = sb("o2Tb", [128, 128], BF16)
    QdT = sb("QdT", [128, 128], BF16)
    QdBD = sb("QdBD", [128, 8 * 32], BF16)

    KcT = sb("KcT", [128, 128], BF16)
    VcT = sb("VcT", [128, 128], F32)
    VcOnes = sb("VcOnes", [16, 16 * 65], BF16)
    onesc = sb("onesc", [1, 128], F32)
    ug = sb("ug", [128, 256], BF16)
    u_sb = sb("u_sb", [128, 256], BF16)
    g_sb = sb("g_sb", [128, 256], BF16)
    y_acc = sb("y_acc", [128, 64], F32)
    bsel_sb = sb("bsel_sb", [128, 16], F32)
    l_run = sb("l_run", [32, 8], F32)
    bonesb = sb("bonesb", [128, 2], BF16)
    rl32 = sb("rl32", [32, 8], F32)
    rlB = sb("rlB", [16, 8], F32)
    lpart = sb("lpart", [32, 2], F32)
    ind2 = sb("ind2", [2, 128], F32)
    mrow2 = sb("mrow2", [2, 8], F32)
    bstg = sb("bstg", [8, 128], F32)

    A1 = cur[0]; cur[0] += 16 * KB + 512
    A3 = cur[0]; cur[0] += 8 * KB
    A4 = cur[0]; cur[0] += 16 * KB
    A5 = cur[0]; cur[0] += 16 * KB
    A6 = cur[0]; cur[0] += 8 * KB
    A7 = cur[0]; cur[0] += 8 * KB
    A8 = cur[0]; cur[0] += 16 * KB
    A9 = cur[0]; cur[0] += 32 * KB
    A10 = cur[0]; cur[0] += 8 * KB
    A11 = cur[0]; cur[0] += 40 * KB
    AKT = cur[0]; cur[0] += max(4 * S, 16 * KB)
    ASQ = cur[0]; cur[0] += 4 * KB

    assert cur[0] <= 212832, f"SBUF over budget: {cur[0]} bytes/partition"

    V_sb = alloc_at("V_sb", [128, NT * 260], BF16, A1)
    W1bfh = alloc_at("W1bfh", [128, 8192], BF16, A1)
    xnTg = alloc_at("xnTg", [128, 8 * 512], BF16, A3)
    QTJ = [alloc_at(f"QTJ{i}", [128, 8 * 512], BF16, A4 + i * 8 * KB) for i in range(2)]
    stage1 = alloc_at("stage1", [128, 4096], F32, A4)
    QcT = alloc_at("QcT", [128, 8 * KB], BF16, A4)
    ctxT = [alloc_at(f"ctxT{i}", [128, 8 * 512], BF16, A5 + i * 8 * KB) for i in range(2)]
    stage2 = alloc_at("stage2", [128, 4096], F32, A5)
    ctxcT = [alloc_at(f"ctxcT{i}", [128, 8 * 512], BF16, A5 + i * 8 * KB) for i in range(2)]
    PT = [alloc_at(f"PT{i}", [128, KB], BF16, A6 + i * 2 * KB) for i in range(4)]
    PTpair = [alloc_at(f"PTp{i}", [128, 2 * KB], BF16, A6 + i * 4 * KB) for i in range(2)]
    KdTg = alloc_at("KdTg", [128, 8 * 512], BF16, A6)
    PcB = [alloc_at(f"PcB{i}", [16, 512], BF16, A6 + i * KB) for i in range(4)]
    KcBD = alloc_at("KcBD", [128, 8 * 80], BF16, A6 + 4 * KB)
    rlb = alloc_at("rlb", [64, 2048], F32, A7)
    ones64 = alloc_at("ones64", [128, 64], F32, A7)
    dbvb = alloc_at("dbvb", [128, KB], F32, A5)
    cbob = alloc_at("cbob", [128, KB], F32, A11)
    xt = [alloc_at(f"xt{i}", [128, KB], F32, A8 + i * 4 * KB) for i in range(4)]
    Wgbfh = alloc_at("Wgbfh", [128, 8192], BF16, A8)
    ffnW = [alloc_at(f"ffnW{i}", [128, 8 * KB], BF16, A9 + i * 16 * KB) for i in range(2)]
    Oall = alloc_at("Oall", [128, KB], F32, A9)
    Yall = alloc_at("Yall", [128, KB], F32, A9 + 4 * KB)
    Gt = alloc_at("Gt", [128, KB], F32, A9 + 8 * KB)
    Bt = alloc_at("Bt", [128, KB], F32, A9 + 12 * KB)
    W2bfw = alloc_at("W2bfw", [128, 8 * KB], BF16, A9 + 16 * KB)
    sqf32 = alloc_at("sqf32", [128, KB], F32, ASQ)
    o_tok = alloc_at("o_tok", [16, KB], F32, lstage_off)
    wstg = [alloc_at(f"wstg{i}", [128, KB], F32, A10 + i * 4 * KB) for i in range(2)]
    pstg = [alloc_at(f"pstg{i}", [128, 4096], F32, A9 + i * 16 * KB) for i in range(2)]
    pwb = [alloc_at(f"pwb{i}", [128, 2048], BF16, A10 + i * 4 * KB) for i in range(2)]
    wqT = alloc_at("wqT", [128, 8 * KB], BF16, A11)
    wkT = alloc_at("wkT", [128, 2 * KB], BF16, A11 + 16 * KB)
    wvT = alloc_at("wvT", [128, 2 * KB], BF16, A11 + 20 * KB)
    woT = alloc_at("woT", [128, 8 * KB], BF16, A11 + 24 * KB)
    VdG = [alloc_at(f"VdG{i}", [128, 4 * 1040], BF16, A11 + i * 9 * KB) for i in range(2)]
    PdB = [alloc_at(f"PdB{i}", [32, 512], BF16, A11 + 18 * KB + i * KB) for i in range(4)]
    PdTG = alloc_at("PdTG", [128, 4 * 256], BF16, A11 + 22 * KB)
    od_acc = alloc_at("od_acc", [16, KB], F32, A11 + 24 * KB)
    b1T = alloc_at("b1T", [128, 64], F32, A11 + 28 * KB)
    bgT = alloc_at("bgT", [128, 64], F32, A11 + 28 * KB + 256)
    b2T = alloc_at("b2T", [128, 16], F32, A11 + 28 * KB + 512)
    eb1c2 = alloc_at("eb1c2", [32, 256], F32, A11 + 29 * KB)
    PcT = [alloc_at(f"PcT{i}", [16, 2048], BF16, A11 + 30 * KB + i * 4 * KB) for i in range(2)]
    Yle = [alloc_at(f"Yle{i}", [8, KB], F32, A6 + i * 4 * KB) for i in range(2)]
    hsel = alloc_at("hsel", [16, KB], BF16, AKT + 11 * KB)
    hTe = alloc_at("hTe", [128, 128], BF16, AKT + 15 * KB)
    esel_sb = alloc_at("esel_sb", [128, 16], F32, AKT + 15 * KB + 256)
    esel_bf = alloc_at("esel_bf", [128, 16], BF16, AKT + 15 * KB + 320)
    Pcn = alloc_at("Pcn", [128, 256], BF16, A11 + 38 * KB)
    Scm = alloc_at("Scm", [128, 16 * 3], F32, A11 + 38 * KB + 512)
    Sct = alloc_at("Sct", [128, 256], F32, A11 + 39 * KB)
    KT = alloc_at("KT", [128, 2 * S], BF16, AKT)
    o2_tok = alloc_at("o2_tok", [16, KB], F32, AKT)
    Vctok = alloc_at("Vctok", [16, KB], F32, AKT + 4 * KB)
    eb1c = alloc_at("eb1c", [32, 256], F32, AKT + 8 * KB)
    ebgc = alloc_at("ebgc", [32, 256], F32, AKT + 9 * KB)
    eb2c = alloc_at("eb2c", [8, 256], F32, AKT + 10 * KB)
    hT = alloc_at("hT", [128, KB], BF16, AKT + 11 * KB)
    h_sb = alloc_at("h_sb", [128, KB], BF16, AKT + 13 * KB)
    qstg = [alloc_at(f"qstg{i}", [128, 512], BF16, A5 + i * KB) for i in range(2)]

    PS_guard = nc.psum_tensor("PS", [128, 4096], F32)
    PS = PS_guard.__enter__()
    PSb = PS.ap().bitcast(BF16)

    def psf(bank, cols=512, rows=128, row0=0):
        return PS[row0:row0 + rows, bank * 512:bank * 512 + cols]

    def psfc(bank, c0, cols, rows=128, row0=0):
        return PS[row0:row0 + rows, bank * 512 + c0:bank * 512 + c0 + cols]

    def psb(bank, c0, cols, rows=128, row0=0):
        return PSb[row0:row0 + rows, bank * 1024 + c0:bank * 1024 + c0 + cols]

    sem_names = ["tM", "vV", "sS", "gP", "cc", "qO", "qM",
                 "qW0", "qW1", "qX0", "qX1", "qQ0", "qQ1", "qT0", "qT1",
                 "qR0", "qR1", "qS0", "qS1", "qLG", "qRB", "qLD0", "qLD1", "qRL",
                 "qF0", "qF1", "qB", "qC", "qE0", "qE1", "qO2",
                 "qPW0", "qPW1", "qPB0", "qPB1"]
    sems = {}
    sem_guards = []
    for nm in sem_names:
        g = nc.semaphore(nm)
        sems[nm] = g.__enter__()
        sem_guards.append(g)

    def ch(t, n):
        return t[:].rearrange("p (c n) -> p c n", n=n)

    def walk(w: Walk):
        E = w.emit

        def tap(name, ap_f, deps):
            if name in dtaps:
                E("gpsimd", lambda e: e.dma_start(out=dtaps[name], in_=ap_f()), sem="qM",
                  deps=deps, key=("tap", name))

        E("gpsimd", lambda e: e.memset(ident[:], 0.0), sem="gP", key="identz")
        E("gpsimd", lambda e: e.affine_select(out=ident[:], in_=ident[:], compare_op=ALU.not_equal,
                                              fill=1.0, base=0, pattern=[[-1, 128]], channel_multiplier=1),
          sem="gP", deps=["identz"], key="ident")
        E("gpsimd", lambda e: e.memset(identf[:], 0.0), sem="gP", deps=["ident"], key="identfz")
        E("gpsimd", lambda e: e.affine_select(out=identf[:], in_=identf[:], compare_op=ALU.not_equal,
                                              fill=1.0, base=0, pattern=[[-1, 128]], channel_multiplier=1),
          sem="gP", deps=["identfz"], key="identf")
        E("gpsimd", lambda e: e.memset(trimask[:], 1.0), sem="gP", deps=["identf"], key="trimaskz")
        E("gpsimd", lambda e: e.affine_select(out=trimask[:], in_=trimask[:], compare_op=ALU.is_ge,
                                              fill=0.0, base=0, pattern=[[1, 128]], channel_multiplier=-1),
          sem="gP", deps=["trimaskz"], key="trimask")
        E("gpsimd", lambda e: e.memset(onesc[:], 1.0), sem="gP", key="onesc")
        E("gpsimd", lambda e: e.memset(V_sb[:], 1.0), sem="gP", key="vones")
        E("gpsimd", lambda e: e.memset(nrm[:, 0:2], 0.0), sem="gP", key="bz0")
        E("gpsimd", lambda e: e.memset(nrm[0:64, 0:1], 1.0), sem="gP", deps=["bz0"])
        E("gpsimd", lambda e: e.memset(nrm[64:128, 1:2], 1.0), sem="gP", deps=["bz0"])
        E("gpsimd", lambda e: e.memset(bonesb[:], 0.0), sem="gP", key="bz1")
        E("gpsimd", lambda e: e.memset(bonesb[0:64, 0:1], 1.0), sem="gP", deps=["bz1"])
        E("gpsimd", lambda e: e.memset(bonesb[64:128, 1:2], 1.0), sem="gP", deps=["bz1"], key="bones")
        E("gpsimd", lambda e: e.memset(qnacc[:, 0:16], 0.0), sem="gP", key="qnz")
        E("gpsimd", lambda e: e.memset(nrm[0:2, 40:48], 0.0), sem="gP", key="qnz2")
        E("gpsimd", lambda e: e.memset(ones64[64:65, 0:64], 1.0), sem="gP", key="ones64")

        prev_tag = [None]

        def prep_attn_w(tag, dsrc, nrows, dstT, cols_per_chunk, permute_q=False):
            nrc = nrows // 128
            pt = prev_tag[0]
            for rc in range(nrc):
                stg = wstg[rc % 2]
                E("sync", lambda e, rc=rc, stg=stg: e.dma_start(out=stg[:], in_=dsrc[rc * 128:(rc + 1) * 128, :]),
                  sem="qW%d" % (rc % 2),
                  deps=[("wprep_cast", tag, rc - 2) if rc >= 2 else (("wprep_done", pt) if pt else None)],
                  key=("wprep_ld", tag, rc))
                E("vector", lambda e, stg=stg: e.tensor_copy(out=xnsc[:], in_=stg[:]),
                  sem="vV",
                  deps=[("wprep_ld", tag, rc),
                        ("wprep_tr", tag, rc - 1) if rc >= 1 else (("wprep_done", pt) if pt else None)],
                  key=("wprep_cast", tag, rc))
                for dcn in range(8):
                    E("tensor", lambda e, dcn=dcn: e.transpose(out=psb(7, dcn * 128, 128),
                                                               in_=xnsc[:, dcn * 128:(dcn + 1) * 128],
                                                               identity=ident[:]),
                      sem="tM" if dcn == 7 else None,
                      deps=[("wprep_cast", tag, rc), "ident",
                            ("wprep_ev", tag, rc - 1) if rc >= 1 else (("wprep_done", pt) if pt else None)],
                      key=("wprep_tr", tag, rc) if dcn == 7 else None)
                if permute_q:
                    for s2 in range(2):
                        h = 2 * rc + s2
                        dc_ = qcol(h)
                        E("vector", lambda e, s2=s2, dc_=dc_: e.tensor_copy(
                            out=ch(dstT, cols_per_chunk)[:, :, dc_:dc_ + 64],
                            in_=psb(7, 0, 1024).rearrange("p (c n) -> p c n", n=128)[:, :, s2 * 64:(s2 + 1) * 64]),
                          sem="vV" if s2 == 1 else None, deps=[("wprep_tr", tag, rc)],
                          key=("wprep_ev", tag, rc) if s2 == 1 else None)
                else:
                    E("vector", lambda e, rc=rc: e.tensor_copy(
                        out=ch(dstT, cols_per_chunk)[:, :, rc * 128:(rc + 1) * 128],
                        in_=psb(7, 0, 1024).rearrange("p (c n) -> p c n", n=128)),
                      sem="vV", deps=[("wprep_tr", tag, rc)], key=("wprep_ev", tag, rc))
            w.mark(("wprep_done", tag), "vV")
            prev_tag[0] = tag

        prep_attn_w("wq", dwq, D, wqT, KB, permute_q=True)
        prep_attn_w("wk", dwk, 256, wkT, 256)
        prep_attn_w("wv", dwv, 256, wvT, 256)
        prep_attn_w("wo", dwo, D, woT, KB)

        def ln_stats(ti, xbuf, kpref, sdst, bdst, extra0=()):
            K = lambda nm: (kpref, nm, ti)
            E("vector", lambda e: e.reduce_sum(out=lnst[:, 0:1], in_=xbuf[:], axis=AX),
              sem="vV", deps=[*extra0], key=K("sum"))
            E("vector", lambda e: e.tensor_mul(out=sqf32[:], in0=xbuf[:], in1=xbuf[:]), sem="vV", key=K("sq"))
            E("vector", lambda e: e.reduce_sum(out=lnst[:, 1:2], in_=sqf32[:], axis=AX),
              sem="vV", deps=[K("sq")], key=K("sumsq"))
            E("vector", lambda e: e.tensor_scalar_mul(out=lnst[:, 2:3], in0=lnst[:, 0:1], scalar1=1.0 / D),
              sem="vV", deps=[K("sum")], key=K("mean"))
            E("vector", lambda e: e.tensor_scalar_mul(out=lnst[:, 3:4], in0=lnst[:, 1:2], scalar1=1.0 / D),
              sem="vV", deps=[K("sumsq")], key=K("ms"))
            E("vector", lambda e: e.tensor_mul(out=lnst[:, 4:5], in0=lnst[:, 2:3], in1=lnst[:, 2:3]),
              sem="vV", deps=[K("mean")], key=K("m2"))
            E("vector", lambda e: e.tensor_sub(out=lnst[:, 3:4], in0=lnst[:, 3:4], in1=lnst[:, 4:5]),
              sem="vV", deps=[K("ms"), K("m2")], key=K("v0"))
            E("vector", lambda e: e.tensor_scalar_add(out=lnst[:, 3:4], in0=lnst[:, 3:4], scalar1=EPS),
              sem="vV", deps=[K("v0")], key=K("var"))
            E("scalar", lambda e: e.activation(out=lnst[:, 5:6], in_=lnst[:, 3:4], func=AF.Ln, bias=0.0, scale=1.0),
              sem="sS", deps=[K("var")], key=K("ln"))
            E("scalar", lambda e: e.activation(out=sdst, in_=lnst[:, 5:6], func=AF.Exp, bias=0.0, scale=-0.5),
              sem="sS", deps=[K("ln")], key=K("rstd"))
            E("vector", lambda e: e.tensor_mul(out=lnst[:, 6:7], in0=lnst[:, 2:3], in1=sdst),
              sem="vV", deps=[K("rstd")], key=K("mr"))
            E("vector", lambda e: e.tensor_scalar_mul(out=bdst, in0=lnst[:, 6:7], scalar1=-1.0),
              sem="vV", deps=[K("mr"), ("xnapply", ti - 2) if (kpref == "l1" and ti >= 2) else None],
              key=(kpref + "bias", ti))

        def ln_tile(ti, xbuf):
            p = ti % 2
            sdst = lnst[:, 8 + 2 * p:9 + 2 * p]
            bdst = lnst[:, 9 + 2 * p:10 + 2 * p]
            ln_stats(ti, xbuf, "l1", sdst, bdst,
                     extra0=[("xld", ti), ("l1", "rstd", ti - 1) if ti >= 1 else None,
                             ("tap", "lnst0") if ("lnst0" in dtaps and ti == 2) else None])
            E("scalar", lambda e: e.activation(out=xnsc[:], in_=xbuf[:], func=AF.Identity, bias=bdst, scale=sdst),
              sem="sS",
              deps=[("l1bias", ti), ("xntr", ti - 1) if ti >= 1 else ("wprep_done", "wo"),
                    ("tap", "xnsc0") if ("xnsc0" in dtaps and ti == 1) else None],
              key=("xnapply", ti))

        def xn_transpose(ti, dstcol):
            bank = 6 if ti % 2 == 0 else 7
            for dcn in range(8):
                E("tensor", lambda e, dcn=dcn, bank=bank: e.transpose(
                    out=psb(bank, dcn * 128, 128), in_=xnsc[:, dcn * 128:(dcn + 1) * 128], identity=ident[:]),
                  sem="tM" if dcn == 7 else None,
                  deps=[("xnapply", ti), "ident",
                        ("xntrev", ti - 2) if ti >= 2 else ("wprep_done", "wo"),
                        ("tap", "xng0") if ("xng0" in dtaps and ti == 4) else None],
                  key=("xntr", ti) if dcn == 7 else None)
            E("scalar", lambda e, bank=bank, dstcol=dstcol: e.copy(
                out=ch(xnTg, 512)[:, :, dstcol:dstcol + 128],
                in_=psb(bank, 0, 1024).rearrange("p (c n) -> p c n", n=128)),
              sem="sS", deps=[("xntr", ti)], key=("xntrev", ti))

        for Jg in range(NJ):
            for tt in range(4):
                ti = Jg * 4 + tt
                xbuf = xt[ti % 2]
                E("sync", lambda e, ti=ti, xbuf=xbuf: e.dma_start(out=xbuf[:], in_=dx[ti * 128:(ti + 1) * 128, :]),
                  sem="qX%d" % (ti % 2), deps=[("xnapply", ti - 2)] if ti >= 2 else [], key=("xld", ti))
                ln_tile(ti, xbuf)
                if ti == 0:
                    tap("xnsc0", lambda: xnsc[:], [("xnapply", 0)])
                    tap("lnst0", lambda: lnst[:], [("xnapply", 0)])
                xn_transpose(ti, tt * 128)
                if ti == 3:
                    tap("xng0", lambda: xnTg[:], [("xntrev", 3)])
                for dcn in range(8):
                    E("tensor", lambda e, dcn=dcn, tt=tt: e.matmul(
                        out=psf(4, 256),
                        lhsT=ch(xnTg, 512)[:, dcn, tt * 128:(tt + 1) * 128],
                        rhs=ch(wvT, 256)[:, dcn, :],
                        start=(dcn == 0), stop=(dcn == 7)),
                      sem="tM" if dcn == 7 else None,
                      deps=[("xntrev", ti), ("wprep_done", "wv"),
                            ("vev", ti - 1) if ti >= 1 else "vones"],
                      key=("vmm", ti) if dcn == 7 else None)
                E("scalar", lambda e, ti=ti: e.copy(
                    out=V_sb[:].rearrange("p (t g n) -> p t g n", g=4, n=65)[:, ti, :, 0:64],
                    in_=psf(4, 256).rearrange("p (g n) -> p g n", n=64)),
                  sem="sS", deps=[("vmm", ti), "vones"], key=("vev", ti))
            for qc in range(8):
                bank = 2 + qc % 2
                for dcn in range(8):
                    E("tensor", lambda e, dcn=dcn, qc=qc, bank=bank: e.matmul(
                        out=psf(bank, 512),
                        lhsT=ch(wqT, KB)[:, dcn, qc * 128:(qc + 1) * 128],
                        rhs=ch(xnTg, 512)[:, dcn, :],
                        start=(dcn == 0), stop=(dcn == 7)),
                      sem="tM" if dcn == 7 else None,
                      deps=[("xntrev", Jg * 4 + 3), ("wprep_done", "wq"),
                            ("qev", Jg, qc - 2) if qc >= 2 else (("qev", Jg - 1, 6 + qc) if Jg >= 1 else None)],
                      key=("qmm", Jg, qc) if dcn == 7 else None)
                stq = qstg[qc % 2]
                E("scalar", lambda e, bank=bank, stq=stq: e.copy(out=stq[:], in_=psf(bank, 512)),
                  sem="sS", deps=[("qmm", Jg, qc), ("qdma", Jg, qc - 2) if qc >= 2 else
                                  (("qdma", Jg - 1, 6 + qc) if Jg >= 1 else None)],
                  key=("qev", Jg, qc))
                E("gpsimd", lambda e, qc=qc, Jg=Jg, stq=stq: e.dma_start(
                    out=dqt[qc, :, Jg * 512:(Jg + 1) * 512], in_=stq[:]),
                  sem="qQ%d" % (qc % 2), deps=[("qev", Jg, qc)], key=("qdma", Jg, qc))
                E("scalar", lambda e, stq=stq: e.activation(out=sqsc[:], in_=stq[:], func=AF.Square, bias=0.0, scale=1.0),
                  sem="sS",
                  deps=[("qev", Jg, qc),
                        ("qnmm", Jg, qc - 1) if qc >= 1 else (("knmm", Jg - 1, 1) if Jg >= 1 else None)],
                  key=("qsq", Jg, qc))
                E("tensor", lambda e: e.matmul(out=psf(5, 512, rows=2), lhsT=bonesb[:], rhs=sqsc[:],
                                               start=True, stop=True),
                  sem="tM", deps=[("qsq", Jg, qc), "bones",
                                  ("qnred0", Jg, qc - 1) if qc >= 1 else (("knred0", Jg - 1, 1) if Jg >= 1 else None)],
                  key=("qnmm", Jg, qc))
                E("vector", lambda e, qc=qc: e.reduce_max(out=qnacc[0:2, 16 + qc:17 + qc],
                                                          in_=psf(5, 512, rows=2), axis=AX),
                  sem="vV", deps=[("qnmm", Jg, qc), ("qnupd", Jg - 1, qc) if Jg >= 1 else "qnz"],
                  key=("qnred0", Jg, qc))
                E("vector", lambda e, qc=qc: e.tensor_max(out=qnacc[0:2, qc:qc + 1],
                                                          in0=qnacc[0:2, qc:qc + 1],
                                                          in1=qnacc[0:2, 16 + qc:17 + qc]),
                  sem="vV", deps=[("qnred0", Jg, qc), "qnz"], key=("qnupd", Jg, qc))
            for kc in range(2):
                bank = 2 + kc
                for dcn in range(8):
                    E("tensor", lambda e, dcn=dcn, kc=kc, bank=bank: e.matmul(
                        out=psf(bank, 512),
                        lhsT=ch(wkT, 256)[:, dcn, kc * 128:(kc + 1) * 128],
                        rhs=ch(xnTg, 512)[:, dcn, :],
                        start=(dcn == 0), stop=(dcn == 7)),
                      sem="tM" if dcn == 7 else None,
                      deps=[("xntrev", Jg * 4 + 3), ("wprep_done", "wk"), ("qev", Jg, 6 + kc)],
                      key=("kmm", Jg, kc) if dcn == 7 else None)
                E("scalar", lambda e, kc=kc, Jg=Jg, bank=bank: e.copy(
                    out=KT[:, kc * S + Jg * 512:kc * S + (Jg + 1) * 512], in_=psf(bank, 512)),
                  sem="sS", deps=[("kmm", Jg, kc)], key=("kev", Jg, kc))
                E("scalar", lambda e, kc=kc, Jg=Jg: e.activation(
                    out=sqsc[:], in_=KT[:, kc * S + Jg * 512:kc * S + (Jg + 1) * 512],
                    func=AF.Square, bias=0.0, scale=1.0),
                  sem="sS", deps=[("kev", Jg, kc),
                                  ("knmm", Jg, kc - 1) if kc >= 1 else ("qnmm", Jg, 7)],
                  key=("ksq", Jg, kc))
                E("tensor", lambda e: e.matmul(out=psf(5, 512, rows=2), lhsT=bonesb[:], rhs=sqsc[:],
                                               start=True, stop=True),
                  sem="tM", deps=[("ksq", Jg, kc), "bones",
                                  ("knred0", Jg, kc - 1) if kc >= 1 else ("qnred0", Jg, 7)],
                  key=("knmm", Jg, kc))
                E("vector", lambda e, kc=kc: e.reduce_max(out=qnacc[0:2, 24 + kc:25 + kc],
                                                          in_=psf(5, 512, rows=2), axis=AX),
                  sem="vV", deps=[("knmm", Jg, kc), ("knupd", Jg - 1, kc) if Jg >= 1 else "qnz"],
                  key=("knred0", Jg, kc))
                E("vector", lambda e, kc=kc: e.tensor_max(out=qnacc[0:2, 8 + kc:9 + kc],
                                                          in0=qnacc[0:2, 8 + kc:9 + kc],
                                                          in1=qnacc[0:2, 24 + kc:25 + kc]),
                  sem="vV", deps=[("knred0", Jg, kc), "qnz"], key=("knupd", Jg, kc))

        E("scalar", lambda e: e.activation(out=qnacc[0:2, 32:40], in_=qnacc[0:2, 0:8], func=AF.Sqrt,
                                           bias=0.0, scale=1.0),
          sem="sS", deps=[("qnupd", NJ - 1, 7)], key="qsqrt")
        E("scalar", lambda e: e.activation(out=qnacc[0:2, 40:42], in_=qnacc[0:2, 8:10], func=AF.Sqrt,
                                           bias=0.0, scale=1.0),
          sem="sS", deps=[("knupd", NJ - 1, 1)], key="ksqrt")
        E("gpsimd", lambda e: e.dma_start(out=qnacc[0:1, 48:58], in_=qnacc[0:1, 32:42]),
          sem="qM", deps=["qsqrt", "ksqrt"])
        E("gpsimd", lambda e: e.dma_start(out=qnacc[0:1, 58:68], in_=qnacc[1:2, 32:42]),
          sem="qM", deps=["qsqrt", "ksqrt"], key="nflat")
        for slot in range(8):
            for s2 in range(2):
                h = PAIRS[slot][s2]
                qcol_ = 48 + (h % 2) * 10 + h // 2
                kcol_ = 48 + ((h >> 2) % 2) * 10 + 8 + (h >> 2) // 2
                col = slot * 2 + s2
                E("vector", lambda e, qcol_=qcol_, kcol_=kcol_, col=col: e.tensor_tensor(
                    out=nrm[0:1, 32 + col:33 + col], in0=qnacc[0:1, qcol_:qcol_ + 1],
                    in1=qnacc[0:1, kcol_:kcol_ + 1], op=ALU.mult),
                  sem="vV", deps=["nflat"], key=("mh0", col))
                E("vector", lambda e, col=col: e.tensor_scalar_mul(
                    out=nrm[0:1, 32 + col:33 + col], in0=nrm[0:1, 32 + col:33 + col], scalar1=-SCALE * 1.02),
                  sem="vV", deps=[("mh0", col)], key=("mh", col))
        E("tensor", lambda e: e.matmul(out=psf(5, 16), lhsT=onesc[0:1, :], rhs=nrm[0:1, 32:48],
                                       start=True, stop=True),
          sem="tM", deps=[("mh", 15), "onesc", ("knred0", NJ - 1, 1)], key="mhbc")
        E("vector", lambda e: e.tensor_copy(out=bias16[:], in_=psf(5, 16)), sem="vV",
          deps=["mhbc"], key="bias16")
        for pi_ in range(8):
            E("vector", lambda e, pi_=pi_: e.tensor_tensor(
                out=bias16p[:, pi_:pi_ + 1], in0=bias16[:, 2 * pi_:2 * pi_ + 1],
                in1=bias16[:, 2 * pi_ + 1:2 * pi_ + 2], op=ALU.min),
              sem="vV", deps=["bias16"], key="bias16pz" if pi_ == 7 else None)
        tap("qt", lambda: dqt, [("qdma", NJ - 1, 7)])
        tap("kt", lambda: KT[:], [("kev", NJ - 1, 1)])
        tap("vsb", lambda: V_sb[:], [("vev", NT - 1)])
        tap("bias16", lambda: bias16[:], ["bias16"])
        tap("wqT", lambda: wqT[:], [("wprep_done", "wq")])

        def ln2_tile(ti, xbuf):
            ln_stats(ti, xbuf, "l2", st_s[:, ti:ti + 1], st_b[:, ti:ti + 1],
                     extra0=[("x1ev", ti), ("l2", "rstd", ti - 1) if ti >= 1 else None])

        FNAMES = ("dWq", "dWk", "dWv", "dWo", "cWq", "cWk", "cWv", "cWo")
        pp_pieces = []
        _sc = [0, 0]

        def _add_small(src, dst, skey):
            vs = _sc[0] % 8
            pp_pieces.append(dict(big=False, sl=vs % 2, vq=vs // 2, src=src,
                                  dsts=[(dst, 0)], skey=skey))
            _sc[0] += 1

        for m, nm in enumerate(FNAMES):
            for rc in range(8):
                _add_small(dW[nm][rc * 128:(rc + 1) * 128, :], dewbF[m][:, rc:rc + 1, :], ("pwsF", nm, rc))
        for mat, dsrc in ((0, deW1), (1, deWg)):
            for le in range(2):
                for rcb in range(8):
                    dsts = [(dewb1[(mat * 2 + le) * 4 + q4][:, rcb:rcb + 1, :], q4) for q4 in range(4)]
                    pp_pieces.append(dict(big=True, sl=_sc[1] % 2, vq=None,
                                          src=dsrc[le, rcb * 128:(rcb + 1) * 128, :],
                                          dsts=dsts, skey=("pws1", mat, le, rcb)))
                    _sc[1] += 1
        for le in range(2):
            for wv2 in range(4):
                for rc in range(8):
                    _add_small(deW2[le, wv2 * KB + rc * 128:wv2 * KB + (rc + 1) * 128, :],
                               dewb2[le * 4 + wv2][:, rc:rc + 1, :], ("pws2", le, wv2, rc))

        pp_state = dict(loads=0, casts=0,
                        lastload={0: None, 1: None},
                        lastcast={},
                        laststore={0: ("wprep_cast", "wo", 6), 1: ("wprep_cast", "wo", 7)})

        def pp_quota(t):
            q = min(max(t - 8, 0) * 6, 64)
            q += min(max(t - 19, 0) * 2, 32)
            q += min(max(t - 35, 0) * 4, 64)
            return q

        def pp_emit_n(nl, ncst):
            st = pp_state
            while st["casts"] < min(ncst, len(pp_pieces)):
                k = st["casts"]
                pc = pp_pieces[k]
                sl = pc["sl"]
                ckeys = []
                if pc["big"]:
                    for h in range(2):
                        E("gpsimd", lambda e, h=h, sl=sl: e.tensor_copy(
                            out=pwb[h][:], in_=pstg[sl][:, h * 2048:(h + 1) * 2048]),
                          sem="gP", deps=[st["lastload"][sl], st["laststore"][h]],
                          key=("pwc", k, h))
                        ckeys.append(("pwc", k, h))
                    st["lastcast"][("p", sl)] = ckeys
                    for dst, q4 in pc["dsts"]:
                        h = q4 // 2
                        col = (q4 % 2) * KB
                        E("gpsimd", lambda e, dst=dst, h=h, col=col: e.dma_start(
                            out=dst, in_=pwb[h][:, col:col + KB].rearrange("p (o n) -> p o n", o=1)),
                          sem="qPB%d" % h, deps=[("pwc", k, h)], key=(*pc["skey"], q4))
                        st["laststore"][h] = (*pc["skey"], q4)
                else:
                    h = sl
                    vq = pc["vq"]
                    E("gpsimd", lambda e, h=h, sl=sl, vq=vq: e.tensor_copy(
                        out=pwb[h][:, 0:KB], in_=pstg[sl][:, vq * KB:(vq + 1) * KB]),
                      sem="gP", deps=[st["lastload"][sl], st["laststore"][h]],
                      key=("pwc", k, 0))
                    st["lastcast"][("p", sl, vq)] = [("pwc", k, 0)]
                    dst, _ = pc["dsts"][0]
                    E("gpsimd", lambda e, dst=dst, h=h: e.dma_start(
                        out=dst, in_=pwb[h][:, 0:KB].rearrange("p (o n) -> p o n", o=1)),
                      sem="qPB%d" % h, deps=[("pwc", k, 0)], key=(*pc["skey"], 0))
                    st["laststore"][h] = (*pc["skey"], 0)
                st["casts"] = k + 1
            while st["loads"] < min(nl, len(pp_pieces)):
                k = st["loads"]
                pc = pp_pieces[k]
                sl = pc["sl"]
                if pc["big"]:
                    rs = [("p", sl)] + [("p", sl, q) for q in range(4)]
                    tgt = pstg[sl][:]
                else:
                    rs = [("p", sl), ("p", sl, pc["vq"])]
                    tgt = pstg[sl][:, pc["vq"] * KB:(pc["vq"] + 1) * KB]
                deps = []
                for r in rs:
                    deps += st["lastcast"].get(r, [])
                E("gpsimd", lambda e, tgt=tgt, src=pc["src"]: e.dma_start(out=tgt, in_=src),
                  sem="qPW%d" % sl, deps=deps, key=("pwl", k))
                st["lastload"][sl] = ("pwl", k)
                st["loads"] = k + 1

        def pp_emit(t):
            pp_emit_n(pp_quota(t), pp_quota(t - 1))

        def kinfo(h):
            g = h >> 2
            return g // 2, 64 * (g % 2)

        def qinfo(h):
            if h in B0SET:
                return B0SET.index(h), 0
            return B64SET.index(h), 64

        for J in range(NJ):
            qb = QTJ[J % 2]
            for qcc in range(8):
                E("sync", lambda e, J=J, qb=qb, qcc=qcc: e.dma_start(
                    out=qb[:, qcc * 512:(qcc + 1) * 512], in_=dqt[qcc, :, J * 512:(J + 1) * 512]),
                  sem="qT%d" % (J % 2),
                  deps=[("qdma", J, 6), ("qdma", J, 7), ("scJlast", J - 2) if J >= 2 else None],
                  key=("qtj", J) if qcc == 7 else None)
            ntt = 4 * J + 4
            nbt = _ceil(ntt, 2)
            for pi in range(8):
                ha, hb = PAIRS[pi]
                ob = 4 if pi % 2 == 0 else 6

                def emit_pv(tv, pi=pi, ha=ha, hb=hb, ob=ob, ntt=ntt, J=J):
                    cv = max(0, (tv - 4 * J) * 128)
                    am = tv >= 4 * J
                    for s2 in range(2):
                        h = (ha, hb)[s2]
                        g = h >> 2
                        ptb = PTpair[tv % 2]
                        E("tensor", lambda e, s2=s2, tv=tv, cv=cv, g=g, ptb=ptb, ob=ob, ntt=ntt: e.matmul(
                            out=psfc(ob + s2, cv, 512 - cv, rows=65),
                            lhsT=V_sb[:].rearrange("p (t gg n) -> p t gg n", gg=4, n=65)[:, tv, g, :],
                            rhs=ptb[:, s2 * 512 + cv:(s2 + 1) * 512],
                            start=(tv == 0), stop=(tv == ntt - 1)),
                          sem="tM" if s2 == 1 else None,
                          deps=[("exp", J, pi, tv),
                                ("mskd", J, pi, tv) if am else None,
                                ("ctxevp", J, pi - 2) if (tv == 0 and pi >= 2) else None],
                          key=("pv", J, pi, tv) if s2 == 1 else None)
                    w.mark(("pvdone", J, pi, tv), "tM")

                for ts in range(ntt):
                    it = ts
                    c0 = max(0, (it - 4 * J) * 128)
                    first_dep = None
                    extra_dep = None
                    if ts == 0:
                        if pi >= 2:
                            first_dep = ("ctxevp", J, pi - 2)
                        elif J >= 1:
                            first_dep = ("Jwo", J - 1)
                            extra_dep = ("x1ev", 4 * J - 1)
                        else:
                            first_dep = "bias16"
                    for s2 in range(2):
                        h = (ha, hb)[s2]
                        kc2, kr = kinfo(h)
                        qc2, qr = qinfo(h)
                        E("tensor", lambda e, kc2=kc2, kr=kr, qc2=qc2, qr=qr, it=it, s2=s2, c0=c0, ts=ts, qb=qb: e.matmul(
                            out=psfc((ts % 2) * 2 + s2, c0, 512 - c0),
                            lhsT=KT[kr:kr + 64, kc2 * S + it * 128:kc2 * S + (it + 1) * 128],
                            rhs=ch(qb, 512)[qr:qr + 64, qc2, c0:512],
                            start=True, stop=True),
                          sem="tM" if s2 == 1 else None,
                          deps=[("qtj", J), ("kev", NJ - 1, 1), "bias16",
                                ("vev", NT - 1) if (J == 0 and pi == 0 and ts == 0) else None,
                                ("exp", J, pi, ts - 2) if ts >= 2 else None,
                                first_dep, extra_dep],
                          key=("sc", J, pi, ts) if s2 == 1 else None)
                    E("scalar", lambda e, ptp=PTpair[ts % 2], pi=pi, ts=ts: e.activation(
                        out=ptp[:, 0:1024], in_=PS[:, (ts % 2) * 1024:(ts % 2) * 1024 + 1024],
                        func=AF.Exp, bias=bias16p[:, pi:pi + 1], scale=SCALE),
                      sem="sS",
                      deps=[("sc", J, pi, ts), "bias16pz",
                            ("pvdone", J, pi, ts - 2) if ts >= 2 else first_dep,
                            ("mskd", J, pi, ts - 2) if (ts >= 2 and ts - 2 >= 4 * J) else None],
                      key=("exph", J, pi, ts, 1))
                    w.mark(("exph", J, pi, ts, 0), "sS")
                    w.mark(("exp", J, pi, ts), "sS")
                    anymask = it >= 4 * J
                    if anymask:
                        cb = (it - 4 * J) * 128
                        for s2 in range(2):
                            ptb = PTpair[ts % 2]
                            E("gpsimd", lambda e, cb=cb, s2=s2, ptb=ptb: e.tensor_mul(
                                out=ptb[:, s2 * 512 + cb:s2 * 512 + cb + 128],
                                in0=ptb[:, s2 * 512 + cb:s2 * 512 + cb + 128], in1=trimask[:]),
                              sem="gP", deps=[("exph", J, pi, ts, s2), "trimask"],
                              key=("msk", J, pi, ts))
                        w.mark(("mskd", J, pi, ts), "gP")
                    if ts >= 1:
                        emit_pv(ts - 1)
                if ntt >= 1:
                    emit_pv(ntt - 1)
                if pi % 2 == 1:
                    cT = ctxT[J % 2]
                    for pj in (pi - 1, pi):
                        obj = 4 if pj % 2 == 0 else 6
                        if pj == pi:
                            lwar = ("drl", J, pi - 1, 1)
                        elif pi >= 3:
                            lwar = ("drl", J, pi - 2, 1)
                        else:
                            lwar = ("drl", J - 1, 7, 1) if J >= 1 else None
                        E("vector", lambda e, obj=obj: e.reciprocal(
                            out=lstage[64:65, 0:1024], in_=PS[64:65, obj * 512:(obj + 2) * 512]),
                          sem="vV", deps=[("pvdone", J, pj, ntt - 1), lwar],
                          key=("dlrec", J, pj))
                        for s2 in range(2):
                            E("tensor", lambda e, s2=s2, obj=obj: e.matmul(
                                out=PS[64:128, (obj + s2) * 512:(obj + s2 + 1) * 512],
                                lhsT=ones64[64:65, 0:64],
                                rhs=lstage[64:65, s2 * 512:(s2 + 1) * 512],
                                start=True, stop=True, tile_position=(64, 64)),
                              sem="tM",
                              deps=[("dlrec", J, pj), "ones64"],
                              key=("drl", J, pj, s2))
                        for s2 in range(2):
                            hc = (pj * 2 + s2) % 4
                            E("vector", lambda e, obj=obj, s2=s2, hc=hc: e.tensor_copy(
                                out=rlb[0:64, hc * 512:(hc + 1) * 512],
                                in_=PS[64:128, (obj + s2) * 512:(obj + s2 + 1) * 512]),
                              sem="vV", deps=[("drl", J, pj, s2)], key=("drle", J, pj, s2))
                        for s2 in range(2):
                            h = PAIRS[pj][s2]
                            hc = (pj * 2 + s2) % 4
                            E("vector", lambda e, h=h, obj=obj, s2=s2, hc=hc, cT=cT: e.tensor_tensor(
                                out=ch(cT, 512)[64 * (h % 2):64 * (h % 2) + 64, h // 2, :],
                                in0=PS[0:64, (obj + s2) * 512:(obj + s2 + 1) * 512],
                                in1=rlb[:, hc * 512:(hc + 1) * 512],
                                op=ALU.mult),
                              sem="vV",
                              deps=[("drle", J, pj, s2),
                                    ("Jwo", J - 2) if (J >= 2 and pj == pi - 1 and s2 == 0) else None],
                              key=("ctxevp", J, pj) if s2 == 1 else None)
                pp_emit(J * 8 + pi + 1)
            w.mark(("ctxevd", J), "vV")
            w.mark(("scJlast", J), "tM")
            cT = ctxT[J % 2]
            for tt in range(4):
                ti = J * 4 + tt
                wb = 0 if tt % 2 == 0 else 2
                for dcn in range(8):
                    for hf in range(2):
                        E("tensor", lambda e, dcn=dcn, hf=hf, tt=tt, wb=wb, cT=cT: e.matmul(
                            out=psf(wb + hf, 512),
                            lhsT=ch(cT, 512)[:, dcn, tt * 128:(tt + 1) * 128],
                            rhs=ch(woT, KB)[:, dcn, hf * 512:(hf + 1) * 512],
                            start=(dcn == 0), stop=(dcn == 7)),
                          sem="tM" if (dcn == 7 and hf == 1) else None,
                          deps=[("ctxevd", J), ("wprep_done", "wo"),
                                ("x1ev", ti - 2) if tt >= 2 else (("x1ev", J * 4 - 1) if (tt == 0 and J >= 1) else None)],
                          key=("womm", ti) if (dcn == 7 and hf == 1) else None)
                xre = xt[2 + ti % 2]
                E("sync", lambda e, ti=ti, xre=xre: e.dma_start(out=xre[:], in_=dx[ti * 128:(ti + 1) * 128, :]),
                  sem="qR%d" % (ti % 2),
                  deps=[("x1st", ti - 2) if ti >= 2 else None,
                        ("l2", "sq", ti - 2) if ti >= 2 else None], key=("xreld", ti))
                E("vector", lambda e, wb=wb, xre=xre: e.tensor_add(out=xre[:], in0=xre[:],
                                                                   in1=PS[:, wb * 512:(wb + 2) * 512]),
                  sem="vV", deps=[("womm", ti), ("xreld", ti)], key=("x1ev", ti))
                E("gpsimd", lambda e, ti=ti, xre=xre: e.dma_start(out=dx1[ti * 128:(ti + 1) * 128, :], in_=xre[:]),
                  sem="qS%d" % (ti % 2), deps=[("x1ev", ti)], key=("x1st", ti))
                ln2_tile(ti, xre)
            w.mark(("Jwo", J), "tM")

        tap("ctx0", lambda: ctxT[0][:], [("ctxevd", 0)])
        tap("x1", lambda: dx1, [("x1st", NT - 1)])

        FD = ("Jwo", NJ - 1)
        X1ALL = [("x1st", NT - 2), ("x1st", NT - 1)]

        for _k in range(len(pp_pieces)):
            pp_emit_n(_k + 1, _k)
        pp_emit_n(len(pp_pieces), len(pp_pieces))
        PP_DONE = [pp_state["laststore"][0], pp_state["laststore"][1]]
        FIDX = {nm: i for i, nm in enumerate(FNAMES)}
        _slot_first = [True, True]

        def load_ffnW(tag, dsrc, slot, free_ev=None):
            deps = [free_ev]
            if _slot_first[slot]:
                deps += PP_DONE
                _slot_first[slot] = False
            E("sync", lambda e, m=FIDX[tag], slot=slot: e.dma_start(
                out=ffnW[slot][:].rearrange("p (a n) -> p a n", n=KB), in_=dewbF[m]),
              sem="qW%d" % slot, deps=deps, key=(tag, "cast", 7))
            w.mark((tag, "cast", 6), "qW%d" % slot)
            w.mark((tag, "done"), "qW%d" % slot)

        qBprev = [None]

        def qb_dma(kind_eng, f, key, extra=()):
            E(kind_eng, f, sem="qB" if kind_eng == "sync" else "qC", deps=[qBprev[0], *extra], key=key)
            qBprev[0] = key

        BIAS_ORDER = ("dbk", "dbq", "dbo", "cbq", "cbk", "cbv")
        for i, nm in enumerate(BIAS_ORDER):
            qb_dma("sync", lambda e, nm=nm: e.dma_start(
                out=bstg[:], in_=dB[nm].rearrange("a (b c) -> (a b) c", c=128)),
                ("fbld", i), extra=[("fbev", i - 1) if i >= 1 else None])
            E("tensor", lambda e: e.transpose(out=PS[0:128, 2 * 512:2 * 512 + 8], in_=bstg[:], identity=identf[0:8, 0:8]),
              sem="tM", deps=[("fbld", i), "identf", FD, ("x1ev", NT - 1), ("x1ev", NT - 2),
                              ("fbev", i - 1) if i >= 1 else None],
              key=("fbtr", i))
            E("vector", lambda e, i=i: e.tensor_copy(out=bcols[:, i * 8:(i + 1) * 8], in_=PS[0:128, 2 * 512:2 * 512 + 8]),
              sem="vV", deps=[("fbtr", i)], key=("fbev", i))
        qb_dma("sync", lambda e: e.dma_start(out=Vctok[:], in_=dsq), ("fsqld",))
        qb_dma("sync", lambda e: e.dma_start(out=ind2[:], in_=dind2), ("find2",))
        qb_dma("sync", lambda e: e.dma_start(out=dbvb[:], in_=dB["dbv"][0:1, :].broadcast_to([128, KB])), ("fdbvb",))
        E("vector", lambda e: e.tensor_copy(out=h_sb[0:16, :], in_=Vctok[:]),
          sem="vV", deps=[("fsqld",)], key="fsqcast")
        for oc in range(8):
            E("tensor", lambda e, oc=oc: e.transpose(out=psb(3, oc * 16, 16), in_=h_sb[0:16, oc * 128:(oc + 1) * 128],
                                                     identity=ident[0:16, 0:16]),
              sem="tM" if oc == 7 else None, deps=["fsqcast", "ident", FD, ("fbtr", 5),
                                                    ("x1ev", NT - 1), ("x1ev", NT - 2)],
              key="fsqtrb" if oc == 7 else None)
        E("vector", lambda e: e.tensor_copy(out=sqT[:], in_=psb(3, 0, 128)), sem="vV",
          deps=["fsqtrb"], key="fsqTev")
        for oc in range(8):
            E("tensor", lambda e, oc=oc: e.transpose(out=PS[0:128, 4 * 512 + oc * 16:4 * 512 + (oc + 1) * 16],
                                                     in_=Vctok[0:16, oc * 128:(oc + 1) * 128], identity=identf[0:16, 0:16]),
              sem="tM" if oc == 7 else None, deps=[("fsqld",), "identf", FD, "fsqtrb"],
              key="fsqtrf" if oc == 7 else None)
        for oc in range(8):
            E("vector", lambda e, oc=oc: e.tensor_scalar_add(
                out=sqTb[:, oc * 16:(oc + 1) * 16], in0=PS[0:128, 4 * 512 + oc * 16:4 * 512 + (oc + 1) * 16],
                scalar1=bcols[:, 16 + oc:17 + oc]),
              sem="vV", deps=["fsqtrf", ("fbev", 2)], key=("fsqb", oc))

        load_ffnW("dWq", dW["dWq"], 0)
        for oc in range(8):
            for dcn in range(8):
                E("tensor", lambda e, oc=oc, dcn=dcn: e.matmul(
                    out=psf(0, 16), lhsT=ffnW[0][:, dcn * KB + oc * 128:dcn * KB + (oc + 1) * 128],
                    rhs=sqT[:, dcn * 16:(dcn + 1) * 16], start=(dcn == 0), stop=(dcn == 7)),
                  sem="tM" if dcn == 7 else None,
                  deps=[("dWq", "done"), "fsqTev", FD, ("qdtev", oc - 1) if oc >= 1 else "fsqtrf"],
                  key=("qdtmm", oc) if dcn == 7 else None)
            E("scalar", lambda e, oc=oc: e.activation(out=QdT[:, oc * 16:(oc + 1) * 16], in_=psf(0, 16),
                                                      func=AF.Identity, bias=bcols[:, 8 + oc:9 + oc], scale=1.0),
              sem="sS", deps=[("qdtmm", oc), ("fbev", 1)], key=("qdtev", oc))
        E("gpsimd", lambda e: e.memset(QdBD[:], 0.0), sem="gP", deps=[("qdtev", 7)], key="qdbdz")
        for pp in range(8):
            E("vector", lambda e, pp=pp: e.tensor_copy(out=QdBD[0:64, pp * 32:pp * 32 + 16],
                                                       in_=QdT[0:64, pp * 16:(pp + 1) * 16]),
              sem="vV", deps=[("qdtev", pp), "qdbdz"])
            E("vector", lambda e, pp=pp: e.tensor_copy(out=QdBD[64:128, pp * 32 + 16:pp * 32 + 32],
                                                       in_=QdT[64:128, pp * 16:(pp + 1) * 16]),
              sem="vV", deps=[("qdtev", pp), "qdbdz"], key=("qdbd", pp))
        for oc in range(8):
            E("scalar", lambda e, oc=oc: e.activation(out=sqsc[:, 0:16], in_=QdT[:, oc * 16:(oc + 1) * 16],
                                                      func=AF.Square, bias=0.0, scale=1.0),
              sem="sS", deps=[("qdtev", oc), ("qdnm", oc - 1) if oc >= 1 else None], key=("qdsq", oc))
            E("tensor", lambda e: e.matmul(out=psf(2, 16, rows=2), lhsT=bonesb[:], rhs=sqsc[:, 0:16],
                                           start=True, stop=True),
              sem="tM", deps=[("qdsq", oc), "bones", ("qdnr", oc - 1) if oc >= 1 else ("fbtr", 5)],
              key=("qdnm", oc))
            E("vector", lambda e, oc=oc: e.reduce_max(out=nrm[0:2, 48 + oc:49 + oc], in_=psf(2, 16, rows=2), axis=AX),
              sem="vV", deps=[("qdnm", oc)], key=("qdnr", oc))

        load_ffnW("dWk", dW["dWk"], 0, free_ev=("qdtmm", 7))

        def xn1_group(Jg, pref, bankpair, start_dep=None, war=None):
            for tt in range(4):
                ti = Jg * 4 + tt
                xbuf = xt[ti % 2]
                E("sync", lambda e, ti=ti, xbuf=xbuf: e.dma_start(out=xbuf[:], in_=dx1[ti * 128:(ti + 1) * 128, :]),
                  sem="qR%d" % (ti % 2),
                  deps=[*(X1ALL if ti < 2 else []), (pref, "ap", ti - 2) if ti >= 2 else start_dep],
                  key=(pref, "ld", ti))
                E("scalar", lambda e, ti=ti, xbuf=xbuf: e.activation(
                    out=xnsc[:], in_=xbuf[:], func=AF.Identity,
                    bias=st_b[:, ti:ti + 1], scale=st_s[:, ti:ti + 1]),
                  sem="sS", deps=[(pref, "ld", ti), ("l2", "rstd", NT - 1), ("l2bias", NT - 1),
                                  (pref, "tr", ti - 1) if ti >= 1 else start_dep],
                  key=(pref, "ap", ti))
                bank = bankpair[tt % 2]
                for dcn in range(8):
                    E("tensor", lambda e, dcn=dcn, bank=bank: e.transpose(
                        out=psb(bank, dcn * 128, 128), in_=xnsc[:, dcn * 128:(dcn + 1) * 128], identity=ident[:]),
                      sem="tM" if dcn == 7 else None,
                      deps=[(pref, "ap", ti), "ident", (pref, "ev", ti - 2) if ti >= 2 else FD,
                            war(Jg, tt) if (war is not None and tt < 2) else None],
                      key=(pref, "tr", ti) if dcn == 7 else None)
                if pref == "i1":
                    E("scalar", lambda e, bank=bank, tt=tt: e.copy(
                        out=ch(xnTg, 512)[:, :, tt * 128:(tt + 1) * 128],
                        in_=psb(bank, 0, 1024).rearrange("p (c n) -> p c n", n=128)),
                      sem="sS", deps=[(pref, "tr", ti), (pref, "con", Jg - 1) if (Jg >= 1 and tt == 0) else
                                      (start_dep if (Jg == 0 and tt == 0) else None)],
                      key=(pref, "ev", ti))
                else:
                    E("vector", lambda e, bank=bank, tt=tt: e.tensor_copy(
                        out=ch(xnTg, 512)[:, :, tt * 128:(tt + 1) * 128],
                        in_=psb(bank, 0, 1024).rearrange("p (c n) -> p c n", n=128)),
                      sem="vV", deps=[(pref, "tr", ti), (pref, "con", Jg - 1) if (Jg >= 1 and tt == 0) else
                                      (start_dep if (Jg == 0 and tt == 0) else None)],
                      key=(pref, "ev", ti))
            w.mark((pref, "grp", Jg), "sS" if pref == "i1" else "vV")

        for Jg in range(NJ):
            xn1_group(Jg, "f0", (6, 7), start_dep=FD)
            for kc in range(8):
                bank = kc % 2
                for dcn in range(8):
                    E("tensor", lambda e, kc=kc, dcn=dcn, bank=bank: e.matmul(
                        out=psf(bank, 512), lhsT=ffnW[0][:, dcn * KB + kc * 128:dcn * KB + (kc + 1) * 128],
                        rhs=ch(xnTg, 512)[:, dcn, :], start=(dcn == 0), stop=(dcn == 7)),
                      sem="tM" if dcn == 7 else None,
                      deps=[("f0", "grp", Jg), ("dWk", "done"),
                            ("f0ks", Jg, kc - 2) if kc >= 2 else (("f0ks", Jg - 1, 6 + kc) if Jg >= 1 else None)],
                      key=("f0km", Jg, kc) if dcn == 7 else None)
                E("scalar", lambda e, bank=bank: e.activation(out=sqsc[:], in_=psf(bank, 512),
                                                              func=AF.Square, bias=0.0, scale=1.0),
                  sem="sS", deps=[("f0km", Jg, kc), ("f0knm", Jg, kc - 1) if kc >= 1 else
                                  (("f0knm", Jg - 1, 7) if Jg >= 1 else ("qdnm", 7))],
                  key=("f0ks", Jg, kc))
                E("tensor", lambda e: e.matmul(out=psf(2, 512, rows=2), lhsT=bonesb[:], rhs=sqsc[:],
                                               start=True, stop=True),
                  sem="tM", deps=[("f0ks", Jg, kc), "bones",
                                  ("f0knr", Jg, kc - 1) if kc >= 1 else (("f0knr", Jg - 1, 7) if Jg >= 1 else ("qdnr", 7))],
                  key=("f0knm", Jg, kc))
                E("vector", lambda e, kc=kc: e.reduce_max(out=nrm[0:2, 56 + kc:57 + kc], in_=psf(2, 512, rows=2), axis=AX),
                  sem="vV", deps=[("f0knm", Jg, kc), ("f0knu", Jg - 1, kc) if Jg >= 1 else "qnz"],
                  key=("f0knr", Jg, kc))
                E("vector", lambda e, kc=kc: e.tensor_max(out=nrm[0:2, 40 + kc:41 + kc],
                                                          in0=nrm[0:2, 40 + kc:41 + kc], in1=nrm[0:2, 56 + kc:57 + kc]),
                  sem="vV", deps=[("f0knr", Jg, kc), "qnz2"], key=("f0knu", Jg, kc))
            w.mark(("f0", "con", Jg), "tM")

        E("scalar", lambda e: e.activation(out=nrm[0:2, 48:56], in_=nrm[0:2, 48:56], func=AF.Sqrt, bias=0.0, scale=1.0),
          sem="sS", deps=[("qdnr", 7), ("f0ks", NJ - 1, 7)], key="dqsqrt")
        E("scalar", lambda e: e.activation(out=nrm[0:2, 40:48], in_=nrm[0:2, 40:48], func=AF.Sqrt, bias=0.0, scale=1.0),
          sem="sS", deps=[("f0knu", NJ - 1, 7), "dqsqrt"], key="dksqrt")
        qb_dma("gpsimd", lambda e: e.dma_start(out=qnacc[0:1, 96:104], in_=nrm[0:1, 48:56]), ("dnf", 0), extra=["dqsqrt", "dksqrt"])
        qb_dma("gpsimd", lambda e: e.dma_start(out=qnacc[0:1, 104:112], in_=nrm[1:2, 48:56]), ("dnf", 1))
        qb_dma("gpsimd", lambda e: e.dma_start(out=qnacc[0:1, 112:120], in_=nrm[0:1, 40:48]), ("dnf", 2))
        qb_dma("gpsimd", lambda e: e.dma_start(out=qnacc[0:1, 120:128], in_=nrm[1:2, 40:48]), ("dnf", 3))
        for h in range(16):
            qc_ = 96 + (h % 2) * 8 + h // 2
            kc_ = 112 + (h % 2) * 8 + h // 2
            E("vector", lambda e, qc_=qc_, kc_=kc_, h=h: e.tensor_tensor(
                out=qnacc[0:1, 68 + h:69 + h], in0=qnacc[0:1, qc_:qc_ + 1], in1=qnacc[0:1, kc_:kc_ + 1], op=ALU.mult),
              sem="vV", deps=[("dnf", 3)], key=("dmh0", h))
            E("vector", lambda e, h=h: e.tensor_scalar_mul(out=qnacc[0:1, 68 + h:69 + h],
                                                           in0=qnacc[0:1, 68 + h:69 + h], scalar1=-SCALE * 1.02),
              sem="vV", deps=[("dmh0", h)], key=("dmh", h))
        qb_dma("gpsimd", lambda e: e.dma_start(out=mrow2[0:1, :], in_=qnacc[0:1, 68:84].rearrange("a (b c) -> a b c", c=2)[:, :, 0],
                                               allow_slow_non_contiguous=True),
               ("dmr", 0), extra=[("dmh", 15)])
        qb_dma("gpsimd", lambda e: e.dma_start(out=mrow2[1:2, :], in_=qnacc[0:1, 68:84].rearrange("a (b c) -> a b c", c=2)[:, :, 1],
                                               allow_slow_non_contiguous=True),
               ("dmr", 1))
        E("tensor", lambda e: e.matmul(out=psf(4, 8), lhsT=ind2[:], rhs=mrow2[:], start=True, stop=True),
          sem="tM", deps=[("dmr", 1), ("find2",), ("f0", "con", NJ - 1)], key="dbias_mm")
        E("vector", lambda e: e.tensor_copy(out=bias16d_t[:], in_=psf(4, 8)), sem="vV",
          deps=["dbias_mm"], key="dbias")

        load_ffnW("dWv", dW["dWv"], 1)
        E("gpsimd", lambda e: e.memset(VdG[0][:], 1.0), sem="gP", deps=[("ctxevd", NJ - 1)], key=("vdz", 0))
        E("gpsimd", lambda e: e.memset(VdG[1][:], 1.0), sem="gP", key=("vdz", 1))
        E("gpsimd", lambda e: e.memset(l_run[:], 0.0), sem="gP", key="lrz2")
        E("gpsimd", lambda e: e.memset(od_acc[:], 0.0), sem="gP", key="odz2")
        SLOT_BANK = [(4, 0), (5, 0), (4, 64), (5, 64)]
        for Jg in range(NJ):
            xn1_group(Jg, "f1", (2, 3),
                      start_dep=("f0", "ap", NT - 1) if Jg == 0 else None,
                      war=(lambda Jg_, tt: ("pdev", Jg_ - 1, tt % 2) if Jg_ >= 1 else ("f0", "con", NJ - 1)))
            for kc in range(8):
                bank = kc % 2
                for dcn in range(8):
                    E("tensor", lambda e, kc=kc, dcn=dcn, bank=bank: e.matmul(
                        out=psf(bank, 512), lhsT=ffnW[0][:, dcn * KB + kc * 128:dcn * KB + (kc + 1) * 128],
                        rhs=ch(xnTg, 512)[:, dcn, :], start=(dcn == 0), stop=(dcn == 7)),
                      sem="tM" if dcn == 7 else None,
                      deps=[("f1", "grp", Jg), ("dWk", "done"),
                            ("f1ke", Jg, kc - 2) if kc >= 2 else (("vdev", Jg - 1) if Jg >= 1 else ("f0", "con", NJ - 1))],
                      key=("f1km", Jg, kc) if dcn == 7 else None)
                E("scalar", lambda e, kc=kc, bank=bank: e.activation(
                    out=KdTg[:, kc * 512:(kc + 1) * 512], in_=psf(bank, 512),
                    func=AF.Identity, bias=bcols[:, 0 + kc:1 + kc], scale=1.0),
                  sem="sS", deps=[("f1km", Jg, kc), ("fbev", 0),
                                  ("sdmm", Jg - 1, kc) if Jg >= 1 else None],
                  key=("f1ke", Jg, kc))
            for tt in range(4):
                ti = Jg * 4 + tt
                for dcn in range(8):
                    for hf in range(2):
                        E("tensor", lambda e, dcn=dcn, hf=hf, tt=tt: e.matmul(
                            out=psf(hf, 512), lhsT=ch(xnTg, 512)[:, dcn, tt * 128:(tt + 1) * 128],
                            rhs=ffnW[1][:, dcn * KB + hf * 512:dcn * KB + (hf + 1) * 512],
                            start=(dcn == 0), stop=(dcn == 7)),
                          sem="tM" if (dcn == 7 and hf == 1) else None,
                          deps=[("f1", "grp", Jg), ("dWv", "done"), ("f1ke", Jg, 7),
                                ("vde", Jg, tt - 1) if tt >= 1 else None],
                          key=("vdm", Jg, tt) if (dcn == 7 and hf == 1) else None)
                E("vector", lambda e, tt=tt, Jg=Jg: e.tensor_tensor(
                    out=VdG[Jg % 2][:].rearrange("p (t g n) -> p t g n", g=16, n=65)[:, tt, :, 0:64],
                    in0=PS[:, 0:1024].rearrange("p (g n) -> p g n", n=64),
                    in1=dbvb[:].rearrange("p (g n) -> p g n", n=64), op=ALU.add),
                  sem="vV", deps=[("vdm", Jg, tt), ("fdbvb",), ("vdz", Jg % 2),
                                  ("odup", Jg - 2) if Jg >= 2 else None],
                  key=("vde", Jg, tt))
            w.mark(("vdev", Jg), "vV")
            w.mark(("f1", "con", Jg), "tM")
            for pp in range(8):
                bank, b0 = SLOT_BANK[pp % 4]
                E("tensor", lambda e, pp=pp, bank=bank, b0=b0: e.matmul(
                    out=PS[b0:b0 + 32, bank * 512:(bank + 1) * 512],
                    lhsT=QdBD[:, pp * 32:(pp + 1) * 32], rhs=KdTg[:, pp * 512:(pp + 1) * 512],
                    start=True, stop=True, tile_position=(0, b0) if b0 else None),
                  sem="tM", deps=[("f1ke", Jg, pp), ("qdbd", pp),
                                  "dbias" if (Jg == 0 and pp <= 1) else None,
                                  ("sdex", Jg, pp - 2) if pp >= 2 else (("sdex", Jg - 1, pp + 6) if Jg >= 1 else None)],
                  key=("sdmm", Jg, pp))
                E("scalar", lambda e, pp=pp, bank=bank, b0=b0: e.activation(
                    out=PdB[pp % 4][:], in_=PS[b0:b0 + 32, bank * 512:(bank + 1) * 512],
                    func=AF.Exp, bias=bias16d_t[b0:b0 + 32, pp:pp + 1], scale=SCALE),
                  sem="sS", deps=[("sdmm", Jg, pp), "dbias",
                                  ("pdtp", Jg, pp - 4) if pp >= 4 else
                                  (("pdtp", Jg - 1, pp + 4) if Jg >= 1 else None),
                                  ("sdls", Jg, pp - 4) if pp >= 4 else
                                  (("sdls", Jg - 1, pp + 4) if Jg >= 1 else None)],
                  key=("sdex", Jg, pp))
                E("vector", lambda e, pp=pp: e.reduce_sum(out=lpart[:, 0:1], in_=PdB[pp % 4][:], axis=AX),
                  sem="vV", deps=[("sdex", Jg, pp), ("sdlu", Jg, pp - 1) if pp >= 1 else
                                  (("sdlu", Jg - 1, 7) if Jg >= 1 else "lrz2")],
                  key=("sdls", Jg, pp))
                E("vector", lambda e, pp=pp: e.tensor_add(out=l_run[:, pp:pp + 1], in0=l_run[:, pp:pp + 1],
                                                          in1=lpart[:, 0:1]),
                  sem="vV", deps=[("sdls", Jg, pp)], key=("sdlu", Jg, pp))
                for it in range(4):
                    tb = 2 + it // 2
                    E("tensor", lambda e, it=it, pp=pp, tb=tb: e.transpose(
                        out=psb(tb, (it % 2) * 256 + pp * 32, 32),
                        in_=PdB[pp % 4][:, it * 128:(it + 1) * 128], identity=ident[0:32, 0:32]),
                      sem="tM" if it == 3 else None,
                      deps=[("sdex", Jg, pp),
                            ("f1", "ev", Jg * 4 + 2), ("f1", "ev", Jg * 4 + 3),
                            ("pdev", Jg - 1, 0) if (Jg >= 1 and pp == 0) else None,
                            ("pdev", Jg - 1, 1) if (Jg >= 1 and pp == 0) else None],
                      key=("pdtp", Jg, pp) if it == 3 else None)
            for half in range(2):
                E("vector", lambda e, half=half: e.tensor_copy(
                    out=PdTG[:, half * 512:(half + 1) * 512], in_=psb(2 + half, 0, 512)),
                  sem="vV", deps=[("pdtp", Jg, 7)], key=("pdev", Jg, half))
            w.mark(("pdtrd", Jg), "vV")
            for h in range(16):
                obank = 6 + h // 8
                for it in range(4):
                    E("tensor", lambda e, h=h, it=it, Jg=Jg, obank=obank: e.matmul(
                        out=PS[0:16, obank * 512 + (h % 8) * 64:obank * 512 + (h % 8) * 64 + 64],
                        lhsT=PdTG[:, (it // 2) * 512 + (it % 2) * 256 + (h // 2) * 32 + (h % 2) * 16:
                                   (it // 2) * 512 + (it % 2) * 256 + (h // 2) * 32 + (h % 2) * 16 + 16],
                        rhs=VdG[Jg % 2][:].rearrange("p (t g n) -> p t g n", g=16, n=65)[:, it, h, 0:64],
                        start=(it == 0), stop=(it == 3)),
                      sem="tM" if it == 3 else None,
                      deps=[("pdev", Jg, it // 2), ("vde", Jg, it),
                            ("odup", Jg - 1) if (Jg >= 1 and h == 0 and it == 0) else None],
                      key=("pvod", Jg, h) if it == 3 else None)
            E("vector", lambda e: e.tensor_add(out=od_acc[:], in0=od_acc[:], in1=PS[0:16, 6 * 512:8 * 512]),
              sem="vV", deps=[("pvod", Jg, 7), ("pvod", Jg, 15), "odz2",
                              ("odup", Jg - 1) if Jg >= 1 else None],
              key=("odup", Jg))

        E("vector", lambda e: e.reciprocal(out=rl32[:], in_=l_run[:]),
          sem="vV", deps=[("sdlu", NJ - 1, 7)], key="drl")
        qb_dma("gpsimd", lambda e: e.dma_start(out=rlB[:], in_=rl32[16:32, :]), ("drlB",), extra=["drl"])
        for h in range(16):
            pp, s2 = h // 2, h % 2
            E("vector", lambda e, h=h, pp=pp, s2=s2: e.tensor_scalar_mul(
                out=od_acc[:, h * 64:(h + 1) * 64], in0=od_acc[:, h * 64:(h + 1) * 64],
                scalar1=(rl32 if s2 == 0 else rlB)[0:16, pp:pp + 1]),
              sem="vV", deps=[("odup", NJ - 1), "drl", ("drlB",)], key=("odn", h))
        E("vector", lambda e: e.tensor_copy(out=h_sb[0:16, :], in_=od_acc[:]),
          sem="vV", deps=[("odn", 15), "fsqTev", ("qdtev", 7)], key="odcast")
        for oc in range(8):
            E("tensor", lambda e, oc=oc: e.transpose(out=psb(3, oc * 16, 16), in_=h_sb[0:16, oc * 128:(oc + 1) * 128],
                                                     identity=ident[0:16, 0:16]),
              sem="tM" if oc == 7 else None,
              deps=["odcast", "ident", ("pdev", NJ - 1, 0), ("pdev", NJ - 1, 1)],
              key="odtr" if oc == 7 else None)
        E("vector", lambda e: e.tensor_copy(out=odTb[:], in_=psb(3, 0, 128)), sem="vV",
          deps=["odtr"], key="odtev")
        load_ffnW("dWo", dW["dWo"], 0, free_ev=("f1km", NJ - 1, 7))
        for oc in range(8):
            for dcn in range(8):
                E("tensor", lambda e, oc=oc, dcn=dcn: e.matmul(
                    out=psf(0, 16), lhsT=ffnW[0][:, dcn * KB + oc * 128:dcn * KB + (oc + 1) * 128],
                    rhs=odTb[:, dcn * 16:(dcn + 1) * 16], start=(dcn == 0), stop=(dcn == 7)),
                  sem="tM" if dcn == 7 else None,
                  deps=[("dWo", "done"), "odtev", ("odwe", oc - 1) if oc >= 1 else None],
                  key=("odwm", oc) if dcn == 7 else None)
            E("vector", lambda e, oc=oc: e.scalar_tensor_tensor(
                out=oT[:, oc * 16:(oc + 1) * 16], in0=psf(0, 16), scalar=1.0,
                in1=sqTb[:, oc * 16:(oc + 1) * 16], op0=ALU.mult, op1=ALU.add),
              sem="vV", deps=[("odwm", oc), ("fsqb", oc)], key=("odwe", oc))
        for oc in range(8):
            E("tensor", lambda e, oc=oc: e.transpose(out=PS[0:16, 2 * 512 + oc * 128:2 * 512 + (oc + 1) * 128],
                                                     in_=oT[:, oc * 16:(oc + 1) * 16], identity=identf[:]),
              sem="tM" if oc == 7 else None,
              deps=[("odwe", oc), "identf", ("odwm", 7)],
              key="otokr" if oc == 7 else None)
        E("vector", lambda e: e.tensor_copy(out=o_tok[:], in_=PS[0:16, 2 * 512:2 * 512 + 1024]),
          sem="vV", deps=["otokr"], key="otok")
        qb_dma("gpsimd", lambda e: e.dma_start(out=dcc1i.ap(), in_=o_tok[:]), ("cc1w",), extra=["otok"])
        E("gpsimd", lambda e: e.collective_compute(
            "AllGather", ALU.bypass, replica_groups=[list(range(NCORES))],
            ins=[dcc1i.ap().opt()], outs=[dcc1o.ap().opt()]),
          sem="cc", inc=1, deps=[("cc1w",)], key="cc1")
        qb_dma("gpsimd", lambda e: e.dma_start(out=Oall[:], in_=dcc1o.ap()), ("oall",),
               extra=["cc1", ("odwm", 7)])

        if upto == "F":
            for ti in range(NT):
                E("sync", lambda e, ti=ti: e.dma_start(out=xt[ti % 2][:], in_=dx1[ti * 128:(ti + 1) * 128, :]),
                  sem="qF%d" % (ti % 2),
                  deps=[("x1st", NT - 2), ("x1st", NT - 1), ("f1", "ap", NT - 1),
                        ("fdbg", ti - 2) if ti >= 2 else None], key=("fdbgl", ti))
                E("gpsimd", lambda e, ti=ti: e.dma_start(out=dout[ti * 128:(ti + 1) * 128, :], in_=xt[ti % 2][:]),
                  sem="qO" if ti % 2 == 0 else "qO2", deps=[("fdbgl", ti)], key=("fdbg", ti))
            w.barrier_wait("gpsimd", [("fdbg", NT - 2), ("fdbg", NT - 1), ("oall",)])
            w.barrier_wait("gpsimd", [("tap", nm) for nm in dtaps])
            return

        for rep in range(8):
            qb_dma("sync", lambda e, rep=rep: e.dma_start(out=Gt[16 * rep:16 * rep + 16, :], in_=dlng),
                   ("gld", rep), extra=PP_DONE if rep == 0 else [])
        for rep in range(8):
            qb_dma("sync", lambda e, rep=rep: e.dma_start(out=Bt[16 * rep:16 * rep + 16, :], in_=dlnb),
                   ("bld", rep), extra=[("odwm", 7)] if rep == 0 else [])
        ln_stats(100, Oall, "lh", lnst[:, 8:9], lnst[:, 9:10], extra0=[("oall",), ("l2", "rstd", NT - 1)])
        E("scalar", lambda e: e.activation(out=sqf32[:], in_=Oall[:], func=AF.Identity,
                                           bias=lnst[:, 9:10], scale=lnst[:, 8:9]),
          sem="sS", deps=[("lhbias", 100)], key="hap")
        E("vector", lambda e: e.tensor_mul(out=sqf32[:], in0=sqf32[:], in1=Gt[:]),
          sem="vV", deps=["hap", ("gld", 7)], key="hg")
        E("vector", lambda e: e.tensor_tensor(out=h_sb[:], in0=sqf32[:], in1=Bt[:], op=ALU.add),
          sem="vV", deps=["hg", ("bld", 7), "odcast"], key="hb")
        qb_dma("sync", lambda e: e.dma_start(out=esel_sb[:], in_=desel), ("eselld",))
        E("vector", lambda e: e.tensor_copy(out=esel_bf[:], in_=esel_sb[:]),
          sem="vV", deps=[("eselld",)], key="eselbf")
        for half in range(2):
            E("tensor", lambda e, half=half: e.matmul(
                out=PS[0:16, 5 * 512 + half * 512:5 * 512 + (half + 1) * 512],
                lhsT=esel_bf[:], rhs=h_sb[:, half * 512:(half + 1) * 512], start=True, stop=True),
              sem="tM" if half == 1 else None,
              deps=["hb", "eselbf", "otokr", ("f0", "tr", NT - 1)],
              key="hselmm" if half == 1 else None)
        E("vector", lambda e: e.tensor_copy(out=hsel[:], in_=PS[0:16, 5 * 512:5 * 512 + 1024]),
          sem="vV", deps=["hselmm"], key="hselev")
        for dcn in range(8):
            E("tensor", lambda e, dcn=dcn: e.transpose(out=psb(6, dcn * 16, 16),
                                                       in_=hsel[0:16, dcn * 128:(dcn + 1) * 128],
                                                       identity=ident[0:16, 0:16]),
              sem="tM" if dcn == 7 else None,
              deps=["hselev", "ident", "hselmm"],
              key="htr" if dcn == 7 else None)
        E("vector", lambda e: e.tensor_copy(out=hTe[:], in_=psb(6, 0, 128)), sem="vV",
          deps=["htr"], key="hTev")
        for i, (dsrc, dst, wdt) in enumerate(((deb1, b1T, eb1c), (debg, bgT, ebgc), (deb2, b2T, eb2c))):
            nr = 32 if i < 2 else 8
            for le in range(2):
                qb_dma("sync", lambda e, dsrc=dsrc, le=le, nr=nr, wdt=wdt: e.dma_start(
                    out=wdt[0:nr, le * 128:(le + 1) * 128],
                    in_=dsrc[le].rearrange("(a b) -> a b", b=128)),
                    ("ebld", i, le))
                E("tensor", lambda e, le=le, nr=nr, wdt=wdt: e.transpose(
                    out=PS[0:128, 3 * 512 + le * 32:3 * 512 + le * 32 + 32][:, 0:nr],
                    in_=wdt[0:nr, le * 128:(le + 1) * 128],
                    identity=(identf[0:32, 0:32] if nr == 32 else identf[0:8, 0:8])),
                  sem="tM", deps=[("ebld", i, le), "identf",
                                  "htr" if (i == 0 and le == 0) else (("ebev", i - 1, 1) if (i >= 1 and le == 0) else None),
                                  ("ebev", i, le - 1) if le >= 1 else None],
                  key=("ebtr", i, le))
                E("vector", lambda e, i=i, le=le, nr=nr, dst=dst: e.tensor_copy(
                    out=dst[:, le * nr:(le + 1) * nr],
                    in_=PS[0:128, 3 * 512 + le * 32:3 * 512 + le * 32 + nr]),
                  sem="vV", deps=[("ebtr", i, le)], key=("ebev", i, le))
        for le in range(2):
            for q4 in range(4):
                for mat, (dsrc, wbuf, bT, psbank, osb) in enumerate(
                        ((deW1, W1bfh, b1T, (0 if q4 % 2 == 0 else 3), u_sb),
                         (deWg, Wgbfh, bgT, (1 if q4 % 2 == 0 else 7), g_sb))):
                    tagq = ("ew", le, q4, mat)
                    if q4 >= 1:
                        lddeps = [("ew", le, q4 - 1, mat, "mmz")]
                    elif le >= 1:
                        lddeps = [("w2", 0, 2 + mat, "mmz"), ("ew", 0, 3, mat, "mmz")]
                    elif mat == 0:
                        lddeps = [("odwm", 7), ("Jwo", NJ - 1)] + PP_DONE
                    else:
                        lddeps = [("f1", "ap", NT - 1), ("x1st", NT - 1), ("x1st", NT - 2),
                                  ("vdm", NJ - 1, 3)] + PP_DONE
                    E("sync", lambda e, g=(mat * 2 + le) * 4 + q4, wbuf=wbuf: e.dma_start(
                        out=wbuf[:].rearrange("p (a n) -> p a n", n=KB), in_=dewb1[g]),
                      sem="qW%d" % mat, deps=lddeps, key=(*tagq, "cast", 7))
                    for fc8 in range(8):
                        fc = q4 * 8 + fc8
                        for dcn in range(8):
                            E("tensor", lambda e, fc8=fc8, fc=fc, dcn=dcn, wbuf=wbuf, psbank=psbank, le=le: e.matmul(
                                out=PS[:, psbank * 512 + fc * 8:psbank * 512 + fc * 8 + 8],
                                lhsT=wbuf[:, dcn * KB + fc8 * 128:dcn * KB + (fc8 + 1) * 128],
                                rhs=hTe[:, dcn * 16 + le * 8:dcn * 16 + le * 8 + 8],
                                start=(dcn == 0), stop=(dcn == 7)),
                              sem="tM" if dcn == 7 else None,
                              deps=[(*tagq, "cast", 7), "hTev",
                                    ("exsil", le, q4 - 2, mat) if (q4 >= 2 and fc8 == 0) else None,
                                    (("exsil", le - 1, 2 + q4, mat) if (le >= 1 and q4 < 2 and fc8 == 0) else None),
                                    ("ebev", 2, 1) if (le == 0 and q4 < 2 and fc8 == 0) else None],
                              key=(*tagq, "mm", fc8) if dcn == 7 else None)
                    w.mark((*tagq, "mmz"), "tM")
                    for fc8 in range(8):
                        fc = q4 * 8 + fc8
                        E("scalar", lambda e, fc=fc, fc8=fc8, psbank=psbank, bT=bT, le=le, osb=osb: e.activation(
                            out=osb[:, fc * 8:fc * 8 + 8], in_=PS[:, psbank * 512 + fc * 8:psbank * 512 + fc * 8 + 8],
                            func=AF.Silu, bias=bT[:, le * 32 + fc:le * 32 + fc + 1], scale=1.0),
                          sem="sS", deps=[(*tagq, "mm", 7), ("ebev", mat, 1)],
                          key=("exsil", le, q4, mat) if fc8 == 7 else None)
            for q4 in range(4):
                E("vector", lambda e, q4=q4: e.tensor_mul(out=ug[:, q4 * 64:(q4 + 1) * 64],
                                                          in0=u_sb[:, q4 * 64:(q4 + 1) * 64],
                                                          in1=g_sb[:, q4 * 64:(q4 + 1) * 64]),
                  sem="vV", deps=[("exsil", le, q4, 0), ("exsil", le, q4, 1)], key=("ugm", le, q4))
            for wv2 in range(4):
                tagw = ("w2", le, wv2)
                gwv = le * 4 + wv2
                wbuf2 = W2bfw if gwv % 2 == 0 else ffnW[0]
                if gwv >= 2:
                    ple, pwv2 = divmod(gwv - 2, 4)
                    lddeps = [("w2", ple, pwv2, "mmz")]
                    if gwv in (4, 5):
                        lddeps.append(("ew", 1, 3, gwv - 4, "mmz"))
                elif gwv == 0:
                    lddeps = [("vdm", NJ - 1, 3), ("ew", 0, 3, 0, "mmz")]
                else:
                    lddeps = [("odwm", 7), "hap", "hg", "hb", ("ew", 0, 3, 1, "mmz")]
                E("sync", lambda e, g2=gwv, wbuf2=wbuf2: e.dma_start(
                    out=wbuf2[:].rearrange("p (a n) -> p a n", n=KB), in_=dewb2[g2]),
                  sem="qW%d" % (gwv % 2), deps=lddeps, key=(*tagw, "cast", 7))
                yb = 2 if wv2 % 2 == 0 else 5
                for dq8 in range(8):
                    for fcw in range(8):
                        E("tensor", lambda e, dq8=dq8, fcw=fcw, wv2=wv2, le=le, yb=yb, wbuf2=wbuf2: e.matmul(
                            out=PS[:, yb * 512 + dq8 * 8:yb * 512 + dq8 * 8 + 8],
                            lhsT=wbuf2[:, fcw * KB + dq8 * 128:fcw * KB + (dq8 + 1) * 128],
                            rhs=ug[:, (wv2 * 8 + fcw) * 8:(wv2 * 8 + fcw) * 8 + 8],
                            start=(fcw == 0), stop=(fcw == 7)),
                          sem="tM" if fcw == 7 else None,
                          deps=[(*tagw, "cast", 7), ("ugm", le, wv2) if dq8 == 0 else None,
                                ("ugm", le, 3) if dq8 == 0 else None,
                                ("yac", le, wv2 - 2) if (wv2 >= 2 and dq8 == 0) else None,
                                (("yac", le - 1, 2 + wv2) if (le >= 1 and wv2 < 2 and dq8 == 0) else None),
                                (("ytr", le - 1) if (le >= 1 and wv2 == 0 and dq8 == 0) else None),
                                ("selmm" if False else None)],
                          key=(*tagw, "mm", dq8) if fcw == 7 else None)
                w.mark((*tagw, "mmz"), "tM")
                if wv2 == 0:
                    E("vector", lambda e, yb=yb: e.tensor_copy(out=y_acc[:], in_=PS[:, yb * 512:yb * 512 + 64]),
                      sem="vV", deps=[(*tagw, "mm", 7)], key=("yac", le, 0))
                else:
                    E("vector", lambda e, yb=yb: e.tensor_add(out=y_acc[:], in0=y_acc[:], in1=PS[:, yb * 512:yb * 512 + 64]),
                      sem="vV", deps=[(*tagw, "mm", 7), ("yac", le, wv2 - 1)], key=("yac", le, wv2))
            for oc in range(8):
                E("vector", lambda e, oc=oc, le=le: e.tensor_scalar_add(
                    out=y_acc[:, oc * 8:(oc + 1) * 8], in0=y_acc[:, oc * 8:(oc + 1) * 8],
                    scalar1=b2T[:, le * 8 + oc:le * 8 + oc + 1]),
                  sem="vV", deps=[("yac", le, 3), ("ebev", 2, 1)], key=("yb2", le, oc))
            for oc in range(8):
                E("tensor", lambda e, oc=oc, le=le: e.transpose(
                    out=PS[0:8, 4 * 512 + oc * 128:4 * 512 + (oc + 1) * 128],
                    in_=y_acc[:, oc * 8:(oc + 1) * 8], identity=identf[:]),
                  sem="tM" if oc == 7 else None,
                  deps=[("yb2", le, oc), "identf", ("w2", le, 3, "mmz"),
                        ("ytev", le - 1) if (le >= 1 and oc == 0) else None],
                  key=("ytr", le) if oc == 7 else None)
            E("vector", lambda e, le=le: e.tensor_copy(out=Yle[le][0:8, :],
                                                       in_=PS[0:8, 4 * 512:4 * 512 + 1024]),
              sem="vV", deps=[("ytr", le)], key=("ytev", le))
        qb_dma("gpsimd", lambda e: e.dma_start(out=dcc2i.ap()[0:8, :], in_=Yle[0][0:8, :]), ("cc2w0",),
               extra=[("ytev", 0), ("ytev", 1)])
        qb_dma("gpsimd", lambda e: e.dma_start(out=dcc2i.ap()[8:16, :], in_=Yle[1][0:8, :]), ("cc2w",))
        E("gpsimd", lambda e: e.collective_compute(
            "AllGather", ALU.bypass, replica_groups=[list(range(NCORES))],
            ins=[dcc2i.ap().opt()], outs=[dcc2o.ap().opt()]),
          sem="cc", inc=1, deps=[("cc2w",)], key="cc2")
        qb_dma("gpsimd", lambda e: e.dma_start(out=Yall[:], in_=dcc2o.ap()), ("yall",), extra=["cc2"])
        qb_dma("sync", lambda e: e.dma_start(out=bsel_sb[:], in_=dbsel), ("bselld",))
        qb_dma("sync", lambda e: e.dma_start(out=cbob[:], in_=dB["cbo"][0:1, :].broadcast_to([128, KB])),
               ("cbobld",), extra=[("sdex", NJ - 1, 7)])
        for half in range(2):
            E("tensor", lambda e, half=half: e.matmul(
                out=PS[0:16, 4 * 512 + half * 512:4 * 512 + (half + 1) * 512],
                lhsT=bsel_sb[:], rhs=Yall[:, half * 512:(half + 1) * 512], start=True, stop=True),
              sem="tM" if half == 1 else None,
              deps=[("yall",), ("bselld",), ("ytr", 1), ("ytev", 1)],
              key="selmm" if half == 1 else None)
        E("vector", lambda e: e.tensor_add(out=o2_tok[:], in0=o_tok[:], in1=PS[0:16, 4 * 512:4 * 512 + 1024]),
          sem="vV", deps=["selmm", "otok"], key="o2t")

        if upto == "H":
            tap("o2tok", lambda: o2_tok[:], ["o2t"])
            for ti in range(NT):
                E("sync", lambda e, ti=ti: e.dma_start(out=xt[ti % 2][:], in_=dx1[ti * 128:(ti + 1) * 128, :]),
                  sem="qF%d" % (ti % 2),
                  deps=[("x1st", NT - 2), ("x1st", NT - 1), ("ew", 1, 3, 1, "mmz"), "o2t",
                        ("fdbg", ti - 2) if ti >= 2 else None], key=("fdbgl", ti))
                E("gpsimd", lambda e, ti=ti: e.dma_start(out=dout[ti * 128:(ti + 1) * 128, :], in_=xt[ti % 2][:]),
                  sem="qO" if ti % 2 == 0 else "qO2", deps=[("fdbgl", ti)], key=("fdbg", ti))
            w.barrier_wait("gpsimd", [("fdbg", NT - 2), ("fdbg", NT - 1), ("oall",), ("yall",)])
            w.barrier_wait("gpsimd", [("tap", nm) for nm in dtaps])
            return

        for oc in range(8):
            E("tensor", lambda e, oc=oc: e.transpose(out=PS[0:128, 5 * 512 + oc * 16:5 * 512 + (oc + 1) * 16],
                                                     in_=o2_tok[0:16, oc * 128:(oc + 1) * 128],
                                                     identity=identf[0:16, 0:16]),
              sem="tM" if oc == 7 else None, deps=["o2t", "identf", "selmm"],
              key="o2tr" if oc == 7 else None)
        E("vector", lambda e: e.tensor_copy(out=o2T[:], in_=PS[0:128, 5 * 512:5 * 512 + 128]),
          sem="vV", deps=["o2tr"], key="o2Tev")
        E("vector", lambda e: e.tensor_copy(out=o2Tb[:], in_=o2T[:]), sem="vV",
          deps=["o2Tev"], key="o2Tb")
        load_ffnW("cWk", dW["cWk"], 1, free_ev=("w2", 1, 3, "mmz"))
        for oc in range(8):
            kb_ = oc % 2
            for dcn in range(8):
                E("tensor", lambda e, oc=oc, dcn=dcn, kb_=kb_: e.matmul(
                    out=psf(kb_, 16), lhsT=ffnW[1][:, dcn * KB + oc * 128:dcn * KB + (oc + 1) * 128],
                    rhs=o2Tb[:, dcn * 16:(dcn + 1) * 16], start=(dcn == 0), stop=(dcn == 7)),
                  sem="tM" if dcn == 7 else None,
                  deps=[("cWk", "done"), "o2Tb", ("kcev", oc - 2) if oc >= 2 else None],
                  key=("kcmm", oc) if dcn == 7 else None)
            E("scalar", lambda e, oc=oc, kb_=kb_: e.activation(out=KcT[:, oc * 16:(oc + 1) * 16], in_=psf(kb_, 16),
                                                      func=AF.Identity, bias=bcols[:, 32 + oc:33 + oc], scale=1.0),
              sem="sS", deps=[("kcmm", oc), ("fbev", 4)], key=("kcev", oc))
        load_ffnW("cWv", dW["cWv"], 1, free_ev=("kcmm", 7))
        for oc in range(8):
            kb_ = oc % 2
            for dcn in range(8):
                E("tensor", lambda e, oc=oc, dcn=dcn, kb_=kb_: e.matmul(
                    out=psf(kb_, 16), lhsT=ffnW[1][:, dcn * KB + oc * 128:dcn * KB + (oc + 1) * 128],
                    rhs=o2Tb[:, dcn * 16:(dcn + 1) * 16], start=(dcn == 0), stop=(dcn == 7)),
                  sem="tM" if dcn == 7 else None,
                  deps=[("cWv", "done"), "o2Tb", ("vcev", oc - 2) if oc >= 2 else ("kcev", 7)],
                  key=("vcmm", oc) if dcn == 7 else None)
            E("vector", lambda e, oc=oc, kb_=kb_: e.tensor_scalar_add(out=VcT[:, oc * 16:(oc + 1) * 16], in0=psf(kb_, 16),
                                                             scalar1=bcols[:, 40 + oc:41 + oc]),
              sem="vV", deps=[("vcmm", oc), ("fbev", 5)], key=("vcev", oc))
        for oc in range(8):
            E("tensor", lambda e, oc=oc: e.transpose(out=PS[0:16, 5 * 512 + oc * 128:5 * 512 + (oc + 1) * 128],
                                                     in_=VcT[:, oc * 16:(oc + 1) * 16], identity=identf[:]),
              sem="tM" if oc == 7 else None, deps=[("vcev", oc), "identf", ("vcmm", 7), "o2tr"],
              key="vctr" if oc == 7 else None)
        E("vector", lambda e: e.tensor_copy(out=Vctok[:], in_=PS[0:16, 5 * 512:5 * 512 + 1024]),
          sem="vV", deps=["vctr", "fsqcast"], key="vctok")
        E("gpsimd", lambda e: e.memset(VcOnes[:], 1.0), sem="gP", deps=[("cc1w",)], key="vcoz")
        for h in range(16):
            E("vector", lambda e, h=h: e.tensor_copy(out=VcOnes[0:16, h * 65:h * 65 + 64],
                                                     in_=Vctok[0:16, h * 64:(h + 1) * 64]),
              sem="vV", deps=["vctok", "vcoz"], key=("vco", h))
        load_ffnW("cWq", dW["cWq"], 0, free_ev="o2t")
        load_ffnW("cWo", dW["cWo"], 1, free_ev=("vcmm", 7))
        if upto == "I0":
            for ti in range(NT):
                E("sync", lambda e, ti=ti: e.dma_start(out=xt[ti % 2][:], in_=dx1[ti * 128:(ti + 1) * 128, :]),
                  sem="qF%d" % (ti % 2),
                  deps=[("x1st", NT - 2), ("x1st", NT - 1), ("vco", 15), ("cWo", "done"), ("cWq", "done"),
                        ("fdbg", ti - 2) if ti >= 2 else None], key=("fdbgl", ti))
                E("gpsimd", lambda e, ti=ti: e.dma_start(out=dout[ti * 128:(ti + 1) * 128, :], in_=xt[ti % 2][:]),
                  sem="qO" if ti % 2 == 0 else "qO2", deps=[("fdbgl", ti)], key=("fdbg", ti))
            w.barrier_wait("gpsimd", [("fdbg", NT - 2), ("fdbg", NT - 1), ("oall",), ("yall",)])
            w.barrier_wait("gpsimd", [("tap", nm) for nm in dtaps])
            return
        E("gpsimd", lambda e: e.memset(nrm[0:2, 16:32], 0.0), sem="gP", deps=[("f0knu", NJ - 1, 7), "dksqrt"], key="kcnz")
        for c2 in range(8):
            E("scalar", lambda e, c2=c2: e.activation(out=sqsc[:, 0:16], in_=KcT[:, c2 * 16:(c2 + 1) * 16],
                                                      func=AF.Square, bias=0.0, scale=1.0),
              sem="sS", deps=[("kcev", c2), ("kcnm", c2 - 1) if c2 >= 1 else None], key=("kcsq", c2))
            E("tensor", lambda e: e.matmul(out=psf(6, 16, rows=2), lhsT=bonesb[:], rhs=sqsc[:, 0:16],
                                           start=True, stop=True),
              sem="tM", deps=[("kcsq", c2), "bones", ("kcnr", c2 - 1) if c2 >= 1 else "vctok"],
              key=("kcnm", c2))
            E("vector", lambda e, c2=c2: e.reduce_max(out=nrm[0:2, 16 + c2:17 + c2], in_=psf(6, 16, rows=2), axis=AX),
              sem="vV", deps=[("kcnm", c2), "kcnz"], key=("kcnr", c2))
        E("scalar", lambda e: e.activation(out=nrm[0:2, 16:24], in_=nrm[0:2, 16:24], func=AF.Sqrt, bias=0.0, scale=1.0),
          sem="sS", deps=[("kcnr", 7)], key="kcsqrt")
        qb_dma("gpsimd", lambda e: e.dma_start(out=qnacc[0:1, 96:104], in_=nrm[0:1, 16:24]), ("kcf", 0), extra=["kcsqrt"])
        qb_dma("gpsimd", lambda e: e.dma_start(out=qnacc[0:1, 104:112], in_=nrm[1:2, 16:24]), ("kcf", 1))
        E("gpsimd", lambda e: e.memset(KcBD[:], 0.0), sem="gP", deps=[("kcev", 7)], key="kcbdz")
        for c2 in range(8):
            E("vector", lambda e, c2=c2: e.tensor_copy(out=KcBD[0:64, c2 * 80:c2 * 80 + 16],
                                                       in_=KcT[0:64, c2 * 16:(c2 + 1) * 16]),
              sem="vV", deps=[("kcev", c2), "kcbdz"])
            E("vector", lambda e, c2=c2: e.tensor_copy(out=KcBD[64:128, c2 * 80 + 64:c2 * 80 + 80],
                                                       in_=KcT[64:128, c2 * 16:(c2 + 1) * 16]),
              sem="vV", deps=[("kcev", c2), "kcbdz"], key=("kcbd", c2))
        for Jg in range(NJ):
            xn1_group(Jg, "i1", (0, 1),
                      start_dep=("ew", 1, 3, 1, "mmz") if Jg == 0 else None,
                      war=(lambda Jg_, tt: ("qcev", Jg_ - 1, 6 + tt % 2) if Jg_ >= 1 else "hTev"))
            for oc in range(8):
                bank = oc % 2
                for dcn in range(8):
                    E("tensor", lambda e, oc=oc, dcn=dcn, bank=bank: e.matmul(
                        out=psf(bank, 512), lhsT=ffnW[0][:, dcn * KB + oc * 128:dcn * KB + (oc + 1) * 128],
                        rhs=ch(xnTg, 512)[:, dcn, :], start=(dcn == 0), stop=(dcn == 7)),
                      sem="tM" if dcn == 7 else None,
                      deps=[("i1", "grp", Jg), ("cWq", "done"),
                            ("qcev", Jg, oc - 2) if oc >= 2 else
                            (("i1att", Jg - 1) if Jg >= 1 else ("vcev", 7))],
                      key=("qcmm", Jg, oc) if dcn == 7 else None)
                E("scalar", lambda e, oc=oc, bank=bank: e.activation(
                    out=QcT[:, oc * 512:(oc + 1) * 512], in_=psf(bank, 512),
                    func=AF.Identity, bias=bcols[:, 24 + oc:25 + oc], scale=1.0),
                  sem="sS", deps=[("qcmm", Jg, oc), ("fbev", 3),
                                  ("qcnm", Jg, oc - 1) if oc >= 1 else (("scmm", Jg - 1, 7) if Jg >= 1 else None)],
                  key=("qcev", Jg, oc))
                E("scalar", lambda e, oc=oc: e.activation(out=sqsc[:], in_=QcT[:, oc * 512:(oc + 1) * 512],
                                                          func=AF.Square, bias=0.0, scale=1.0),
                  sem="sS", deps=[("qcev", Jg, oc)], key=("qcsq", Jg, oc))
                E("tensor", lambda e: e.matmul(out=psf(7, 512, rows=2), lhsT=bonesb[:], rhs=sqsc[:],
                                               start=True, stop=True),
                  sem="tM", deps=[("qcsq", Jg, oc), "bones",
                                  ("qcnr", Jg, oc - 1) if oc >= 1 else
                                  (("cexp", Jg - 1, 7) if Jg >= 1 else ("kcnm", 7))],
                  key=("qcnm", Jg, oc))
                E("vector", lambda e, oc=oc: e.reduce_max(out=nrm[0:2, 24 + oc:25 + oc], in_=psf(7, 512, rows=2), axis=AX),
                  sem="vV", deps=[("qcnm", Jg, oc)], key=("qcnr", Jg, oc))
            w.mark(("i1", "con", Jg), "tM")
            E("scalar", lambda e: e.activation(out=nrm[0:2, 24:32], in_=nrm[0:2, 24:32], func=AF.Sqrt, bias=0.0, scale=1.0),
              sem="sS", deps=[("qcnr", Jg, 7)], key=("qcsqrt", Jg))
            qb_dma("gpsimd", lambda e: e.dma_start(out=qnacc[0:1, 112:120], in_=nrm[0:1, 24:32]),
                   ("qcf", Jg, 0), extra=[("qcsqrt", Jg)])
            qb_dma("gpsimd", lambda e: e.dma_start(out=qnacc[0:1, 120:128], in_=nrm[1:2, 24:32]), ("qcf", Jg, 1))
            for h in range(16):
                qc_ = 112 + (h % 2) * 8 + h // 2
                kc_ = 96 + (h % 2) * 8 + h // 2
                E("vector", lambda e, qc_=qc_, kc_=kc_, h=h: e.tensor_tensor(
                    out=qnacc[0:1, 68 + h:69 + h], in0=qnacc[0:1, qc_:qc_ + 1], in1=qnacc[0:1, kc_:kc_ + 1], op=ALU.mult),
                  sem="vV", deps=[("qcf", Jg, 1), ("kcf", 1), ("cbias", Jg - 1) if Jg >= 1 else None],
                  key=("cmh0", Jg, h))
                E("vector", lambda e, h=h: e.tensor_scalar_mul(out=qnacc[0:1, 68 + h:69 + h],
                                                               in0=qnacc[0:1, 68 + h:69 + h], scalar1=-SCALE * 1.02),
                  sem="vV", deps=[("cmh0", Jg, h)], key=("cmh", Jg, h))
            qb_dma("gpsimd", lambda e: e.dma_start(out=mrow2[0:1, :],
                                                   in_=qnacc[0:1, 68:84].rearrange("a (b c) -> a b c", c=2)[:, :, 0],
                                                   allow_slow_non_contiguous=True),
                   ("cmr", Jg, 0), extra=[("cmh", Jg, 15)])
            qb_dma("gpsimd", lambda e: e.dma_start(out=mrow2[1:2, :],
                                                   in_=qnacc[0:1, 68:84].rearrange("a (b c) -> a b c", c=2)[:, :, 1],
                                                   allow_slow_non_contiguous=True),
                   ("cmr", Jg, 1))
            E("tensor", lambda e: e.matmul(out=psf(7, 8), lhsT=ind2[:], rhs=mrow2[:], start=True, stop=True),
              sem="tM", deps=[("cmr", Jg, 1), ("find2",), ("qcnm", Jg, 7)], key=("cbmm", Jg))
            E("vector", lambda e: e.tensor_copy(out=bias16c[:, 0:8], in_=psf(7, 8)), sem="vV",
              deps=[("cbmm", Jg)], key=("cbias", Jg))
            def ctail(pc2, Jg=Jg):
                ob = 2 if pc2 % 2 == 0 else 4
                for s2 in range(2):
                    h = 2 * pc2 + s2
                    slot = (2 * pc2 + s2) % 4
                    E("tensor", lambda e, s2=s2, h=h, slot=slot, ob=ob: e.matmul(
                        out=psf(ob + s2, 512, rows=65), lhsT=VcOnes[0:16, h * 65:h * 65 + 65],
                        rhs=PcB[slot][:], start=True, stop=True),
                      sem="tM" if s2 == 1 else None,
                      deps=[("cexp", Jg, pc2), ("vco", 15),
                            ("cctxp", Jg, pc2 - 2) if pc2 >= 2 else
                            (("cctxp", Jg - 1, 7) if Jg >= 1 else None),
                            ("outev", Jg * 4 - 1) if (pc2 < 2 and Jg >= 1) else None],
                      key=("cpv", Jg, pc2) if s2 == 1 else None)
                lrow = lstage if pc2 % 2 == 0 else sqf32
                if pc2 >= 2:
                    clwar = ("crl", Jg, pc2 - 2, 1)
                elif Jg >= 1:
                    clwar = ("crl", Jg - 1, pc2 + 6, 1)
                else:
                    clwar = ("drl", NJ - 1, 7, 1) if pc2 == 0 else ("lhbias", 100)
                E("vector", lambda e, ob=ob, lrow=lrow: e.reciprocal(
                    out=lrow[64:65, 0:1024], in_=PS[64:65, ob * 512:(ob + 2) * 512]),
                  sem="vV", deps=[("cpv", Jg, pc2), clwar], key=("clrec", Jg, pc2))
                for s2 in range(2):
                    E("tensor", lambda e, s2=s2, ob=ob, lrow=lrow: e.matmul(
                        out=PS[64:128, (ob + s2) * 512:(ob + s2 + 1) * 512],
                        lhsT=ones64[64:65, 0:64],
                        rhs=lrow[64:65, s2 * 512:(s2 + 1) * 512],
                        start=True, stop=True, tile_position=(64, 64)),
                      sem="tM", deps=[("clrec", Jg, pc2), "ones64"],
                      key=("crl", Jg, pc2, s2))
                for s2 in range(2):
                    hc = (2 * pc2 + s2) % 4
                    E("scalar", lambda e, ob=ob, s2=s2, hc=hc: e.copy(
                        out=rlb[0:64, hc * 512:(hc + 1) * 512],
                        in_=PS[64:128, (ob + s2) * 512:(ob + s2 + 1) * 512]),
                      sem="sS", deps=[("crl", Jg, pc2, s2)], key=("crle", Jg, pc2, s2))
                cT = ctxcT[Jg % 2]
                for s2 in range(2):
                    h = 2 * pc2 + s2
                    hc = (2 * pc2 + s2) % 4
                    E("vector", lambda e, h=h, ob=ob, s2=s2, hc=hc, cT=cT: e.tensor_tensor(
                        out=ch(cT, 512)[64 * (h % 2):64 * (h % 2) + 64, h // 2, :],
                        in0=PS[0:64, (ob + s2) * 512:(ob + s2 + 1) * 512],
                        in1=rlb[:, hc * 512:(hc + 1) * 512], op=ALU.mult),
                      sem="vV",
                      deps=[("crle", Jg, pc2, s2),
                            ("a2mm", Jg * 4 - 1) if (Jg >= 1 and pc2 == 0 and s2 == 0) else None],
                      key=("cctxp", Jg, pc2) if s2 == 1 else None)

            for pc2 in range(8):
                sb_ = 6 if pc2 % 2 == 0 else 7
                E("tensor", lambda e, pc2=pc2, sb_=sb_: e.matmul(
                    out=psf(sb_, 512, rows=80), lhsT=KcBD[:, pc2 * 80:(pc2 + 1) * 80],
                    rhs=QcT[:, pc2 * 512:(pc2 + 1) * 512], start=True, stop=True),
                  sem="tM", deps=[("qcev", Jg, pc2), ("kcbd", pc2), ("cbmm", Jg),
                                  ("cbias", Jg) if pc2 == 1 else None,
                                  ("cexp", Jg, pc2 - 2) if pc2 >= 2 else
                                  (("cexp", Jg - 1, pc2 + 6) if Jg >= 1 else None)],
                  key=("scmm", Jg, pc2))
                for s2 in range(2):
                    slot = (2 * pc2 + s2) % 4
                    E("scalar", lambda e, s2=s2, sb_=sb_, slot=slot, pc2=pc2: e.activation(
                        out=PcB[slot][:], in_=PS[64 * s2:64 * s2 + 16, sb_ * 512:(sb_ + 1) * 512],
                        func=AF.Exp, bias=bias16c[64 * s2:64 * s2 + 16, pc2:pc2 + 1], scale=SCALE),
                      sem="sS", deps=[("scmm", Jg, pc2), ("cbias", Jg),
                                      ("cpv", Jg, pc2 - 2) if pc2 >= 2 else
                                      (("cpv", Jg - 1, pc2 + 6) if Jg >= 1 else None)],
                      key=("cexp", Jg, pc2) if s2 == 1 else None)
                if pc2 >= 1:
                    ctail(pc2 - 1)
            ctail(7)
            w.mark(("cctxd", Jg), "vV")
            cT = ctxcT[Jg % 2]
            for tt in range(4):
                ti = Jg * 4 + tt
                ab0 = 2 if tt % 2 == 0 else 4
                for dcn in range(8):
                    for hf in range(2):
                        E("tensor", lambda e, dcn=dcn, hf=hf, tt=tt, ab0=ab0, cT=cT: e.matmul(
                            out=psf(ab0 + hf, 512),
                            lhsT=ch(cT, 512)[:, dcn, tt * 128:(tt + 1) * 128],
                            rhs=ffnW[1][:, dcn * KB + hf * 512:dcn * KB + (hf + 1) * 512],
                            start=(dcn == 0), stop=(dcn == 7)),
                          sem="tM" if (dcn == 7 and hf == 1) else None,
                          deps=[("cctxd", Jg), ("cWo", "done"),
                                ("outev", ti - 2) if tt >= 2 else (("outev", Jg * 4 - 1) if (tt == 0 and Jg >= 1) else None)],
                          key=("a2mm", ti) if (dcn == 7 and hf == 1) else None)
                xe = xt[2 + ti % 2]
                E("sync", lambda e, ti=ti, xe=xe: e.dma_start(out=xe[:], in_=dx1[ti * 128:(ti + 1) * 128, :]),
                  sem="qE%d" % (ti % 2),
                  deps=[*(X1ALL if ti < 2 else []), ("fout", ti - 2) if ti >= 2 else ("ew", 1, 3, 1, "mmz")],
                  key=("oxld", ti))
                E("gpsimd", lambda e, xe=xe: e.tensor_tensor(out=xe[:], in0=xe[:], in1=cbob[:], op=ALU.add),
                  sem="gP", deps=[("oxld", ti), ("cbobld",)], key=("oadd1", ti))
                E("vector", lambda e, xe=xe, ab0=ab0: e.tensor_add(out=xe[:, 0:512], in0=xe[:, 0:512], in1=psf(ab0, 512)),
                  sem="vV", deps=[("a2mm", ti), ("oadd1", ti)], key=("oadd2", ti))
                E("vector", lambda e, xe=xe, ab0=ab0: e.tensor_add(out=xe[:, 512:1024], in0=xe[:, 512:1024],
                                                                   in1=psf(ab0 + 1, 512)),
                  sem="vV", deps=[("oadd2", ti)], key=("outev", ti))
                E("gpsimd", lambda e, ti=ti, xe=xe: e.dma_start(out=dout[ti * 128:(ti + 1) * 128, :], in_=xe[:]),
                  sem="qO" if ti % 2 == 0 else "qO2", deps=[("outev", ti)], key=("fout", ti))
            w.mark(("i1att", Jg), "tM")
        if upto in ("IL1", "IL2", "ILq", "ILs"):
            lastev = {"IL1": ("pcn", NJ - 1, 3), "IL2": ("ctxcev", NT - 1),
                      "ILq": ("qcev", NJ - 1, 7), "ILs": ("scmm", NJ - 1, 3)}[upto]
            for ti in range(NT):
                E("sync", lambda e, ti=ti: e.dma_start(out=xt[ti % 2][:], in_=dx1[ti * 128:(ti + 1) * 128, :]),
                  sem="qF%d" % (ti % 2),
                  deps=[("x1st", NT - 2), ("x1st", NT - 1), lastev,
                        ("fdbg", ti - 2) if ti >= 2 else None], key=("fdbgl", ti))
                E("gpsimd", lambda e, ti=ti: e.dma_start(out=dout[ti * 128:(ti + 1) * 128, :], in_=xt[ti % 2][:]),
                  sem="qO" if ti % 2 == 0 else "qO2", deps=[("fdbgl", ti)], key=("fdbg", ti))
            w.barrier_wait("gpsimd", [("fdbg", NT - 2), ("fdbg", NT - 1), ("oall",), ("yall",)])
            w.barrier_wait("gpsimd", [("tap", nm) for nm in dtaps])
            return

        tap("qdt", lambda: QdT[:], [("qdtev", 7)])
        tap("biasd", lambda: bias16d_t[:], ["dbias"])
        tap("lrun", lambda: l_run[:], [("sdlu", NJ - 1, 7)])
        tap("odacc", lambda: od_acc[:], [("odn", 15)])
        tap("odtb", lambda: odTb[:], ["odtev"])
        tap("otok", lambda: o_tok[:], ["otok"])
        tap("oall", lambda: Oall[:], [("oall",)])


        w.barrier_wait("gpsimd", [("fout", NT - 2), ("fout", NT - 1), ("oall",), ("yall",)])
        w.barrier_wait("gpsimd", [("tap", nm) for nm in dtaps])

    with nc.Block() as block:
        @block.sync
        def _(eng):
            walk(Walk("sync", eng, sems))

        @block.gpsimd
        def _(eng):
            walk(Walk("gpsimd", eng, sems))

        @block.vector
        def _(eng):
            walk(Walk("vector", eng, sems))

        @block.scalar
        def _(eng):
            walk(Walk("scalar", eng, sems))

        @block.tensor
        def _(eng):
            walk(Walk("tensor", eng, sems))

    for g in sem_guards:
        g.__exit__(None, None, None)
    PS_guard.__exit__(None, None, None)
    return nc



def _inputs_for_core(c, kw):
    m = dict(
        x=kw["L0_attn_x"][c],
        wq=kw["L0_attn_wq"], wk=kw["L0_attn_wk"], wv=kw["L0_attn_wv"], wo=kw["L0_attn_wo"],
        slotq=kw["L0_ffn_slot_query"][0],
        dWq=kw["L0_ffn_d_Wq"], dWk=kw["L0_ffn_d_Wk"], dWv=kw["L0_ffn_d_Wv"], dWo=kw["L0_ffn_d_Wo"],
        cWq=kw["L0_ffn_c_Wq"], cWk=kw["L0_ffn_c_Wk"], cWv=kw["L0_ffn_c_Wv"], cWo=kw["L0_ffn_c_Wo"],
        dbq=kw["L0_ffn_d_bq"][None, :], dbk=kw["L0_ffn_d_bk"][None, :],
        dbv=kw["L0_ffn_d_bv"][None, :], dbo=kw["L0_ffn_d_bo"][None, :],
        cbq=kw["L0_ffn_c_bq"][None, :], cbk=kw["L0_ffn_c_bk"][None, :],
        cbv=kw["L0_ffn_c_bv"][None, :], cbo=kw["L0_ffn_c_bo"][None, :],
        lng=kw["L0_ffn_e_ln_g"], lnb=kw["L0_ffn_e_ln_b"],
        eW1=kw["L0_ffn_e_W1"][2 * c:2 * c + 2],
        eWg=kw["L0_ffn_e_Wg"][2 * c:2 * c + 2],
        eW2=kw["L0_ffn_e_W2"][2 * c:2 * c + 2],
        eb1=kw["L0_ffn_e_b1"][2 * c:2 * c + 2],
        ebg=kw["L0_ffn_e_bg"][2 * c:2 * c + 2],
        eb2=kw["L0_ffn_e_b2"][2 * c:2 * c + 2],
    )
    esel = np.zeros((128, 16), np.float32)
    for le in range(2):
        for r in range(8):
            esel[r * 16 + 2 * c + le, le * 8 + r] = 1.0
    m["esel"] = esel
    ind2 = np.zeros((2, 128), np.float32)
    ind2[0, :] = (np.arange(128) % 32 < 16)
    ind2[1, :] = (np.arange(128) % 32 >= 16)
    m["ind2"] = ind2
    sel = np.zeros((128, 16), np.float32)
    for cc in range(8):
        for le in range(2):
            sel[16 * cc + 8 * le + c, 2 * cc + le] = 1.0
    m["bsel"] = sel
    for k in m:
        m[k] = np.ascontiguousarray(m[k], dtype=np.float32)
    return m


def kernel(**kw):
    S = kw["L0_attn_x"].shape[1]
    nc = build_program(S)
    in_maps = [_inputs_for_core(c, kw) for c in range(NCORES)]
    res = run_bass_kernel_spmd(nc, in_maps, core_ids=list(range(NCORES)))
    out = np.stack([res.results[c]["out"] for c in range(NCORES)], axis=0)
    return out.astype(np.float32)
